# version 01
# baseline (speedup 1.0000x reference)
import contextlib
import os
import random

EVAC_ACT = int(os.environ.get("K_EVAC_ACT", "0"))

import numpy as np
import ml_dtypes

import concourse.bass as bass
import concourse.mybir as mybir
from concourse.bass_utils import run_bass_kernel_spmd
from concourse.alu_op_type import AluOpType as OP

F32 = mybir.dt.float32
BF16 = mybir.dt.bfloat16
F16 = mybir.dt.float16
I32 = mybir.dt.int32
AF = mybir.ActivationFunctionType
AX = mybir.AxisListType

D = 4096
NCH = 32
T = 512
S = 2048
B = 4
NHEAD = 32
HDIM = 128
LN_EPS = 1e-5
CLIP = 8.0
SCALE0 = float(HDIM) ** -0.5
SCALE1 = float(HDIM) ** -0.5
ALIBI_MAX = 8.0

BLOCK0 = {0: 0, 1: 512}
BLOCK1 = {0: 1536, 1: 1024}

SLOPES = (1.0 / np.power(2.0, np.arange(1, 33) * (ALIBI_MAX / 32.0))).astype(np.float32)

PERM = (np.arange(4096) % 128) * 32 + (np.arange(4096) // 128)
INV = (np.arange(4096) % 32) * 128 + (np.arange(4096) // 32)



def _img_from_T(wT, ob=128):
    din, dout = wT.shape
    nblk = dout // ob
    a = wT.reshape(din // 128, 128, nblk, ob).transpose(2, 1, 0, 3)
    return np.ascontiguousarray(a.reshape(nblk, 128, (din // 128) * ob).astype(np.float32, copy=False))


def _cols(vec):
    return np.ascontiguousarray(vec.reshape(32, 128).T.astype(np.float32, copy=False))


def prep_inputs(inp):
    f = {k: np.asarray(v, dtype=np.float32) for k, v in inp.items()}
    f["L0_attn_hidden_states"] = f["L0_attn_hidden_states"][:, :, PERM]
    f["L0_ffn_W"] = f["L0_ffn_W"][PERM][:, PERM]
    f["L0_ffn_bias"] = f["L0_ffn_bias"][PERM]
    f["L0_ffn_A"] = f["L0_ffn_A"][:, :, PERM]
    f["L0_ffn_Bf"] = f["L0_ffn_Bf"][:, PERM, :]
    f["L0_attn_Wqkv"] = f["L0_attn_Wqkv"][:, PERM]
    f["L0_attn_Wout"] = f["L0_attn_Wout"][PERM, :]
    f["L1_attn_Wq"] = f["L1_attn_Wq"][PERM, :, :]
    f["L1_attn_Wk"] = f["L1_attn_Wk"][PERM, :, :]
    f["L1_attn_Wv"] = f["L1_attn_Wv"][PERM, :, :]
    f["L1_ffn_W"] = f["L1_ffn_W"][PERM][:, PERM]
    f["L1_ffn_b"] = f["L1_ffn_b"][PERM]
    f["L1_ffn_A"] = f["L1_ffn_A"][:, PERM]
    f["L1_ffn_B"] = f["L1_ffn_B"][PERM, :]
    f["L1_ffn_gate_w"] = f["L1_ffn_gate_w"][:, PERM]
    c = {}

    W0T = np.ascontiguousarray(f["L0_ffn_W"].T)
    BallT = np.ascontiguousarray(f["L0_ffn_Bf"].transpose(0, 2, 1).reshape(256, 4096))
    c["f0_main"] = np.ascontiguousarray(
        np.concatenate([_img_from_T(W0T), _img_from_T(BallT)], axis=2))
    c["f0_A"] = _img_from_T(np.ascontiguousarray(f["L0_ffn_A"].reshape(256, 4096).T))
    c["f0_bias"] = _cols(f["L0_ffn_bias"])
    c["f0_coef"] = np.ascontiguousarray(np.repeat(f["L0_ffn_coeffs"], 16).reshape(2, 128).T)
    c["f0_scal"] = np.ascontiguousarray(np.repeat(f["L0_ffn_scales"], 16).reshape(2, 128).T)

    c["a0_wqkv"] = _img_from_T(np.ascontiguousarray(f["L0_attn_Wqkv"].T))
    c["a0_wout"] = _img_from_T(np.ascontiguousarray(f["L0_attn_Wout"].T))
    c["a0_qlnw"] = _cols(f["L0_attn_q_ln_w"])
    c["a0_qlnb"] = _cols(f["L0_attn_q_ln_b"])
    c["a0_klnw"] = _cols(f["L0_attn_k_ln_w"])
    c["a0_klnb"] = _cols(f["L0_attn_k_ln_b"])

    c["a1_wq"] = _img_from_T(np.ascontiguousarray(
        f["L1_attn_Wq"].transpose(0, 2, 1).reshape(4096, 4096)))
    c["a1_wk"] = _img_from_T(np.ascontiguousarray(
        f["L1_attn_Wk"].transpose(0, 2, 1).reshape(4096, 1024)))
    c["a1_wv"] = _img_from_T(np.ascontiguousarray(
        f["L1_attn_Wv"].transpose(0, 2, 1).reshape(4096, 1024)))
    c["a1_bq"] = np.ascontiguousarray(f["L1_attn_bq"])
    c["a1_bk"] = np.ascontiguousarray(f["L1_attn_bk"])
    c["a1_bv"] = np.ascontiguousarray(f["L1_attn_bv"])

    W1T = np.ascontiguousarray(f["L1_ffn_W"].T)
    B1Tp = np.concatenate([np.ascontiguousarray(f["L1_ffn_B"].T),
                           np.zeros((64, 4096), np.float32)], axis=0)
    c["f1_main"] = np.ascontiguousarray(
        np.concatenate([_img_from_T(W1T), _img_from_T(B1Tp)], axis=2))
    c["f1_A"] = _img_from_T(np.ascontiguousarray(f["L1_ffn_A"].T), ob=64)
    c["f1_gate"] = np.ascontiguousarray(
        f["L1_ffn_gate_w"].T.reshape(32, 128, 4).transpose(1, 0, 2).reshape(128, 128))
    c["f1_gateb"] = np.ascontiguousarray(f["L1_ffn_gate_b"].reshape(1, 4))
    c["f1_bias"] = _cols(f["L1_ffn_b"])

    BIG = np.float16(-30000.0)
    rels = {}
    for par in (0, 1):
        tabs = []
        for chunk in (0, 1):
            nk = 1024 if chunk == 0 else 2048
            q0 = (BLOCK0 if chunk == 0 else BLOCK1)[par]
            tab = np.zeros((4, 128, 2048), np.float16)
            for qt in range(4):
                qabs = q0 + qt * 128 + np.arange(128)[:, None]
                k = np.arange(nk)[None, :]
                r = (k - qabs).astype(np.float32)
                r = np.where(k <= qabs, r, np.float32(-30000.0) * 256.0)
                tab[qt, :, 0:nk] = r.astype(np.float16)
            tabs.append(tab)
        rels[par] = np.ascontiguousarray(np.stack(tabs, 0))
    nonce = random.getrandbits(24)
    flags = np.zeros((4, 16), np.int32)
    for i in range(4):
        flags[i, :] = nonce + 1 + i
    c["sync_flags"] = flags

    x = f["L0_attn_hidden_states"]
    in_maps = []
    for core in range(8):
        b, par = core // 2, core % 2
        rows = np.concatenate([x[b, BLOCK0[par]:BLOCK0[par] + T, :],
                               x[b, BLOCK1[par]:BLOCK1[par] + T, :]], axis=0)
        ximg = np.ascontiguousarray(rows.T.reshape(32, 128, 1024).transpose(1, 0, 2))
        m = dict(c)
        m["x_in"] = ximg
        m["a0_rel"] = rels[par]
        in_maps.append(m)
    return in_maps


def assemble_output(results):
    out = np.empty((B, S, D), np.float32)
    for core in range(8):
        b, par = core // 2, core % 2
        y = np.asarray(results[core]["y_out"])
        rows = y.transpose(1, 0, 2).reshape(4096, 1024).T[:, PERM.argsort()] if False \
            else y.transpose(1, 0, 2).reshape(4096, 1024).T
        rows = rows[:, INV]
        out[b, BLOCK0[par]:BLOCK0[par] + T, :] = rows[:T]
        out[b, BLOCK1[par]:BLOCK1[par] + T, :] = rows[T:]
    return out



class Sem:
    __slots__ = ("h", "n")

    def __init__(self, h):
        self.h = h
        self.n = 0

    def inc(self, instr, k=1):
        instr.then_inc(self.h, k)
        self.n += k
        return instr


class Prog:
    def __init__(self, nc, block, stack):
        self.nc = nc
        self.block = block
        self.st = stack
        self.sems = {}
        self.dram = {}
        self.sb = {}

    def sem(self, name):
        if name not in self.sems:
            self.sems[name] = Sem(self.st.enter_context(self.nc.semaphore(name)))
        return self.sems[name]

    def pe(self, fn):
        self.block.tensor(fn)

    def vec(self, fn):
        self.block.vector(fn)

    def act(self, fn):
        self.block.scalar(fn)

    def pool(self, fn):
        self.block.gpsimd(fn)

    def syn(self, fn):
        self.block.sync(fn)

    def sbuf(self, name, shape, dtype):
        t = self.st.enter_context(self.nc.sbuf_tensor(name, shape, dtype))
        self.sb[name] = t
        return t

    def psum(self, name, shape, dtype=F32):
        return self.st.enter_context(self.nc.psum_tensor(name, shape, dtype))


def declare_io(nc, want):
    d = {}

    def inp(name, shape, dtype=F32):
        d[name] = nc.dram_tensor(name, shape, dtype, kind="ExternalInput").ap()

    inp("x_in", [128, 32, 1024])
    d["y_out"] = nc.dram_tensor("y_out", [128, 32, 1024], F32, kind="ExternalOutput").ap()
    d["dbg"] = nc.dram_tensor("dbg", [128, 4096], F32, kind="ExternalOutput").ap()
    inp("sync_flags", [4, 16], I32)
    if "f0" in want:
        inp("f0_main", [32, 128, 4352])
        inp("f0_A", [2, 128, 4096])
        inp("f0_bias", [128, 32])
        inp("f0_coef", [128, 2])
        inp("f0_scal", [128, 2])
    if "a0" in want:
        inp("a0_wqkv", [96, 128, 4096])
        inp("a0_wout", [32, 128, 4096])
        for n in ("a0_qlnw", "a0_qlnb", "a0_klnw", "a0_klnb"):
            inp(n, [128, 32])
        d["a0_rel"] = nc.dram_tensor("a0_rel", [2, 4, 128, 2048], F16,
                                     kind="ExternalInput").ap()
        d["d_qraw"] = nc.dram_tensor("d_qraw", [32, 128, T], BF16, kind="Internal").ap()
        d["d_kraw"] = nc.dram_tensor("d_kraw", [32, 128, T], BF16, kind="Internal").ap()
        d["d_q"] = nc.dram_tensor("d_q", [32, 128, T], BF16, kind="Internal").ap()
        d["sh_k"] = nc.dram_tensor("sh_k", [2, 4096, 1024], BF16, kind="Internal",
                                   addr_space="Shared").ap()
        d["sh_v"] = nc.dram_tensor("sh_v", [2, 1024, 4096], BF16, kind="Internal",
                                   addr_space="Shared").ap()
        d["sh_flag"] = nc.dram_tensor("sh_flag", [2, 4, 16], I32, kind="Internal",
                                      addr_space="Shared").ap()
    if "a1" in want:
        inp("a1_wq", [32, 128, 4096])
        inp("a1_wk", [8, 128, 4096])
        inp("a1_wv", [8, 128, 4096])
        inp("a1_bq", [128, 32])
        inp("a1_bk", [128, 8])
        inp("a1_bv", [128, 8])
    if "f1" in want:
        inp("f1_main", [32, 128, 4224])
        inp("f1_A", [1, 128, 2048])
        inp("f1_gate", [128, 128])
        inp("f1_gateb", [1, 4])
        inp("f1_bias", [128, 32])
    return d


def alloc_common(P, want):
    P.sbuf("xt", [128, NCH * T], F32)
    P.sbuf("xn", [128, NCH * T], BF16)
    P.sbuf("wbuf0", [128, 4352], BF16)
    P.sbuf("wbuf1", [128, 4352], BF16)
    P.sbuf("identb", [128, 128], BF16)
    P.sbuf("ones_col_b", [128, 1], BF16)
    P.sbuf("ones_row_b", [1, 128], BF16)
    P.sbuf("ones_row_f", [1, 128], F32)
    P.sbuf("flg", [4, 16], I32)
    P.sbuf("pflg", [1, 16], I32)
    P.sbuf("stat", [1, 4 * T], F32)
    P.sbuf("m_b", [128, T], F32)
    P.sbuf("r_b", [128, T], F32)
    P.sbuf("xb16", [128, 2 * T], BF16)
    P.sbuf("sqb", [128, 2 * T], BF16)
    P.sbuf("uapp", [128, 2 * T], F32)
    P.sbuf("tmp", [128, 2 * T], F32)
    if "f0" in want or "f1" in want:
        P.sbuf("usb", [128, 2 * T], BF16)
    if "f1" in want:
        P.sbuf("msk", [128, T], BF16)
        P.sbuf("gsc", [128, 16], F32)
        P.sbuf("gmx", [128, 4], F32)
        P.sbuf("gms", [128, 16], BF16)
        P.sbuf("mtt", [4, 4 * 128], BF16)
    if "a0" in want:
        P.sbuf("qkstage", [128, 2 * T], BF16)
        P.sbuf("sqq", [128, 2 * T], BF16)
        P.sbuf("vout", [128, T], BF16)
        P.sbuf("rel", [128, 4 * 2048], F16)
        P.sbuf("zat", [128, 2048], BF16)
        P.sbuf("pat", [128, 2048], BF16)
        P.sbuf("pnt", [128, 2048], BF16)
        P.sbuf("ksb", [2 * 128, 2048] if False else [128, 2 * 2048], BF16)
        P.sbuf("vsb", [128, 2 * 2048], BF16)
        P.sbuf("qsb", [128, 2 * T], BF16)
        P.sbuf("qnst", [128, 2 * T], BF16)
        P.sbuf("km_b", [128, T], F32)
        P.sbuf("kr_b", [128, T], F32)
        P.sbuf("ptsb", [128, 2 * T], BF16)
        P.sbuf("nmx", [128, 1], F32)
        P.sbuf("lsm", [128, 4], F32)
        P.sbuf("ltot", [128, 1], F32)
        P.sbuf("rcpa", [128, 1], F32)
    if "a1" in want:
        P.sbuf("qgb", [128, 8 * T], BF16)
        P.sbuf("k1b", [128, 8 * T], BF16)
        P.sbuf("v1b", [128, 8 * T], BF16)
        P.sbuf("tqsb", [8, T], BF16)
        P.sbuf("tksb", [8, T], BF16)
        P.sbuf("p1sb", [128, T], BF16)
        P.sbuf("pt1sb", [128, T], BF16)
        P.sbuf("o8sb", [8, T], BF16)
        P.sbuf("ngm1", [128, 4], F32)
        P.sbuf("lsm1", [128, 4], F32)
        P.sbuf("rcp1", [128, 4], F32)
    P.ps = {
        "m0": P.psum("ps_m0", [128, T], F32),
        "m1": P.psum("ps_m1", [128, T], F32),
        "s0": P.psum("ps_s0", [1, T], F32),
        "s1": P.psum("ps_s1", [1, T], F32),
        "x": P.psum("ps_x", [128, T], F32),
        "y": P.psum("ps_y", [128, 1024], BF16),
        "z": P.psum("ps_z", [128, T], F32),
        "w": P.psum("ps_w", [128, 1024], BF16),
    }
    P.last_proj = None
    return


def emit_init(P, want):
    sld = P.sem("i_ld")
    sio = P.sem("i_io")
    siv = P.sem("i_v")

    loads = [("flg", "sync_flags")]
    specs = {}
    if "f0" in want:
        specs.update({"c_f0_bias": ("f0_bias", [128, 32]), "c_f0_coef": ("f0_coef", [128, 2]),
                      "c_f0_scal": ("f0_scal", [128, 2])})
        P.sbuf("f0_bh", [128, 32], F32)
        P.sbuf("f0_w", [128, 2], F32)
    if "a0" in want:
        for n in ("a0_qlnw", "a0_qlnb", "a0_klnw", "a0_klnb"):
            specs["c_" + n] = (n, [128, 32])
        P.sbuf("a0_qws", [128, 32], F32)
        P.sbuf("a0_qbs", [128, 32], F32)
    if "a1" in want:
        specs.update({"c_a1_bq": ("a1_bq", [128, 32]), "c_a1_bk": ("a1_bk", [128, 8]),
                      "c_a1_bv": ("a1_bv", [128, 8])})
        P.sbuf("a1_bqg", [128, 8], F32)
        P.sbuf("z4", [8, 32], BF16)
        P.sbuf("i_z4", [8, 32], F32)
    if "f1" in want:
        specs.update({"c_f1_gate": ("f1_gate", [128, 128]), "c_f1_gateb": ("f1_gateb", [1, 4]),
                      "c_f1_bias": ("f1_bias", [128, 32])})
        P.sbuf("f1_bh", [128, 32], F32)
        P.sbuf("gate_b", [128, 128], BF16)
        P.sbuf("gateb_b", [1, 4], BF16)
        P.sbuf("zrt", [4, 64], BF16)
        P.sbuf("i_blk", [1, 64], F32)
        P.sbuf("i_iop", [4, 64], F32)
    for sbn, (dn, sh) in specs.items():
        P.sbuf(sbn, sh, F32)
        loads.append((sbn, dn))
    scr_i = P.sbuf("scr_i", [128, 128], F32)
    P.sbuf("cst", [1, 8], F32)

    def g(g):
        P.par_reg = g.to_reg(g.partition_id() % 2)
        for sbn, dn in loads:
            sld.inc(g.dma_start(out=P.sb[sbn][:, :], in_=P.dram[dn]), 16)
        sio.inc(g.iota(scr_i[:, :], pattern=[[1, 128]], base=0, channel_multiplier=-1,
                       allow_small_or_imprecise_dtypes=True))
        if "a1" in want:
            sio.inc(g.iota(P.sb["i_z4"][:, :].rearrange("p (a b) -> p a b", b=4),
                           pattern=[[1, 8], [0, 4]], base=0, channel_multiplier=-1,
                           allow_small_or_imprecise_dtypes=True))
        if "f1" in want:
            sio.inc(g.iota(P.sb["i_iop"][:, :], pattern=[[0, 64]], base=0,
                           channel_multiplier=1, allow_small_or_imprecise_dtypes=True))
    P.pool(g)

    def v(v):
        v.memset(P.sb["cst"][0:1, 0:1], LN_EPS)
        v.memset(P.sb["ones_col_b"][:, :], 1.0)
        v.memset(P.sb["ones_row_b"][:, :], 1.0)
        v.memset(P.sb["ones_row_f"][:, :], 1.0)
        v.wait_ge(sio.h, sio.n)
        v.wait_ge(sld.h, sld.n)
        siv.inc(v.tensor_scalar(P.sb["identb"][:, :], scr_i[:, :], 0.0, None, OP.is_equal))
        if "f0" in want:
            v.tensor_scalar(P.sb["f0_bh"][:, :], P.sb["c_f0_bias"][:, :], 0.5, None, OP.mult)
            siv.inc(v.tensor_mul(P.sb["f0_w"][:, :], P.sb["c_f0_coef"][:, :],
                                 P.sb["c_f0_scal"][:, :]))
        if "a0" in want:
            v.tensor_scalar(P.sb["a0_qws"][:, :], P.sb["c_a0_qlnw"][:, :], SCALE0, None, OP.mult)
            siv.inc(v.tensor_scalar(P.sb["a0_qbs"][:, :], P.sb["c_a0_qlnb"][:, :],
                                    SCALE0, None, OP.mult))
        if "a1" in want:
            v.tensor_scalar(P.sb["z4"][:, :], P.sb["i_z4"][:, :], 0.0, None, OP.is_equal)
            siv.inc(v.reduce_sum(P.sb["a1_bqg"][:, :],
                                 P.sb["c_a1_bq"][:, :].rearrange("p (g j) -> p g j", j=4),
                                 axis=AX.X))
            v.wait_ge(siv.h, siv.n)
        if "f1" in want:
            v.tensor_scalar(P.sb["f1_bh"][:, :], P.sb["c_f1_bias"][:, :], 0.5, None, OP.mult)
            v.tensor_copy(P.sb["gate_b"][:, :], P.sb["c_f1_gate"][:, :])
            v.tensor_copy(P.sb["gateb_b"][:, :], P.sb["c_f1_gateb"][:, :])
            v.memset(P.sb["i_blk"][0:1, 0:8], 0.0)
            v.memset(P.sb["i_blk"][0:1, 8:16], 1.0)
            v.memset(P.sb["i_blk"][0:1, 16:32], 2.0)
            siv.inc(v.memset(P.sb["i_blk"][0:1, 32:64], 3.0))
    P.vec(v)

    if "f1" in want:
        ps = P.ps["x"]
        szb = P.sem("i_zb")

        def pe(t):
            t.wait_ge(siv.h, siv.n)
            szb.inc(t.matmul(ps[0:4, 0:64], P.sb["ones_row_f"][0:1, 0:4],
                             P.sb["i_blk"][0:1, :], start=True, stop=True))
        P.pe(pe)

        def v2(v):
            v.wait_ge(szb.h, szb.n)
            v.tensor_sub(P.sb["i_iop"][0:4, :], P.sb["i_iop"][0:4, :], ps[0:4, 0:64])
            siv.inc(v.tensor_scalar(P.sb["zrt"][:, :], P.sb["i_iop"][0:4, :], 0.0,
                                    None, OP.is_ge))
        P.vec(v2)

    def w(e):
        e.wait_ge(siv.h, siv.n)
        e.wait_ge(sld.h, sld.n)
    for sec in (P.pe, P.act, P.pool, P.syn):
        sec(w)



def emit_stream_load(P, chunk):
    s = P.sem("st_ld")

    def sy(sy):
        s.inc(sy.dma_start(out=P.sb["xt"][:, :].rearrange("p (c t) -> p c t", t=T),
                           in_=P.dram["x_in"][:, :, chunk * T:(chunk + 1) * T]), 16)
    P.syn(sy)
    return s


def emit_stream_store(P, chunk, deps):
    s = P.sem("st_st")

    def sy(sy):
        for dsem, dn in deps:
            sy.wait_ge(dsem.h, dn)
        s.inc(sy.dma_start(out=P.dram["y_out"][:, :, chunk * T:(chunk + 1) * T],
                           in_=P.sb["xt"][:, :].rearrange("p (c t) -> p c t", t=T)), 16)
        sy.wait_ge(s.h, s.n)
    P.syn(sy)
    return s



def emit_ln(P, tag, deps):
    xt, xn = P.sb["xt"], P.sb["xn"]
    xb, sq = P.sb["xb16"], P.sb["sqb"]
    stat, mb, rb = P.sb["stat"], P.sb["m_b"], P.sb["r_b"]
    ps_s0, ps_s1 = P.ps["s0"], P.ps["s1"]
    ps_b0, ps_b1 = P.ps["m0"], P.ps["m1"]

    s_c = P.sem(tag + "c")
    s_q = P.sem(tag + "q")
    s_m = P.sem(tag + "m")
    s_f = P.sem(tag + "f")
    s_a = P.sem(tag + "a")
    s_x = P.sem(tag + "x")

    c0 = s_c.n
    q0 = s_q.n
    m0 = s_m.n
    for c in range(NCH):
        def v(v, c=c):
            if c == 0:
                for dsem, dn in deps:
                    v.wait_ge(dsem.h, dn)
            if c >= 2:
                v.wait_ge(s_m.h, m0 + 2 * (c - 2) + 2)
            s_c.inc(v.tensor_copy(xb[:, (c % 2) * T:(c % 2 + 1) * T],
                                  xt[:, c * T:(c + 1) * T]))
        P.vec(v)

        def g(g, c=c):
            g.wait_ge(s_c.h, c0 + c + 1)
            if c >= 2:
                g.wait_ge(s_m.h, m0 + 2 * (c - 2) + 2)
            s_q.inc(g.tensor_mul(sq[:, (c % 2) * T:(c % 2 + 1) * T],
                                 xb[:, (c % 2) * T:(c % 2 + 1) * T],
                                 xb[:, (c % 2) * T:(c % 2 + 1) * T]))
        P.pool(g)

        def pe(t, c=c):
            t.wait_ge(s_c.h, c0 + c + 1)
            s_m.inc(t.matmul(ps_s0[0:1, :], P.sb["ones_col_b"][:, :],
                             xb[:, (c % 2) * T:(c % 2 + 1) * T],
                             start=(c == 0), stop=(c == NCH - 1), skip_group_check=True))
            t.wait_ge(s_q.h, q0 + c + 1)
            s_m.inc(t.matmul(ps_s1[0:1, :], P.sb["ones_col_b"][:, :],
                             sq[:, (c % 2) * T:(c % 2 + 1) * T],
                             start=(c == 0), stop=(c == NCH - 1), skip_group_check=True))
        P.pe(pe)

    mm_all = s_m.n

    def v2(v):
        v.wait_ge(s_m.h, mm_all)
        v.tensor_scalar(stat[0:1, 0:T], ps_s0[0:1, :], 1.0 / D, None, OP.mult)
        v.tensor_scalar(stat[0:1, T:2 * T], ps_s1[0:1, :], 1.0 / D, None, OP.mult)
        v.tensor_mul(stat[0:1, 2 * T:3 * T], stat[0:1, 0:T], stat[0:1, 0:T])
        s_f.inc(v.tensor_sub(stat[0:1, T:2 * T], stat[0:1, T:2 * T],
                             stat[0:1, 2 * T:3 * T]))
    P.vec(v2)
    f_var = s_f.n

    def a(a):
        a.wait_ge(s_f.h, f_var)
        a.activation(stat[0:1, 2 * T:3 * T], stat[0:1, T:2 * T], AF.Ln,
                     bias=P.sb["cst"][0:1, 0:1])
        s_f.inc(a.activation(stat[0:1, 3 * T:4 * T], stat[0:1, 2 * T:3 * T], AF.Exp,
                             scale=-0.5))
    P.act(a)
    f_rstd = s_f.n

    def pe2(t):
        t.wait_ge(s_f.h, f_rstd)
        s_f.inc(t.matmul(ps_b0[:, :], P.sb["ones_row_f"][:, :], stat[0:1, 0:T],
                         start=True, stop=True))
        s_f.inc(t.matmul(ps_b1[:, :], P.sb["ones_row_f"][:, :], stat[0:1, 3 * T:4 * T],
                         start=True, stop=True))
    P.pe(pe2)
    f_bc = s_f.n

    def v3(v):
        v.wait_ge(s_f.h, f_bc)
        v.tensor_copy(mb[:, :], ps_b0[:, :])
        s_f.inc(v.tensor_copy(rb[:, :], ps_b1[:, :]))
    P.vec(v3)
    f_cp = s_f.n

    a0 = s_a.n
    x0 = s_x.n
    u = P.sb["uapp"]
    for c in range(NCH):
        def v4(v, c=c):
            if c == 0:
                v.wait_ge(s_f.h, f_cp)
            if c >= 2:
                v.wait_ge(s_x.h, x0 + c - 1)
            s_a.inc(v.tensor_sub(u[:, (c % 2) * T:(c % 2 + 1) * T],
                                 xt[:, c * T:(c + 1) * T], mb[:, :]))
        P.vec(v4)

        def g4(g, c=c):
            g.wait_ge(s_a.h, a0 + c + 1)
            s_x.inc(g.tensor_mul(xn[:, c * T:(c + 1) * T],
                                 u[:, (c % 2) * T:(c % 2 + 1) * T], rb[:, :]))
        P.pool(g4)
    return s_x, s_x.n



def emit_proj(P, tag, img, nblk, totcols, mm_cb, evac_cb, consume_cb=None):
    wb = (P.sb["wbuf0"], P.sb["wbuf1"])
    ps = (P.ps["m0"], P.ps["m1"])
    s_w = P.sem(tag + "w")
    s_m = P.sem(tag + "m")
    s_e = P.sem(tag + "e")
    w0, m0, e0 = s_w.n, s_m.n, s_e.n
    prev = P.last_proj

    def g0(g):
        if prev is not None:
            g.wait_ge(prev[0].h, prev[1])
        s_w.inc(g.dma_start(out=wb[0][:, 0:totcols], in_=img[0]), 16)
        if nblk > 1:
            s_w.inc(g.dma_start(out=wb[1][:, 0:totcols], in_=img[1]), 16)
    P.pool(g0)

    for j in range(nblk):
        def pe(t, j=j):
            t.wait_ge(s_w.h, w0 + 16 * (j + 1))
            if j >= 2:
                t.wait_ge(s_e.h, e0 + j - 1)
            s_m.inc(mm_cb(t, j, wb[j % 2], ps[j % 2]))
        P.pe(pe)

        evac_cb(j, ps[j % 2], s_m.n)
        if consume_cb is not None and j >= 1:
            consume_cb(j - 1)

        def g1(g, j=j):
            if j + 2 < nblk:
                g.wait_ge(s_m.h, m0 + j + 1)
                s_w.inc(g.dma_start(out=wb[(j + 2) % 2][:, 0:totcols], in_=img[j + 2]), 16)
        P.pool(g1)
    if consume_cb is not None:
        consume_cb(nblk - 1)
    P.last_proj = (s_m, s_m.n)
    return s_e



def _dense_evac_scaled(P, tag, bias_cols, scale):
    s_e = P.sem(tag + "e")
    s_d = P.sem(tag + "d")
    e0 = s_e.n
    d0 = s_d.n

    def evac(j, ps, mm_target):
        def a(a, j=j):
            a.wait_ge(P.sem(tag + "m").h, mm_target)
            if j >= 2:
                a.wait_ge(s_d.h, d0 + j - 1)
            if EVAC_ACT:
                s_e.inc(a.activation(P.sb["tmp"][:, (j % 2) * T:(j % 2 + 1) * T], ps[:, :],
                                     AF.Identity, bias=bias_cols[:, j:j + 1], scale=scale))
            else:
                s_e.inc(a.tensor_scalar(P.sb["tmp"][:, (j % 2) * T:(j % 2 + 1) * T], ps[:, :],
                                        scale, bias_cols[:, j:j + 1], OP.mult, OP.add))
        (P.act if EVAC_ACT else P.vec)(a)

        def v(v, j=j):
            v.wait_ge(s_e.h, e0 + j + 1)
            s_d.inc(v.tensor_add(P.sb["xt"][:, j * T:(j + 1) * T],
                                 P.sb["xt"][:, j * T:(j + 1) * T],
                                 P.sb["tmp"][:, (j % 2) * T:(j % 2 + 1) * T]))
        P.vec(v)
    return evac, s_d


def emit_ffn0(P, tag, s_xn, xn_n):
    xn = P.sb["xn"]
    usb = P.sb["usb"]
    sA = P.sem(tag + "Ae")

    def mmA(t, j, wbuf, ps):
        if j == 0:
            t.wait_ge(s_xn.h, xn_n)
        for c in range(NCH):
            mm = t.matmul(ps[:, :], wbuf[:, c * 128:(c + 1) * 128],
                          xn[:, c * T:(c + 1) * T], start=(c == 0), stop=(c == NCH - 1))
        return mm

    def evA(j, ps, mm_target):
        def v(v):
            v.wait_ge(P.sem(tag + "Am").h, mm_target)
            sA.inc(v.tensor_scalar(usb[:, j * T:(j + 1) * T], ps[:, :],
                                   P.sb["f0_w"][:, j:j + 1], None, OP.mult))
        P.vec(v)

    emit_proj(P, tag + "A", P.dram["f0_A"], 2, 4096, mmA, evA)
    uA_n = sA.n

    def mmM(t, j, wbuf, ps):
        if j == 0:
            t.wait_ge(sA.h, uA_n)
        for c in range(NCH):
            t.matmul(ps[:, :], wbuf[:, c * 128:(c + 1) * 128],
                     xn[:, c * T:(c + 1) * T], start=(c == 0), stop=False)
        t.matmul(ps[:, :], wbuf[:, 32 * 128:33 * 128], usb[:, 0:T],
                 start=False, stop=False)
        return t.matmul(ps[:, :], wbuf[:, 33 * 128:34 * 128], usb[:, T:2 * T],
                        start=False, stop=True)

    evac, s_d = _dense_evac_scaled(P, tag + "M", P.sb["f0_bh"], 0.5)
    emit_proj(P, tag + "M", P.dram["f0_main"], 32, 4352, mmM, evac)
    return s_d, s_d.n


def emit_ffn1(P, tag, s_xn, xn_n):
    xn = P.sb["xn"]
    usb = P.sb["usb"]
    ps_g = P.ps["z"]
    ps_mt = P.ps["y"]
    ps_m64 = P.ps["x"]
    s_g = P.sem(tag + "g")
    s_gv = P.sem(tag + "gv")
    s_mt = P.sem(tag + "mt")
    s_mk = P.sem(tag + "mk")

    def peg(t):
        t.wait_ge(s_xn.h, xn_n)
        for tt in range(4):
            for c in range(NCH):
                t.matmul(ps_g[:, 16 + tt * 4:16 + (tt + 1) * 4],
                         xn[:, c * T + tt * 128: c * T + (tt + 1) * 128],
                         P.sb["gate_b"][:, c * 4:(c + 1) * 4],
                         start=(tt == 0 and c == 0), stop=False, skip_group_check=True)
            s_g.inc(t.matmul(ps_g[:, 16 + tt * 4:16 + (tt + 1) * 4],
                             P.sb["ones_row_b"][0:1, :], P.sb["gateb_b"][0:1, :],
                             start=False, stop=(tt == 3), skip_group_check=True))
    P.pe(peg)
    g_n = s_g.n

    def vg(v):
        v.wait_ge(s_g.h, g_n)
        s_gv.inc(v.tensor_copy(P.sb["gsc"][:, :], ps_g[:, 16:32]))
        v.wait_ge(s_gv.h, s_gv.n)
        s_gv.inc(v.reduce_max(P.sb["gmx"][:, :],
                              P.sb["gsc"][:, :].rearrange("p (a b) -> p a b", b=4),
                              axis=AX.X))
        v.wait_ge(s_gv.h, s_gv.n)
        for tt in range(4):
            s_gv.inc(v.tensor_scalar(P.sb["gms"][:, tt * 4:(tt + 1) * 4],
                                     P.sb["gsc"][:, tt * 4:(tt + 1) * 4],
                                     P.sb["gmx"][:, tt:tt + 1], None, OP.is_ge))
    P.vec(vg)
    gv_n = s_gv.n

    def pet(t):
        t.wait_ge(s_gv.h, gv_n)
        for tt in range(4):
            s_mt.inc(t.transpose(ps_mt[0:4, 512 + tt * 128:512 + (tt + 1) * 128],
                                 P.sb["gms"][:, tt * 4:(tt + 1) * 4], P.sb["identb"][:, :]))
    P.pe(pet)
    mt_n = s_mt.n

    def vt(v):
        v.wait_ge(s_mt.h, mt_n)
        s_mt.inc(v.tensor_copy(P.sb["mtt"][:, :], ps_mt[0:4, 512:1024]))
    P.vec(vt)
    mtc_n = s_mt.n

    def pem(t):
        t.wait_ge(s_mt.h, mtc_n)
        s_mk.inc(t.matmul(ps_m64[0:64, :], P.sb["zrt"][:, :],
                          P.sb["mtt"][0:4, :], start=True, stop=True))
    P.pe(pem)
    mk_n = s_mk.n

    def vm(v):
        v.wait_ge(s_mk.h, mk_n)
        s_mk.inc(v.tensor_copy(P.sb["msk"][0:64, 0:T], ps_m64[0:64, :]))
    P.vec(vm)
    msk_n = s_mk.n

    sA = P.sem(tag + "Ae")

    def mmA(t, j, wbuf, ps):
        for c in range(NCH):
            mm = t.matmul(ps[0:64, :], wbuf[:, c * 64:(c + 1) * 64],
                          xn[:, c * T:(c + 1) * T], start=(c == 0), stop=(c == NCH - 1))
        return mm

    def evA(j, ps, mm_target):
        def v(v):
            v.wait_ge(P.sem(tag + "Am").h, mm_target)
            v.wait_ge(s_mk.h, msk_n)
            sA.inc(v.tensor_mul(usb[0:64, 0:T], ps[0:64, :], P.sb["msk"][0:64, 0:T]))
        P.vec(v)

    emit_proj(P, tag + "A", P.dram["f1_A"], 1, 2048, mmA, evA)
    uA_n = sA.n

    def mmM(t, j, wbuf, ps):
        if j == 0:
            t.wait_ge(sA.h, uA_n)
        for c in range(NCH):
            t.matmul(ps[:, :], wbuf[:, c * 128:(c + 1) * 128],
                     xn[:, c * T:(c + 1) * T], start=(c == 0), stop=False)
        return t.matmul(ps[:, :], wbuf[0:64, 32 * 128:32 * 128 + 128],
                        usb[0:64, 0:T], start=False, stop=True)

    evac, s_d = _dense_evac_scaled(P, tag + "M", P.sb["f1_bh"], 0.5)
    emit_proj(P, tag + "M", P.dram["f1_main"], 32, 4224, mmM, evac)
    if os.environ.get("K_DBG") == "f1" and getattr(P, "cur_chunk", None) == 0:
        sdb = P.sem("dbg")
        dn = s_d.n

        def gdb(g):
            g.wait_ge(s_d.h, dn)
            sdb.inc(g.dma_start(out=P.dram["dbg"][:, 0:16], in_=P.sb["gsc"][:, :]), 16)
            sdb.inc(g.dma_start(out=P.dram["dbg"][:, 16:32], in_=P.sb["gms"][:, :]), 16)
            sdb.inc(g.dma_start(out=P.dram["dbg"][0:4, 32:544], in_=P.sb["mtt"][0:4, :]), 16)
            sdb.inc(g.dma_start(out=P.dram["dbg"][0:64, 544:1056], in_=P.sb["msk"][0:64, 0:T]), 16)
            sdb.inc(g.dma_start(out=P.dram["dbg"][0:64, 1056:1568], in_=P.sb["usb"][0:64, 0:T]), 16)
            g.wait_ge(sdb.h, sdb.n)
        P.pool(gdb)
    return s_d, s_d.n



def emit_attn1(P, tag, s_xn, xn_n):
    xn = P.sb["xn"]
    qgb, k1b, v1b = P.sb["qgb"], P.sb["k1b"], P.sb["v1b"]

    wb = (P.sb["wbuf0"], P.sb["wbuf1"])
    ps = (P.ps["m0"], P.ps["m1"])
    s_w = P.sem(tag + "qw")
    s_m = P.sem(tag + "qm")
    s_e = P.sem(tag + "qe")
    w0, m0_, e0 = s_w.n, s_m.n, s_e.n
    prev = P.last_proj
    img = P.dram["a1_wq"]

    def g0(g):
        if prev is not None:
            g.wait_ge(prev[0].h, prev[1])
        s_w.inc(g.dma_start(out=wb[0][:, 0:4096], in_=img[0]), 16)
        s_w.inc(g.dma_start(out=wb[1][:, 0:4096], in_=img[1]), 16)
    P.pool(g0)

    for h in range(32):
        gidx = h // 4

        def pe(t, h=h, gidx=gidx):
            t.wait_ge(s_w.h, w0 + 16 * (h + 1))
            if h == 0:
                t.wait_ge(s_xn.h, xn_n)
            if h % 4 == 0 and gidx >= 2:
                t.wait_ge(s_e.h, e0 + gidx - 1)
            for c in range(NCH):
                mm = t.matmul(ps[gidx % 2][:, :], wb[h % 2][:, c * 128:(c + 1) * 128],
                              xn[:, c * T:(c + 1) * T],
                              start=(h % 4 == 0 and c == 0),
                              stop=(h % 4 == 3 and c == NCH - 1), skip_group_check=True)
            s_m.inc(mm)
        P.pe(pe)

        if h % 4 == 3:
            mm_now = s_m.n

            def v(v, gidx=gidx):
                v.wait_ge(s_m.h, mm_now)
                s_e.inc(v.tensor_scalar(qgb[:, gidx * T:(gidx + 1) * T], ps[gidx % 2][:, :],
                                        P.sb["a1_bqg"][:, gidx:gidx + 1], None, OP.add))
            P.vec(v)

        def g1(g, h=h):
            if h + 2 < 32:
                g.wait_ge(s_m.h, m0_ + h + 1)
                s_w.inc(g.dma_start(out=wb[h % 2][:, 0:4096], in_=img[h + 2]), 16)
        P.pool(g1)
    P.last_proj = (s_m, s_m.n)
    qg_n = s_e.n

    def mk_kv(which, dst, bias_name, scaled):
        def mm(t, j, wbuf, ps_):
            for c in range(NCH):
                mm_ = t.matmul(ps_[:, :], wbuf[:, c * 128:(c + 1) * 128],
                               xn[:, c * T:(c + 1) * T], start=(c == 0), stop=(c == NCH - 1))
            return mm_

        s_ev = P.sem(tag + which + "e")

        def ev(j, ps_, mm_target):
            def v(v, j=j):
                v.wait_ge(P.sem(tag + which + "m").h, mm_target)
                if scaled:
                    s_ev.inc(v.tensor_scalar(dst[:, j * T:(j + 1) * T], ps_[:, :],
                                             P.sb[bias_name][:, j:j + 1], SCALE1,
                                             OP.add, OP.mult))
                else:
                    s_ev.inc(v.tensor_scalar(dst[:, j * T:(j + 1) * T], ps_[:, :],
                                             P.sb[bias_name][:, j:j + 1], None, OP.add))
            P.vec(v)
        emit_proj(P, tag + which, P.dram["a1_w" + which], 8, 4096, mm, ev)
        return s_ev

    s_k = mk_kv("k", k1b, "c_a1_bk", True)
    k_n = s_k.n
    s_v = mk_kv("v", v1b, "c_a1_bv", False)
    v_n = s_v.n

    ps_x, ps_y, ps_w_, ps_z = P.ps["x"], P.ps["y"], P.ps["w"], P.ps["z"]
    s_tr = P.sem(tag + "tr")
    s_te = P.sem(tag + "te")
    s_sc = P.sem(tag + "sc")
    s_nm = P.sem(tag + "nm")
    s_ex = P.sem(tag + "ex")
    s_pt = P.sem(tag + "pt")
    s_pte = P.sem(tag + "pe2")
    s_pv = P.sem(tag + "pv")
    s_o8 = P.sem(tag + "o8")
    s_ox = P.sem(tag + "ox")
    s_rc = P.sem(tag + "rc")
    s_fin = P.sem(tag + "fin")
    b_tr, b_te, b_sc, b_nm, b_ex = s_tr.n, s_te.n, s_sc.n, s_nm.n, s_ex.n
    b_pt, b_pte, b_pv, b_o8, b_ox, b_rc, b_fin = (s_pt.n, s_pte.n, s_pv.n, s_o8.n,
                                                  s_ox.n, s_rc.n, s_fin.n)
    qgv = qgb.rearrange("p (g t) -> p t g", t=T)
    kv_ = k1b.rearrange("p (g t) -> p t g", t=T)
    vv_ = v1b.rearrange("p (g t) -> p t g", t=T)
    NW = T // 4

    for w in range(NW):
        def pe(t, w=w):
            if w == 0:
                t.wait_ge(s_k.h, k_n)
                t.wait_ge(s_v.h, v_n)
                t.wait_ge(P.sem(tag + "qe").h, qg_n)
            if w >= 1:
                t.wait_ge(s_te.h, b_te + 2 * w)
            for i in range(4):
                tok = w * 4 + i
                t.transpose(ps_y[0:8, i * 128:(i + 1) * 128], qgv[:, tok, :],
                            P.sb["identb"][:, :])
                mmt = t.transpose(ps_y[0:8, 512 + i * 128:512 + (i + 1) * 128],
                                  kv_[:, tok, :], P.sb["identb"][:, :])
            s_tr.inc(mmt)
            t.wait_ge(s_te.h, b_te + 2 * (w + 1))
            if w >= 1:
                t.wait_ge(s_fin.h, b_fin + 4 * w)
            for i in range(4):
                mms = t.matmul(ps_x[:, i * 128:(i + 1) * 128],
                               P.sb["tqsb"][0:8, i * 128:(i + 1) * 128],
                               P.sb["tksb"][0:8, i * 128:(i + 1) * 128],
                               start=True, stop=True)
            s_sc.inc(mms)
            t.wait_ge(s_ex.h, b_ex + 4 * (w + 1))
            if w >= 1:
                t.wait_ge(s_pte.h, b_pte + w)
            for i in range(4):
                mmp = t.transpose(ps_w_[:, i * 128:(i + 1) * 128],
                                  P.sb["p1sb"][:, i * 128:(i + 1) * 128],
                                  P.sb["identb"][:, :])
            s_pt.inc(mmp)
            t.wait_ge(s_pte.h, b_pte + w + 1)
            if w >= 1:
                t.wait_ge(s_o8.h, b_o8 + w)
            for i in range(4):
                tok = w * 4 + i
                mmv = t.matmul(ps_z[0:8, i * 128:(i + 1) * 128], vv_[:, tok, :],
                               P.sb["pt1sb"][:, i * 128:(i + 1) * 128],
                               start=True, stop=True)
            s_pv.inc(mmv)
            t.wait_ge(s_o8.h, b_o8 + w + 1)
            for i in range(4):
                mme = t.matmul(ps_x[:, i * 32:(i + 1) * 32],
                               P.sb["o8sb"][0:8, i * 128:(i + 1) * 128],
                               P.sb["z4"][:, :], start=True, stop=True,
                               skip_group_check=True)
            s_ox.inc(mme)
        P.pe(pe)

        def v(v, w=w):
            v.wait_ge(s_tr.h, b_tr + w + 1)
            if w >= 1:
                v.wait_ge(s_sc.h, b_sc + w)
            s_te.inc(v.tensor_copy(P.sb["tqsb"][0:8, :], ps_y[0:8, 0:512]))
            s_te.inc(v.tensor_copy(P.sb["tksb"][0:8, :], ps_y[0:8, 512:1024]))
            v.wait_ge(s_sc.h, b_sc + w + 1)
            if w >= 1:
                v.wait_ge(s_ex.h, b_ex + 4 * w)
            s_nm.inc(v.reduce_max(P.sb["ngm1"][:, :],
                                  ps_x[:, :].rearrange("p (a b) -> p a b", b=128),
                                  axis=AX.X, negate=True))
            v.wait_ge(s_pt.h, b_pt + w + 1)
            if w >= 1:
                v.wait_ge(s_pv.h, b_pv + w)
            s_pte.inc(v.tensor_copy(P.sb["pt1sb"][:, :], ps_w_[:, 0:512]))
            v.wait_ge(s_pv.h, b_pv + w + 1)
            if w >= 1:
                v.wait_ge(s_ox.h, b_ox + w)
            s_o8.inc(v.tensor_copy(P.sb["o8sb"][0:8, :], ps_z[0:8, :]))
            v.wait_ge(s_ex.h, b_ex + 4 * (w + 1))
            s_rc.inc(v.reciprocal(P.sb["rcp1"][:, :], P.sb["lsm1"][:, :]))
            v.wait_ge(s_rc.h, b_rc + w + 1)
            v.wait_ge(s_ox.h, b_ox + w + 1)
            xtv = P.sb["xt"].rearrange("p (c t) -> p t c", t=T)
            for i in range(4):
                tok = w * 4 + i
                s_fin.inc(v.scalar_tensor_tensor(xtv[:, tok, :], ps_x[:, i * 32:(i + 1) * 32],
                                                 P.sb["rcp1"][:, i:i + 1], xtv[:, tok, :],
                                                 OP.mult, OP.add))
        P.vec(v)

        def a(a, w=w):
            a.wait_ge(s_nm.h, b_nm + w + 1)
            if w >= 1:
                a.wait_ge(s_pt.h, b_pt + w)
                a.wait_ge(s_rc.h, b_rc + w)
            for i in range(4):
                s_ex.inc(a.activation(P.sb["p1sb"][:, i * 128:(i + 1) * 128],
                                      ps_x[:, i * 128:(i + 1) * 128], AF.Exp,
                                      bias=P.sb["ngm1"][:, i:i + 1],
                                      accum_out=P.sb["lsm1"][:, i:i + 1]))
        P.act(a)
    return s_fin, s_fin.n



def _a0_blocks(chunk):
    if chunk == 0:
        return [(0, 0), (1, 0)]
    return [(0, 0), (1, 0), (1, 512), (0, 512)]


A0LVL = int(os.environ.get("K_A0LVL", "5"))
A0SUB = os.environ.get("K_A0SUB", "")


def emit_attn0(P, tag, s_xn, xn_n, chunk):
    xn = P.sb["xn"]
    stage = P.sb["qkstage"]
    sqq = P.sb["sqq"]
    stat = P.sb["stat"]
    ps_s0, ps_s1 = P.ps["s0"], P.ps["s1"]
    ps_x, ps_z, ps_y, ps_w = P.ps["x"], P.ps["z"], P.ps["y"], P.ps["w"]
    s_cl = P.sem(tag + "Pe")
    s_sq = P.sem(tag + "sq")
    s_st = P.sem(tag + "st")
    s_qd = P.sem(tag + "qd")
    s_vt = P.sem(tag + "vt")
    s_ve = P.sem(tag + "ve")
    s_vd = P.sem(tag + "vd")
    cl0, st0, qd0, vt0, ve0, vd0 = s_cl.n, s_st.n, s_qd.n, s_vt.n, s_ve.n, s_vd.n

    def mmQKV(t, j, wbuf, ps):
        if j == 0:
            t.wait_ge(s_xn.h, xn_n)
        for c in range(NCH):
            mm = t.matmul(ps[:, :], wbuf[:, c * 128:(c + 1) * 128],
                          xn[:, c * T:(c + 1) * T], start=(c == 0), stop=(c == NCH - 1))
        return mm

    def evQKV(j, ps, mm_target):
        def v(v, j=j):
            v.wait_ge(P.sem(tag + "Pm").h, mm_target)
            if 2 <= j < 66:
                v.wait_ge(s_qd.h, qd0 + 16 * min(j - 1, 64))
            elif j >= 66 and "v" not in A0SUB:
                v.wait_ge(s_vd.h, vd0 + 16 * (j - 65))
            s_cl.inc(v.tensor_scalar(stage[:, (j % 2) * T:(j % 2 + 1) * T], ps[:, :],
                                     -CLIP, CLIP, OP.max, OP.min))
        P.vec(v)

    def consQKV(j):
        cln = cl0 + j + 1
        if j >= 64 and "v" in A0SUB:
            return
        if j < 64:
            def g2(g, j=j):
                if "r" in A0SUB:
                    g.sem_inc(s_qd.h, 16)
                    s_qd.n += 16
                    return
                g.wait_ge(s_cl.h, cln)
                dst = P.dram["d_qraw"] if j < 32 else P.dram["d_kraw"]
                s_qd.inc(g.dma_start(out=dst[j % 32],
                                     in_=stage[:, (j % 2) * T:(j % 2 + 1) * T]), 16)
            P.pool(g2)
        else:
            jv = j - 64

            def pe(t, j=j, jv=jv):
                t.wait_ge(s_cl.h, cln)
                if jv >= 1:
                    t.wait_ge(s_ve.h, ve0 + jv)
                for tt in range(4):
                    mm = t.transpose(ps_y[:, tt * 128:(tt + 1) * 128],
                                     stage[:, (j % 2) * T + tt * 128:
                                           (j % 2) * T + (tt + 1) * 128],
                                     P.sb["identb"][:, :])
                s_vt.inc(mm)
            P.pe(pe)
            vtn = s_vt.n

            def v2(v, jv=jv):
                v.wait_ge(s_vt.h, vtn)
                if jv >= 1:
                    v.wait_ge(s_vd.h, vd0 + 16 * jv)
                s_ve.inc(v.tensor_copy(P.sb["vout"][:, :], ps_y[:, 0:512]))
            P.vec(v2)
            ven = s_ve.n

            def g3(g, jv=jv):
                g.wait_ge(s_ve.h, ven)
                par = P.par_reg
                srcv = P.sb["vout"][:, :].rearrange("p (tt c) -> p tt c", c=128)
                with g.If(par):
                    g.dma_start(out=P.dram["sh_v"][1, chunk * T:(chunk + 1) * T,
                                                   jv * 128:(jv + 1) * 128]
                                .rearrange("(tt p) c -> p tt c", p=128),
                                in_=srcv).then_inc(s_vd.h, 16)
                with g.Else():
                    g.dma_start(out=P.dram["sh_v"][0, chunk * T:(chunk + 1) * T,
                                                   jv * 128:(jv + 1) * 128]
                                .rearrange("(tt p) c -> p tt c", p=128),
                                in_=srcv).then_inc(s_vd.h, 16)
                s_vd.n += 16
            P.pool(g3)

    emit_proj(P, tag + "P", P.dram["a0_wqkv"], 96, 4096, mmQKV, evQKV,
              consume_cb=consQKV)
    if A0LVL <= 1:
        return s_qd, s_qd.n

    s_bi = P.sem(tag + "bi")
    s_sev = P.sem(tag + "sev")
    bi0 = s_bi.n
    qdn_all = s_qd.n
    vdn_all = s_vd.n
    for grp in range(2):
        srcD = P.dram["d_qraw"] if grp == 0 else P.dram["d_kraw"]
        for oc in range(32):
            i = grp * 32 + oc

            def syb(sy, i=i, oc=oc, srcD=srcD):
                if i == 0:
                    sy.wait_ge(s_qd.h, qdn_all)
                    sy.wait_ge(s_vd.h, vdn_all)
                if i >= 2:
                    sy.wait_ge(s_st.h, st0 + 2 * (i - 1))
                s_bi.inc(sy.dma_start(out=stage[:, (i % 2) * T:(i % 2 + 1) * T],
                                      in_=srcD[oc]), 16)
            P.syn(syb)
            bin_ = s_bi.n

            def gsq(g, i=i):
                g.wait_ge(s_bi.h, bin_)
                if i >= 2:
                    g.wait_ge(s_st.h, st0 + 2 * (i - 1))
                s_sq.inc(g.tensor_mul(sqq[:, (i % 2) * T:(i % 2 + 1) * T],
                                      stage[:, (i % 2) * T:(i % 2 + 1) * T],
                                      stage[:, (i % 2) * T:(i % 2 + 1) * T]))
            P.pool(gsq)
            sqn = s_sq.n
            sevq = s_sev.n

            def peb(t, i=i, oc=oc, grp=grp):
                t.wait_ge(s_bi.h, bin_)
                if grp == 1 and oc == 0:
                    t.wait_ge(s_sev.h, sevq)
                mm0 = t.matmul(ps_s0[0:1, :], P.sb["ones_col_b"][:, :],
                               stage[:, (i % 2) * T:(i % 2 + 1) * T],
                               start=(oc == 0), stop=(oc == 31), skip_group_check=True)
                mm0.then_inc(s_st.h, 1)
                s_st.n += 1
                t.wait_ge(s_sq.h, sqn)
                mm = t.matmul(ps_s1[0:1, :], P.sb["ones_col_b"][:, :],
                              sqq[:, (i % 2) * T:(i % 2 + 1) * T],
                              start=(oc == 0), stop=(oc == 31), skip_group_check=True)
                mm.then_inc(s_st.h, 1)
                s_st.n += 1
            P.pe(peb)
        if grp == 0:
            stq = s_st.n

            def vqe(v):
                v.wait_ge(s_st.h, stq)
                v.tensor_copy(stat[0:1, 0:T], ps_s0[0:1, :])
                s_sev.inc(v.tensor_copy(stat[0:1, T:2 * T], ps_s1[0:1, :]))
            P.vec(vqe)

    if A0LVL == 15:
        return s_st, s_st.n

    s_sf = P.sem(tag + "sf")
    stall = s_st.n

    def vfin(v):
        v.wait_ge(s_st.h, stall)
        v.tensor_scalar(stat[0:1, 0:T], stat[0:1, 0:T], 1.0 / D, None, OP.mult)
        v.tensor_scalar(stat[0:1, T:2 * T], stat[0:1, T:2 * T], 1.0 / D, None, OP.mult)
        v.tensor_scalar(stat[0:1, 2 * T:3 * T], ps_s0[0:1, :], 1.0 / D, None, OP.mult)
        s_sf.inc(v.tensor_scalar(stat[0:1, 3 * T:4 * T], ps_s1[0:1, :], 1.0 / D,
                                 None, OP.mult))
        v.wait_ge(s_sf.h, s_sf.n)
        v.tensor_mul(ps_s0[0:1, :], stat[0:1, 0:T], stat[0:1, 0:T])
        s_sf.inc(v.tensor_mul(ps_s1[0:1, :], stat[0:1, 2 * T:3 * T],
                              stat[0:1, 2 * T:3 * T]))
        v.wait_ge(s_sf.h, s_sf.n)
        v.tensor_sub(stat[0:1, T:2 * T], stat[0:1, T:2 * T], ps_s0[0:1, :])
        s_sf.inc(v.tensor_sub(stat[0:1, 3 * T:4 * T], stat[0:1, 3 * T:4 * T],
                              ps_s1[0:1, :]))
    P.vec(vfin)
    varn = s_sf.n

    def afin(a):
        a.wait_ge(s_sf.h, varn)
        a.activation(ps_s0[0:1, :], stat[0:1, T:2 * T], AF.Ln, bias=P.sb["cst"][0:1, 0:1])
        a.activation(stat[0:1, T:2 * T], ps_s0[0:1, :], AF.Exp, scale=-0.5)
        a.activation(ps_s1[0:1, :], stat[0:1, 3 * T:4 * T], AF.Ln, bias=P.sb["cst"][0:1, 0:1])
        s_sf.inc(a.activation(stat[0:1, 3 * T:4 * T], ps_s1[0:1, :], AF.Exp, scale=-0.5))
    P.act(afin)
    rstdn = s_sf.n

    def pefin(t):
        t.wait_ge(s_sf.h, rstdn)
        t.matmul(P.ps["m0"][:, :], P.sb["ones_row_f"][:, :], stat[0:1, 0:T],
                 start=True, stop=True)
        t.matmul(P.ps["m1"][:, :], P.sb["ones_row_f"][:, :], stat[0:1, T:2 * T],
                 start=True, stop=True)
        t.matmul(ps_x[:, :], P.sb["ones_row_f"][:, :], stat[0:1, 2 * T:3 * T],
                 start=True, stop=True)
        s_sf.inc(t.matmul(ps_z[:, :], P.sb["ones_row_f"][:, :], stat[0:1, 3 * T:4 * T],
                          start=True, stop=True))
    P.pe(pefin)
    bcn = s_sf.n

    def vbc(v):
        v.wait_ge(s_sf.h, bcn)
        v.tensor_copy(P.sb["m_b"][:, :], P.ps["m0"][:, :])
        v.tensor_copy(P.sb["r_b"][:, :], P.ps["m1"][:, :])
        v.tensor_copy(P.sb["km_b"][:, :], ps_x[:, :])
        s_sf.inc(v.tensor_copy(P.sb["kr_b"][:, :], ps_z[:, :]))
    P.vec(vbc)
    bcdone = s_sf.n

    if A0LVL == 16:
        return s_sf, s_sf.n

    s_ai = P.sem(tag + "ai")
    s_ap = P.sem(tag + "ap")
    s_ao = P.sem(tag + "ao")
    s_ko = P.sem(tag + "ko")
    ai0, ap0, ao0, ko0 = s_ai.n, s_ap.n, s_ao.n, s_ko.n
    qn = P.sb["qnst"]
    u = P.sb["uapp"]

    for grp in range(2):
        srcD = P.dram["d_qraw"] if grp == 0 else P.dram["d_kraw"]
        mbn, rbn = ("m_b", "r_b") if grp == 0 else ("km_b", "kr_b")
        wn, bn = ("a0_qws", "a0_qbs") if grp == 0 else ("c_a0_klnw", "c_a0_klnb")
        for oc in range(32):
            i = grp * 32 + oc

            def sy(sy, i=i, oc=oc, srcD=srcD):
                if i == 0:
                    sy.wait_ge(s_qd.h, qd0 + 16 * 64)
                if i >= 2:
                    sy.wait_ge(s_ap.h, ap0 + i - 1)
                s_ai.inc(sy.dma_start(out=stage[:, (i % 2) * T:(i % 2 + 1) * T],
                                      in_=srcD[oc]), 16)
            P.syn(sy)

            def v(v, i=i, oc=oc, mbn=mbn, rbn=rbn, wn=wn, bn=bn, grp=grp):
                v.wait_ge(s_ai.h, ai0 + 16 * (i + 1))
                if i == 0:
                    v.wait_ge(s_sf.h, bcdone)
                if i >= 2:
                    v.wait_ge((s_ao if (i - 2) < 32 else s_ko).h,
                              (ao0 + 16 * (i - 1)) if (i - 2) < 32 else (ko0 + 16 * (i - 33)))
                v.tensor_sub(u[:, 0:T], stage[:, (i % 2) * T:(i % 2 + 1) * T],
                             P.sb[mbn][:, :])
                v.tensor_mul(u[:, T:2 * T], u[:, 0:T], P.sb[rbn][:, :])
                s_ap.inc(v.tensor_scalar(qn[:, (i % 2) * T:(i % 2 + 1) * T],
                                         u[:, T:2 * T], P.sb[wn][:, oc:oc + 1],
                                         P.sb[bn][:, oc:oc + 1], OP.mult, OP.add))
            P.vec(v)
            apn = s_ap.n

            if grp == 0:
                def syo(sy, i=i, oc=oc):
                    sy.wait_ge(s_ap.h, apn)
                    s_ao.inc(sy.dma_start(out=P.dram["d_q"][oc],
                                          in_=qn[:, (i % 2) * T:(i % 2 + 1) * T]), 16)
                P.syn(syo)
            else:
                def go(g, i=i, oc=oc):
                    g.wait_ge(s_ap.h, apn)
                    par = P.par_reg
                    with g.If(par):
                        g.dma_start(out=P.dram["sh_k"][1, oc * 128:(oc + 1) * 128,
                                                       chunk * T:(chunk + 1) * T],
                                    in_=qn[:, (i % 2) * T:(i % 2 + 1) * T]).then_inc(s_ko.h, 16)
                    with g.Else():
                        g.dma_start(out=P.dram["sh_k"][0, oc * 128:(oc + 1) * 128,
                                                       chunk * T:(chunk + 1) * T],
                                    in_=qn[:, (i % 2) * T:(i % 2 + 1) * T]).then_inc(s_ko.h, 16)
                    s_ko.n += 16
                P.pool(go)

    if A0LVL <= 2:
        return s_ko, s_ko.n

    s_rl = P.sem(tag + "rl")
    s_rdy = P.sem(tag + "rdy")
    kon = s_ko.n

    def syr(sy):
        sy.wait_ge(P.sem(tag + "zz").h, P.sem(tag + "zz").n)
        s_rl.inc(sy.dma_start(out=P.sb["rel"][:, :].rearrange("p (q k) -> p q k", k=2048),
                              in_=P.dram["a0_rel"][chunk]
                              .rearrange("q p k -> p q k")), 16)
    P.syn(syr)

    def gfl(g):
        g.wait_ge(s_ko.h, kon)
        par = P.par_reg
        with g.If(par):
            g.dma_start(out=P.dram["sh_flag"][1, chunk:chunk + 1, :],
                        in_=P.sb["flg"][chunk:chunk + 1, :]).then_inc(s_rdy.h, 16)
        with g.Else():
            g.dma_start(out=P.dram["sh_flag"][0, chunk:chunk + 1, :],
                        in_=P.sb["flg"][chunk:chunk + 1, :]).then_inc(s_rdy.h, 16)
        s_rdy.n += 16
        with g.register(tag + "more%d" % chunk) as more, g.register(tag + "rfv%d" % chunk) as rfv, \
             g.register(tag + "rtv%d" % chunk) as rtv, g.register(tag + "rcnt%d" % chunk) as rcnt:
            g.reg_mov(rcnt, 0)
            for ph in range(chunk + 1):
                g.reg_load(rtv, P.sb["flg"][ph:ph + 1, 0:1])
                g.reg_mov(more, 1)
                with g.While(more):
                    g.reg_alu(rcnt, rcnt, 16, op=OP.add)
                    with g.If(par):
                        g.dma_start(out=P.sb["pflg"][:, :],
                                    in_=P.dram["sh_flag"][0, ph:ph + 1, :]).then_inc(
                                        P.sem(tag + "pl").h, 16)
                    with g.Else():
                        g.dma_start(out=P.sb["pflg"][:, :],
                                    in_=P.dram["sh_flag"][1, ph:ph + 1, :]).then_inc(
                                        P.sem(tag + "pl").h, 16)
                    g.wait_ge(P.sem(tag + "pl").h, rcnt)
                    g.reg_load(rfv, P.sb["pflg"][0:1, 0:1])
                    g.reg_alu(more, rfv, rtv, op=OP.not_equal)
        g.sem_inc(s_rdy.h, 1)
        s_rdy.n += 1
    P.pool(gfl)
    rdyn = s_rdy.n
    rln = s_rl.n
    if A0LVL <= 3:
        return s_rdy, s_rdy.n

    blocks = _a0_blocks(chunk)
    nkb = len(blocks)
    nd = 2 * nkb + 1
    s_hd = P.sem(tag + "hd")
    s_sc = P.sem(tag + "sc")
    s_zz = P.sem(tag + "zz")
    s_nm = P.sem(tag + "nm")
    s_ex = P.sem(tag + "ex")
    s_ls = P.sem(tag + "ls")
    s_rc = P.sem(tag + "rc")
    s_pn = P.sem(tag + "pn")
    s_pt = P.sem(tag + "pt")
    s_pe2 = P.sem(tag + "pe2")
    s_pv = P.sem(tag + "pv")
    s_oc = P.sem(tag + "oc")
    hd0, sc0, zz0, nm0, ex0, ls0, rc0 = (s_hd.n, s_sc.n, s_zz.n, s_nm.n, s_ex.n,
                                         s_ls.n, s_rc.n)
    pn0, pt0, pe20, pv0, oc0 = s_pn.n, s_pt.n, s_pe2.n, s_pv.n, s_oc.n
    aon = s_ao.n
    ksb, vsb, qsb = P.sb["ksb"], P.sb["vsb"], P.sb["qsb"]
    zat, pat, pnt, ptsb = P.sb["zat"], P.sb["pat"], P.sb["pnt"], P.sb["ptsb"]

    for h in range(NHEAD):
        hb = (h % 2)

        def syh(sy, h=h, hb=hb):
            if h == 0:
                sy.wait_ge(s_rdy.h, rdyn)
                sy.wait_ge(s_rl.h, rln)
                sy.wait_ge(s_ao.h, aon)
            if h >= 2:
                sy.wait_ge(s_oc.h, oc0 + 4 * (h - 1))
            for kb, (slot, col0) in enumerate(blocks):
                s_hd.inc(sy.dma_start(
                    out=ksb[:, hb * 2048 + kb * 512: hb * 2048 + (kb + 1) * 512],
                    in_=P.dram["sh_k"][slot, h * 128:(h + 1) * 128, col0:col0 + 512]), 16)
                s_hd.inc(sy.dma_start(
                    out=vsb[:, hb * 2048 + kb * 512: hb * 2048 + (kb + 1) * 512]
                    .rearrange("p (tt c) -> p tt c", c=128),
                    in_=P.dram["sh_v"][slot, col0:col0 + 512, h * 128:(h + 1) * 128]
                    .rearrange("(tt p) c -> p tt c", p=128)), 16)
            s_hd.inc(sy.dma_start(out=qsb[:, hb * T:(hb + 1) * T],
                                  in_=P.dram["d_q"][h]), 16)
        P.syn(syh)
        hdn = s_hd.n
        slope = float(SLOPES[h])

        for qt in range(4):
            for kb in range(nkb):
                def pesc(t, h=h, hb=hb, qt=qt, kb=kb):
                    if qt == 0 and kb == 0:
                        t.wait_ge(s_hd.h, hdn)
                    t.wait_ge(s_zz.h, max(zz0, s_zz.n - 1))
                    s_sc.inc(t.matmul(
                        P.ps["m0" if kb % 2 == 0 else "m1"][:, :],
                        qsb[:, hb * T + qt * 128: hb * T + (qt + 1) * 128],
                        ksb[:, hb * 2048 + kb * 512: hb * 2048 + (kb + 1) * 512],
                        start=True, stop=True))
                P.pe(pesc)
                scn = s_sc.n

                def vz(v, h=h, qt=qt, kb=kb, slope=slope):
                    v.wait_ge(s_sc.h, scn)
                    s_zz.inc(v.scalar_tensor_tensor(
                        zat[:, kb * 512:(kb + 1) * 512],
                        P.sb["rel"][:, qt * 2048 + kb * 512: qt * 2048 + (kb + 1) * 512],
                        slope,
                        P.ps["m0" if kb % 2 == 0 else "m1"][:, :],
                        OP.mult, OP.add))
                P.vec(vz)
            zzn = s_zz.n

            exprev = s_ex.n

            def vnm(v, qt=qt):
                v.wait_ge(s_zz.h, zzn)
                v.wait_ge(s_ex.h, exprev)
                s_nm.inc(v.reduce_max(P.sb["nmx"][:, :], zat[:, 0:nkb * 512],
                                      axis=AX.X, negate=True))
            P.vec(vnm)
            nmn = s_nm.n

            ptprev = s_pt.n
            for kb in range(nkb):
                def aex(a, qt=qt, kb=kb, ptprev=ptprev):
                    a.wait_ge(s_nm.h, nmn)
                    if kb == 0:
                        a.wait_ge(s_pt.h, ptprev)
                        a.wait_ge(s_rc.h, s_rc.n)
                    s_ex.inc(a.activation(pat[:, kb * 512:(kb + 1) * 512],
                                          zat[:, kb * 512:(kb + 1) * 512], AF.Exp,
                                          bias=P.sb["nmx"][:, 0:1],
                                          accum_out=P.sb["lsm"][:, kb:kb + 1]))
                P.act(aex)
            exn = s_ex.n

            def vls(v):
                v.wait_ge(s_ex.h, exn)
                s_ls.inc(v.reduce_sum(P.sb["ltot"][:, :], P.sb["lsm"][:, 0:nkb], axis=AX.X))
                v.wait_ge(s_ls.h, s_ls.n)
                s_rc.inc(v.reciprocal(P.sb["rcpa"][:, :], P.sb["ltot"][:, :]))
                v.wait_ge(s_rc.h, s_rc.n)
            P.vec(vls)
            rcn = s_rc.n

            for kb in range(nkb):
                def gpn(g, kb=kb):
                    g.wait_ge(s_rc.h, rcn)
                    if kb == 0:
                        g.wait_ge(s_pt.h, s_pt.n)
                    s_pn.inc(g.tensor_scalar(pnt[:, kb * 512:(kb + 1) * 512],
                                             pat[:, kb * 512:(kb + 1) * 512],
                                             P.sb["rcpa"][:, 0:1], None, OP.mult))
                P.pool(gpn)
                pnn = s_pn.n

                def pet(t, kb=kb):
                    t.wait_ge(s_pn.h, pnn)
                    t.wait_ge(s_pe2.h, max(pe20, s_pe2.n - 1))
                    for tt in range(4):
                        mm = t.transpose(
                            ps_w[:, (kb % 2) * 512 + tt * 128:(kb % 2) * 512 + (tt + 1) * 128],
                            pnt[:, kb * 512 + tt * 128: kb * 512 + (tt + 1) * 128],
                            P.sb["identb"][:, :])
                    s_pt.inc(mm)
                P.pe(pet)
                ptn = s_pt.n

                def vpt(v, kb=kb):
                    v.wait_ge(s_pt.h, ptn)
                    v.wait_ge(s_pv.h, max(pv0, s_pv.n - 1))
                    s_pe2.inc(v.tensor_copy(ptsb[:, (kb % 2) * 512:(kb % 2 + 1) * 512],
                                            ps_w[:, (kb % 2) * 512:(kb % 2 + 1) * 512]))
                P.vec(vpt)
                pen = s_pe2.n

                def pev(t, h=h, hb=hb, kb=kb, qt=qt):
                    t.wait_ge(s_pe2.h, pen)
                    if kb == 0 and not (h == 0 and qt == 0 and chunk == 0):
                        t.wait_ge(s_oc.h, s_oc.n)
                    for tt in range(4):
                        mm = t.matmul(
                            ps_x[:, 0:128],
                            vsb[:, hb * 2048 + kb * 512 + tt * 128:
                                hb * 2048 + kb * 512 + (tt + 1) * 128],
                            ptsb[:, (kb % 2) * 512 + tt * 128:(kb % 2) * 512 + (tt + 1) * 128],
                            start=(kb == 0 and tt == 0), stop=(kb == nkb - 1 and tt == 3),
                            skip_group_check=True)
                    s_pv.inc(mm)
                P.pe(pev)
            pvn = s_pv.n

            def voc(v, h=h, qt=qt):
                v.wait_ge(s_pv.h, pvn)
                s_oc.inc(v.tensor_copy(xn[:, h * T + qt * 128: h * T + (qt + 1) * 128],
                                       ps_x[:, 0:128]))
            P.vec(voc)

    ocn = s_oc.n
    if A0LVL <= 4:
        return s_oc, s_oc.n

    def mmW(t, j, wbuf, ps):
        if j == 0:
            t.wait_ge(s_oc.h, ocn)
        for c in range(NCH):
            mm = t.matmul(ps[:, :], wbuf[:, c * 128:(c + 1) * 128],
                          xn[:, c * T:(c + 1) * T], start=(c == 0), stop=(c == NCH - 1))
        return mm

    s_wd = P.sem(tag + "We")

    def evW(j, ps, mm_target):
        def v(v, j=j):
            v.wait_ge(P.sem(tag + "Wm").h, mm_target)
            s_wd.inc(v.tensor_add(P.sb["xt"][:, j * T:(j + 1) * T],
                                  P.sb["xt"][:, j * T:(j + 1) * T], ps[:, :]))
        P.vec(v)

    emit_proj(P, tag + "W", P.dram["a0_wout"], 32, 4096, mmW, evW)
    return s_wd, s_wd.n



def build(steps):
    nc = bass.Bass("TRN2", target_bir_lowering=False)
    want = set(steps)
    with contextlib.ExitStack() as st:
        d = declare_io(nc, want)
        block = st.enter_context(nc.Block())
        P = Prog(nc, block, st)
        P.dram = d
        alloc_common(P, want)
        emit_init(P, want)
        for chunk in (0, 1):
            P.cur_chunk = chunk
            s_ld = emit_stream_load(P, chunk)
            deps = [(s_ld, s_ld.n)]
            for i, step in enumerate(steps):
                s_x, xn_n = emit_ln(P, "ln%d" % i, deps)
                if step == "lndump":
                    sdmp = P.sem("dmp")

                    def gd(g, chunk=chunk, s_x=s_x, xn_n=xn_n):
                        g.wait_ge(s_x.h, xn_n)
                        sdmp.inc(g.dma_start(
                            out=P.dram["y_out"][:, :, chunk * T:(chunk + 1) * T],
                            in_=P.sb["xn"][:, :].rearrange("p (c t) -> p c t", t=T)), 16)
                        g.wait_ge(sdmp.h, sdmp.n)
                        if chunk == 1:
                            sdmp.inc(g.dma_start(
                                out=P.dram["y_out"][:, 0:1, 0:T],
                                in_=P.sb["m_b"][:, :].unsqueeze(1)), 16)
                            sdmp.inc(g.dma_start(
                                out=P.dram["y_out"][:, 1:2, 0:T],
                                in_=P.sb["r_b"][:, :].unsqueeze(1)), 16)
                            g.wait_ge(sdmp.h, sdmp.n)
                    P.pool(gd)

                    def syw(sy, s_x=s_x, xn_n=xn_n):
                        sy.wait_ge(s_x.h, xn_n)
                    P.syn(syw)
                    s_d, d_n = s_x, xn_n
                    continue
                if step == "f0":
                    s_d, d_n = emit_ffn0(P, "f0_%d" % i, s_x, xn_n)
                elif step == "f1":
                    s_d, d_n = emit_ffn1(P, "f1_%d" % i, s_x, xn_n)
                elif step == "a0":
                    s_d, d_n = emit_attn0(P, "a0_%d" % i, s_x, xn_n, chunk)
                elif step == "a1":
                    s_d, d_n = emit_attn1(P, "a1_%d" % i, s_x, xn_n)
                else:
                    raise ValueError(step)
                deps = [(s_d, d_n)]
            if steps and steps[-1] == "lndump":
                continue
            emit_stream_store(P, chunk, deps)
    return nc


_prog_cache = {}


def _get_prog(steps):
    key = tuple(steps)
    if key not in _prog_cache:
        _prog_cache[key] = build(list(steps))
    return _prog_cache[key]


UNFUSED_SEQ = (("f0",), ("a0",), ("f0",), ("f1",), ("a1",), ("f1",))


def kernel(**inputs):
    in_maps = prep_inputs(inputs)
    res = None
    for steps in UNFUSED_SEQ:
        nc = _get_prog(steps)
        res = run_bass_kernel_spmd(nc, in_maps, core_ids=list(range(8)))
        for c in range(8):
            in_maps[c] = dict(in_maps[c])
            in_maps[c]["x_in"] = np.ascontiguousarray(
                np.asarray(res.results[c]["y_out"], dtype=np.float32))
    return assemble_output(res.results)


# version 02
# speedup vs baseline: 1.2562x; 1.2562x over previous
import contextlib
import os
import random

EVAC_ACT = int(os.environ.get("K_EVAC_ACT", "0"))

import numpy as np
import ml_dtypes

import concourse.bass as bass
import concourse.mybir as mybir
from concourse.bass_utils import run_bass_kernel_spmd
from concourse.alu_op_type import AluOpType as OP

F32 = mybir.dt.float32
BF16 = mybir.dt.bfloat16
F16 = mybir.dt.float16
I32 = mybir.dt.int32
AF = mybir.ActivationFunctionType
AX = mybir.AxisListType

D = 4096
NCH = 32
T = 512
S = 2048
B = 4
NHEAD = 32
HDIM = 128
LN_EPS = 1e-5
CLIP = 8.0
SCALE0 = float(HDIM) ** -0.5
SCALE1 = float(HDIM) ** -0.5
ALIBI_MAX = 8.0

BLOCK0 = {0: 0, 1: 512}
BLOCK1 = {0: 1536, 1: 1024}

SLOPES = (1.0 / np.power(2.0, np.arange(1, 33) * (ALIBI_MAX / 32.0))).astype(np.float32)

PERM = (np.arange(4096) % 128) * 32 + (np.arange(4096) // 128)
INV = (np.arange(4096) % 32) * 128 + (np.arange(4096) // 32)



def _img_from_T(wT, ob=128):
    din, dout = wT.shape
    nblk = dout // ob
    a = wT.reshape(din // 128, 128, nblk, ob).transpose(2, 1, 0, 3)
    return np.ascontiguousarray(a.reshape(nblk, 128, (din // 128) * ob).astype(np.float32, copy=False))


def _cols(vec):
    return np.ascontiguousarray(vec.reshape(32, 128).T.astype(np.float32, copy=False))


def prep_inputs(inp):
    f = {k: np.asarray(v, dtype=np.float32) for k, v in inp.items()}
    f["L0_attn_hidden_states"] = f["L0_attn_hidden_states"][:, :, PERM]
    f["L0_ffn_W"] = f["L0_ffn_W"][PERM][:, PERM]
    f["L0_ffn_bias"] = f["L0_ffn_bias"][PERM]
    f["L0_ffn_A"] = f["L0_ffn_A"][:, :, PERM]
    f["L0_ffn_Bf"] = f["L0_ffn_Bf"][:, PERM, :]
    f["L0_attn_Wqkv"] = f["L0_attn_Wqkv"][:, PERM]
    f["L0_attn_Wout"] = f["L0_attn_Wout"][PERM, :]
    f["L1_attn_Wq"] = f["L1_attn_Wq"][PERM, :, :]
    f["L1_attn_Wk"] = f["L1_attn_Wk"][PERM, :, :]
    f["L1_attn_Wv"] = f["L1_attn_Wv"][PERM, :, :]
    f["L1_ffn_W"] = f["L1_ffn_W"][PERM][:, PERM]
    f["L1_ffn_b"] = f["L1_ffn_b"][PERM]
    f["L1_ffn_A"] = f["L1_ffn_A"][:, PERM]
    f["L1_ffn_B"] = f["L1_ffn_B"][PERM, :]
    f["L1_ffn_gate_w"] = f["L1_ffn_gate_w"][:, PERM]
    c = {}

    W0T = np.ascontiguousarray(f["L0_ffn_W"].T)
    BallT = np.ascontiguousarray(f["L0_ffn_Bf"].transpose(0, 2, 1).reshape(256, 4096))
    c["f0_main"] = np.ascontiguousarray(
        np.concatenate([_img_from_T(W0T), _img_from_T(BallT)], axis=2))
    c["f0_A"] = _img_from_T(np.ascontiguousarray(f["L0_ffn_A"].reshape(256, 4096).T))
    c["f0_bias"] = _cols(f["L0_ffn_bias"])
    c["f0_coef"] = np.ascontiguousarray(np.repeat(f["L0_ffn_coeffs"], 16).reshape(2, 128).T)
    c["f0_scal"] = np.ascontiguousarray(np.repeat(f["L0_ffn_scales"], 16).reshape(2, 128).T)

    c["a0_wqkv"] = _img_from_T(np.ascontiguousarray(f["L0_attn_Wqkv"].T))
    c["a0_wout"] = _img_from_T(np.ascontiguousarray(f["L0_attn_Wout"].T))
    c["a0_qlnw"] = _cols(f["L0_attn_q_ln_w"])
    c["a0_qlnb"] = _cols(f["L0_attn_q_ln_b"])
    c["a0_klnw"] = _cols(f["L0_attn_k_ln_w"])
    c["a0_klnb"] = _cols(f["L0_attn_k_ln_b"])

    c["a1_wq"] = _img_from_T(np.ascontiguousarray(
        f["L1_attn_Wq"].transpose(0, 2, 1).reshape(4096, 4096)))
    c["a1_wk"] = _img_from_T(np.ascontiguousarray(
        f["L1_attn_Wk"].transpose(0, 2, 1).reshape(4096, 1024)))
    c["a1_wv"] = _img_from_T(np.ascontiguousarray(
        f["L1_attn_Wv"].transpose(0, 2, 1).reshape(4096, 1024)))
    c["a1_bq"] = np.ascontiguousarray(f["L1_attn_bq"])
    c["a1_bk"] = np.ascontiguousarray(f["L1_attn_bk"])
    c["a1_bv"] = np.ascontiguousarray(f["L1_attn_bv"])

    W1T = np.ascontiguousarray(f["L1_ffn_W"].T)
    B1Tp = np.concatenate([np.ascontiguousarray(f["L1_ffn_B"].T),
                           np.zeros((64, 4096), np.float32)], axis=0)
    c["f1_main"] = np.ascontiguousarray(
        np.concatenate([_img_from_T(W1T), _img_from_T(B1Tp)], axis=2))
    c["f1_A"] = _img_from_T(np.ascontiguousarray(f["L1_ffn_A"].T), ob=64)
    c["f1_gate"] = np.ascontiguousarray(
        f["L1_ffn_gate_w"].T.reshape(32, 128, 4).transpose(1, 0, 2).reshape(128, 128))
    c["f1_gateb"] = np.ascontiguousarray(f["L1_ffn_gate_b"].reshape(1, 4))
    c["f1_bias"] = _cols(f["L1_ffn_b"])

    BIG = np.float16(-30000.0)
    rels = {}
    for par in (0, 1):
        tabs = []
        for chunk in (0, 1):
            nk = 1024 if chunk == 0 else 2048
            q0 = (BLOCK0 if chunk == 0 else BLOCK1)[par]
            tab = np.zeros((4, 128, 2048), np.float16)
            for qt in range(4):
                qabs = q0 + qt * 128 + np.arange(128)[:, None]
                k = np.arange(nk)[None, :]
                r = (k - qabs).astype(np.float32)
                r = np.where(k <= qabs, r, np.float32(-30000.0) * 256.0)
                tab[qt, :, 0:nk] = r.astype(np.float16)
            tabs.append(tab)
        rels[par] = np.ascontiguousarray(np.stack(tabs, 0))
    nonce = random.getrandbits(24)
    flags = np.zeros((4, 16), np.int32)
    for i in range(4):
        flags[i, :] = nonce + 1 + i
    c["sync_flags"] = flags

    x = f["L0_attn_hidden_states"]
    in_maps = []
    for core in range(8):
        b, par = core // 2, core % 2
        rows = np.concatenate([x[b, BLOCK0[par]:BLOCK0[par] + T, :],
                               x[b, BLOCK1[par]:BLOCK1[par] + T, :]], axis=0)
        ximg = np.ascontiguousarray(rows.T.reshape(32, 128, 1024).transpose(1, 0, 2))
        m = dict(c)
        m["x_in"] = ximg
        m["a0_rel"] = rels[par]
        in_maps.append(m)
    return in_maps


def assemble_output(results):
    out = np.empty((B, S, D), np.float32)
    for core in range(8):
        b, par = core // 2, core % 2
        y = np.asarray(results[core]["y_out"])
        rows = y.transpose(1, 0, 2).reshape(4096, 1024).T[:, PERM.argsort()] if False \
            else y.transpose(1, 0, 2).reshape(4096, 1024).T
        rows = rows[:, INV]
        out[b, BLOCK0[par]:BLOCK0[par] + T, :] = rows[:T]
        out[b, BLOCK1[par]:BLOCK1[par] + T, :] = rows[T:]
    return out



class Sem:
    __slots__ = ("h", "n")

    def __init__(self, h):
        self.h = h
        self.n = 0

    def inc(self, instr, k=1):
        instr.then_inc(self.h, k)
        self.n += k
        return instr


class Prog:
    def __init__(self, nc, block, stack):
        self.nc = nc
        self.block = block
        self.st = stack
        self.sems = {}
        self.dram = {}
        self.sb = {}

    def sem(self, name):
        if name not in self.sems:
            self.sems[name] = Sem(self.st.enter_context(self.nc.semaphore(name)))
        return self.sems[name]

    def pe(self, fn):
        self.block.tensor(fn)

    def vec(self, fn):
        self.block.vector(fn)

    def act(self, fn):
        self.block.scalar(fn)

    def pool(self, fn):
        self.block.gpsimd(fn)

    def syn(self, fn):
        self.block.sync(fn)

    def sbuf(self, name, shape, dtype):
        t = self.st.enter_context(self.nc.sbuf_tensor(name, shape, dtype))
        self.sb[name] = t
        return t

    def psum(self, name, shape, dtype=F32):
        return self.st.enter_context(self.nc.psum_tensor(name, shape, dtype))


def declare_io(nc, want):
    d = {}

    def inp(name, shape, dtype=F32):
        d[name] = nc.dram_tensor(name, shape, dtype, kind="ExternalInput").ap()

    inp("x_in", [128, 32, 1024])
    d["y_out"] = nc.dram_tensor("y_out", [128, 32, 1024], F32, kind="ExternalOutput").ap()
    d["dbg"] = nc.dram_tensor("dbg", [128, 4096], F32, kind="ExternalOutput").ap()
    inp("sync_flags", [4, 16], I32)
    if "f0" in want:
        inp("f0_main", [32, 128, 4352])
        inp("f0_A", [2, 128, 4096])
        inp("f0_bias", [128, 32])
        inp("f0_coef", [128, 2])
        inp("f0_scal", [128, 2])
    if "a0" in want:
        inp("a0_wqkv", [96, 128, 4096])
        inp("a0_wout", [32, 128, 4096])
        for n in ("a0_qlnw", "a0_qlnb", "a0_klnw", "a0_klnb"):
            inp(n, [128, 32])
        d["a0_rel"] = nc.dram_tensor("a0_rel", [2, 4, 128, 2048], F16,
                                     kind="ExternalInput").ap()
        d["d_qraw"] = nc.dram_tensor("d_qraw", [32, 128, T], BF16, kind="Internal").ap()
        d["d_kraw"] = nc.dram_tensor("d_kraw", [32, 128, T], BF16, kind="Internal").ap()
        d["d_q"] = nc.dram_tensor("d_q", [32, 128, T], BF16, kind="Internal").ap()
        d["sh_k"] = nc.dram_tensor("sh_k", [2, 4096, 1024], BF16, kind="Internal",
                                   addr_space="Shared").ap()
        d["sh_v"] = nc.dram_tensor("sh_v", [2, 1024, 4096], BF16, kind="Internal",
                                   addr_space="Shared").ap()
        d["sh_flag"] = nc.dram_tensor("sh_flag", [2, 4, 16], I32, kind="Internal",
                                      addr_space="Shared").ap()
    if "a1" in want:
        inp("a1_wq", [32, 128, 4096])
        inp("a1_wk", [8, 128, 4096])
        inp("a1_wv", [8, 128, 4096])
        inp("a1_bq", [128, 32])
        inp("a1_bk", [128, 8])
        inp("a1_bv", [128, 8])
    if "f1" in want:
        inp("f1_main", [32, 128, 4224])
        inp("f1_A", [1, 128, 2048])
        inp("f1_gate", [128, 128])
        inp("f1_gateb", [1, 4])
        inp("f1_bias", [128, 32])
    return d


def alloc_common(P, want):
    P.sbuf("xt", [128, NCH * T], F32)
    P.sbuf("xn", [128, NCH * T], BF16)
    P.sbuf("wbuf0", [128, 4352], BF16)
    P.sbuf("wbuf1", [128, 4352], BF16)
    P.sbuf("identb", [128, 128], BF16)
    P.sbuf("ones_col_b", [128, 1], BF16)
    P.sbuf("ones_row_b", [1, 128], BF16)
    P.sbuf("ones_row_f", [1, 128], F32)
    P.sbuf("flg", [4, 16], I32)
    P.sbuf("pflg", [1, 16], I32)
    P.sbuf("stat", [1, 4 * T], F32)
    P.sbuf("m_b", [128, T], F32)
    P.sbuf("r_b", [128, T], F32)
    P.sbuf("xb16", [128, 2 * T], BF16)
    P.sbuf("sqb", [128, 2 * T], BF16)
    P.sbuf("uapp", [128, 2 * T], F32)
    P.sbuf("tmp", [128, 2 * T], F32)
    if "f0" in want or "f1" in want:
        P.sbuf("usb", [128, 2 * T], BF16)
    if "f1" in want:
        P.sbuf("msk", [128, T], BF16)
        P.sbuf("gsc", [128, 16], F32)
        P.sbuf("gmx", [128, 4], F32)
        P.sbuf("gms", [128, 16], BF16)
        P.sbuf("mtt", [4, 4 * 128], BF16)
    if "a0" in want or "a1" in want:
        ovl = P.sbuf("ovl", [128, 12288], BF16)
    if "a0" in want:
        P.sb["qkstage"] = P.sb["xb16"][:, :]
        P.sb["sqq"] = P.sb["sqb"][:, :]
        P.sbuf("vout", [128, T], BF16)
        P.sb["rel"] = ovl[:, 0:8192].bitcast(F16)
        P.sb["zat"] = ovl[:, 8192:10240]
        P.sb["pat"] = ovl[:, 10240:12288]
        P.sbuf("ksb", [128, 2 * 2048], BF16)
        P.sbuf("vsb", [128, 2 * 2048], BF16)
        P.sbuf("qsb", [128, 2 * T], BF16)
        P.sbuf("qnst", [128, 2 * T], BF16)
        P.sbuf("km_b", [128, T], F32)
        P.sbuf("kr_b", [128, T], F32)
        P.sbuf("ptsb", [128, 2 * T], BF16)
        P.sbuf("nmx", [128, 1], F32)
        P.sbuf("lsm", [128, 4], F32)
        P.sbuf("ltot", [128, 1], F32)
        P.sbuf("rcpa", [128, 1], F32)
    if "a1" in want:
        P.sb["qgb"] = ovl[:, 0:4096]
        P.sb["k1b"] = ovl[:, 4096:8192]
        P.sb["v1b"] = ovl[:, 8192:12288]
        P.sbuf("tqsb", [8, T], BF16)
        P.sbuf("tksb", [8, T], BF16)
        P.sbuf("p1sb", [128, T], BF16)
        P.sbuf("pt1sb", [128, T], BF16)
        P.sbuf("o8sb", [8, T], BF16)
        P.sbuf("ngm1", [128, 4], F32)
        P.sbuf("lsm1", [128, 4], F32)
        P.sbuf("rcp1", [128, 4], F32)
    P.ps = {
        "m0": P.psum("ps_m0", [128, T], F32),
        "m1": P.psum("ps_m1", [128, T], F32),
        "s0": P.psum("ps_s0", [1, T], F32),
        "s1": P.psum("ps_s1", [1, T], F32),
        "x": P.psum("ps_x", [128, T], F32),
        "y": P.psum("ps_y", [128, 1024], BF16),
        "z": P.psum("ps_z", [128, T], F32),
        "w": P.psum("ps_w", [128, 1024], BF16),
    }
    P.last_proj = None
    return


def emit_init(P, want):
    sld = P.sem("i_ld")
    sio = P.sem("i_io")
    siv = P.sem("i_v")

    loads = [("flg", "sync_flags")]
    specs = {}
    if "f0" in want:
        specs.update({"c_f0_bias": ("f0_bias", [128, 32]), "c_f0_coef": ("f0_coef", [128, 2]),
                      "c_f0_scal": ("f0_scal", [128, 2])})
        P.sbuf("f0_bh", [128, 32], F32)
        P.sbuf("f0_w", [128, 2], F32)
    if "a0" in want:
        for n in ("a0_qlnw", "a0_qlnb", "a0_klnw", "a0_klnb"):
            specs["c_" + n] = (n, [128, 32])
        P.sbuf("a0_qws", [128, 32], F32)
        P.sbuf("a0_qbs", [128, 32], F32)
    if "a1" in want:
        specs.update({"c_a1_bq": ("a1_bq", [128, 32]), "c_a1_bk": ("a1_bk", [128, 8]),
                      "c_a1_bv": ("a1_bv", [128, 8])})
        P.sbuf("a1_bqg", [128, 8], F32)
        P.sbuf("z4", [8, 32], BF16)
        P.sbuf("i_z4", [8, 32], F32)
    if "f1" in want:
        specs.update({"c_f1_gate": ("f1_gate", [128, 128]), "c_f1_gateb": ("f1_gateb", [1, 4]),
                      "c_f1_bias": ("f1_bias", [128, 32])})
        P.sbuf("f1_bh", [128, 32], F32)
        P.sbuf("gate_b", [128, 128], BF16)
        P.sbuf("gateb_b", [1, 4], BF16)
        P.sbuf("zrt", [4, 64], BF16)
        P.sbuf("i_blk", [1, 64], F32)
        P.sbuf("i_iop", [4, 64], F32)
    for sbn, (dn, sh) in specs.items():
        P.sbuf(sbn, sh, F32)
        loads.append((sbn, dn))
    scr_i = P.sbuf("scr_i", [128, 128], F32)
    P.sbuf("cst", [1, 8], F32)

    def g(g):
        P.par_reg = g.to_reg(g.partition_id() % 2)
        for sbn, dn in loads:
            sld.inc(g.dma_start(out=P.sb[sbn][:, :], in_=P.dram[dn]), 16)
        sio.inc(g.iota(scr_i[:, :], pattern=[[1, 128]], base=0, channel_multiplier=-1,
                       allow_small_or_imprecise_dtypes=True))
        if "a1" in want:
            sio.inc(g.iota(P.sb["i_z4"][:, :].rearrange("p (a b) -> p a b", b=4),
                           pattern=[[1, 8], [0, 4]], base=0, channel_multiplier=-1,
                           allow_small_or_imprecise_dtypes=True))
        if "f1" in want:
            sio.inc(g.iota(P.sb["i_iop"][:, :], pattern=[[0, 64]], base=0,
                           channel_multiplier=1, allow_small_or_imprecise_dtypes=True))
    P.pool(g)

    def v(v):
        v.memset(P.sb["cst"][0:1, 0:1], LN_EPS)
        v.memset(P.sb["ones_col_b"][:, :], 1.0)
        v.memset(P.sb["ones_row_b"][:, :], 1.0)
        v.memset(P.sb["ones_row_f"][:, :], 1.0)
        v.wait_ge(sio.h, sio.n)
        v.wait_ge(sld.h, sld.n)
        siv.inc(v.tensor_scalar(P.sb["identb"][:, :], scr_i[:, :], 0.0, None, OP.is_equal))
        if "f0" in want:
            v.tensor_scalar(P.sb["f0_bh"][:, :], P.sb["c_f0_bias"][:, :], 0.5, None, OP.mult)
            siv.inc(v.tensor_mul(P.sb["f0_w"][:, :], P.sb["c_f0_coef"][:, :],
                                 P.sb["c_f0_scal"][:, :]))
        if "a0" in want:
            v.tensor_scalar(P.sb["a0_qws"][:, :], P.sb["c_a0_qlnw"][:, :], SCALE0, None, OP.mult)
            siv.inc(v.tensor_scalar(P.sb["a0_qbs"][:, :], P.sb["c_a0_qlnb"][:, :],
                                    SCALE0, None, OP.mult))
        if "a1" in want:
            v.tensor_scalar(P.sb["z4"][:, :], P.sb["i_z4"][:, :], 0.0, None, OP.is_equal)
            siv.inc(v.reduce_sum(P.sb["a1_bqg"][:, :],
                                 P.sb["c_a1_bq"][:, :].rearrange("p (g j) -> p g j", j=4),
                                 axis=AX.X))
            v.wait_ge(siv.h, siv.n)
        if "f1" in want:
            v.tensor_scalar(P.sb["f1_bh"][:, :], P.sb["c_f1_bias"][:, :], 0.5, None, OP.mult)
            v.tensor_copy(P.sb["gate_b"][:, :], P.sb["c_f1_gate"][:, :])
            v.tensor_copy(P.sb["gateb_b"][:, :], P.sb["c_f1_gateb"][:, :])
            v.memset(P.sb["i_blk"][0:1, 0:8], 0.0)
            v.memset(P.sb["i_blk"][0:1, 8:16], 1.0)
            v.memset(P.sb["i_blk"][0:1, 16:32], 2.0)
            siv.inc(v.memset(P.sb["i_blk"][0:1, 32:64], 3.0))
    P.vec(v)

    if "f1" in want:
        ps = P.ps["x"]
        szb = P.sem("i_zb")

        def pe(t):
            t.wait_ge(siv.h, siv.n)
            szb.inc(t.matmul(ps[0:4, 0:64], P.sb["ones_row_f"][0:1, 0:4],
                             P.sb["i_blk"][0:1, :], start=True, stop=True))
        P.pe(pe)

        def v2(v):
            v.wait_ge(szb.h, szb.n)
            v.tensor_sub(P.sb["i_iop"][0:4, :], P.sb["i_iop"][0:4, :], ps[0:4, 0:64])
            siv.inc(v.tensor_scalar(P.sb["zrt"][:, :], P.sb["i_iop"][0:4, :], 0.0,
                                    None, OP.is_ge))
        P.vec(v2)

    def w(e):
        e.wait_ge(siv.h, siv.n)
        e.wait_ge(sld.h, sld.n)
    for sec in (P.pe, P.act, P.pool, P.syn):
        sec(w)



def emit_stream_load(P, chunk):
    s = P.sem("st_ld")

    def sy(sy):
        s.inc(sy.dma_start(out=P.sb["xt"][:, :].rearrange("p (c t) -> p c t", t=T),
                           in_=P.dram["x_in"][:, :, chunk * T:(chunk + 1) * T]), 16)
    P.syn(sy)
    return s


def emit_stream_store(P, chunk, deps):
    s = P.sem("st_st")

    def sy(sy):
        for dsem, dn in deps:
            sy.wait_ge(dsem.h, dn)
        s.inc(sy.dma_start(out=P.dram["y_out"][:, :, chunk * T:(chunk + 1) * T],
                           in_=P.sb["xt"][:, :].rearrange("p (c t) -> p c t", t=T)), 16)
        sy.wait_ge(s.h, s.n)
    P.syn(sy)
    return s



def emit_ln(P, tag, deps):
    xt, xn = P.sb["xt"], P.sb["xn"]
    xb, sq = P.sb["xb16"], P.sb["sqb"]
    stat, mb, rb = P.sb["stat"], P.sb["m_b"], P.sb["r_b"]
    ps_s0, ps_s1 = P.ps["s0"], P.ps["s1"]
    ps_b0, ps_b1 = P.ps["m0"], P.ps["m1"]

    s_c = P.sem(tag + "c")
    s_q = P.sem(tag + "q")
    s_m = P.sem(tag + "m")
    s_f = P.sem(tag + "f")
    s_a = P.sem(tag + "a")
    s_x = P.sem(tag + "x")

    c0 = s_c.n
    q0 = s_q.n
    m0 = s_m.n
    for c in range(NCH):
        def v(v, c=c):
            if c == 0:
                for dsem, dn in deps:
                    v.wait_ge(dsem.h, dn)
            if c >= 2:
                v.wait_ge(s_m.h, m0 + 2 * (c - 2) + 2)
            s_c.inc(v.tensor_copy(xb[:, (c % 2) * T:(c % 2 + 1) * T],
                                  xt[:, c * T:(c + 1) * T]))
        P.vec(v)

        def g(g, c=c):
            g.wait_ge(s_c.h, c0 + c + 1)
            if c >= 2:
                g.wait_ge(s_m.h, m0 + 2 * (c - 2) + 2)
            s_q.inc(g.tensor_mul(sq[:, (c % 2) * T:(c % 2 + 1) * T],
                                 xb[:, (c % 2) * T:(c % 2 + 1) * T],
                                 xb[:, (c % 2) * T:(c % 2 + 1) * T]))
        P.pool(g)

        def pe(t, c=c):
            t.wait_ge(s_c.h, c0 + c + 1)
            s_m.inc(t.matmul(ps_s0[0:1, :], P.sb["ones_col_b"][:, :],
                             xb[:, (c % 2) * T:(c % 2 + 1) * T],
                             start=(c == 0), stop=(c == NCH - 1), skip_group_check=True))
            t.wait_ge(s_q.h, q0 + c + 1)
            s_m.inc(t.matmul(ps_s1[0:1, :], P.sb["ones_col_b"][:, :],
                             sq[:, (c % 2) * T:(c % 2 + 1) * T],
                             start=(c == 0), stop=(c == NCH - 1), skip_group_check=True))
        P.pe(pe)

    mm_all = s_m.n

    def v2(v):
        v.wait_ge(s_m.h, mm_all)
        v.tensor_scalar(stat[0:1, 0:T], ps_s0[0:1, :], 1.0 / D, None, OP.mult)
        v.tensor_scalar(stat[0:1, T:2 * T], ps_s1[0:1, :], 1.0 / D, None, OP.mult)
        v.tensor_mul(stat[0:1, 2 * T:3 * T], stat[0:1, 0:T], stat[0:1, 0:T])
        s_f.inc(v.tensor_sub(stat[0:1, T:2 * T], stat[0:1, T:2 * T],
                             stat[0:1, 2 * T:3 * T]))
    P.vec(v2)
    f_var = s_f.n

    def a(a):
        a.wait_ge(s_f.h, f_var)
        a.activation(stat[0:1, 2 * T:3 * T], stat[0:1, T:2 * T], AF.Ln,
                     bias=P.sb["cst"][0:1, 0:1])
        s_f.inc(a.activation(stat[0:1, 3 * T:4 * T], stat[0:1, 2 * T:3 * T], AF.Exp,
                             scale=-0.5))
    P.act(a)
    f_rstd = s_f.n

    def pe2(t):
        t.wait_ge(s_f.h, f_rstd)
        s_f.inc(t.matmul(ps_b0[:, :], P.sb["ones_row_f"][:, :], stat[0:1, 0:T],
                         start=True, stop=True))
        s_f.inc(t.matmul(ps_b1[:, :], P.sb["ones_row_f"][:, :], stat[0:1, 3 * T:4 * T],
                         start=True, stop=True))
    P.pe(pe2)
    f_bc = s_f.n

    def v3(v):
        v.wait_ge(s_f.h, f_bc)
        v.tensor_copy(mb[:, :], ps_b0[:, :])
        s_f.inc(v.tensor_copy(rb[:, :], ps_b1[:, :]))
    P.vec(v3)
    f_cp = s_f.n

    a0 = s_a.n
    x0 = s_x.n
    u = P.sb["uapp"]
    for c in range(NCH):
        def v4(v, c=c):
            if c == 0:
                v.wait_ge(s_f.h, f_cp)
            if c >= 2:
                v.wait_ge(s_x.h, x0 + c - 1)
            s_a.inc(v.tensor_sub(u[:, (c % 2) * T:(c % 2 + 1) * T],
                                 xt[:, c * T:(c + 1) * T], mb[:, :]))
        P.vec(v4)

        def g4(g, c=c):
            g.wait_ge(s_a.h, a0 + c + 1)
            s_x.inc(g.tensor_mul(xn[:, c * T:(c + 1) * T],
                                 u[:, (c % 2) * T:(c % 2 + 1) * T], rb[:, :]))
        P.pool(g4)
    return s_x, s_x.n



def emit_proj(P, tag, img, nblk, totcols, mm_cb, evac_cb, consume_cb=None):
    wb = (P.sb["wbuf0"], P.sb["wbuf1"])
    ps = (P.ps["m0"], P.ps["m1"])
    s_w = P.sem(tag + "w")
    s_m = P.sem(tag + "m")
    s_e = P.sem(tag + "e")
    w0, m0, e0 = s_w.n, s_m.n, s_e.n
    prev = P.last_proj

    def g0(g):
        if prev is not None:
            g.wait_ge(prev[0].h, prev[1])
        s_w.inc(g.dma_start(out=wb[0][:, 0:totcols], in_=img[0]), 16)
        if nblk > 1:
            s_w.inc(g.dma_start(out=wb[1][:, 0:totcols], in_=img[1]), 16)
    P.pool(g0)

    for j in range(nblk):
        def pe(t, j=j):
            t.wait_ge(s_w.h, w0 + 16 * (j + 1))
            if j >= 2:
                t.wait_ge(s_e.h, e0 + j - 1)
            s_m.inc(mm_cb(t, j, wb[j % 2], ps[j % 2]))
        P.pe(pe)

        evac_cb(j, ps[j % 2], s_m.n)
        if consume_cb is not None and j >= 1:
            consume_cb(j - 1)

        def g1(g, j=j):
            if j + 2 < nblk:
                g.wait_ge(s_m.h, m0 + j + 1)
                s_w.inc(g.dma_start(out=wb[(j + 2) % 2][:, 0:totcols], in_=img[j + 2]), 16)
        P.pool(g1)
    if consume_cb is not None:
        consume_cb(nblk - 1)
    P.last_proj = (s_m, s_m.n)
    return s_e



def _dense_evac_scaled(P, tag, bias_cols, scale):
    s_e = P.sem(tag + "e")
    s_d = P.sem(tag + "d")
    e0 = s_e.n
    d0 = s_d.n

    def evac(j, ps, mm_target):
        def a(a, j=j):
            a.wait_ge(P.sem(tag + "m").h, mm_target)
            if j >= 2:
                a.wait_ge(s_d.h, d0 + j - 1)
            if EVAC_ACT:
                s_e.inc(a.activation(P.sb["tmp"][:, (j % 2) * T:(j % 2 + 1) * T], ps[:, :],
                                     AF.Identity, bias=bias_cols[:, j:j + 1], scale=scale))
            else:
                s_e.inc(a.tensor_scalar(P.sb["tmp"][:, (j % 2) * T:(j % 2 + 1) * T], ps[:, :],
                                        scale, bias_cols[:, j:j + 1], OP.mult, OP.add))
        (P.act if EVAC_ACT else P.vec)(a)

        def v(v, j=j):
            v.wait_ge(s_e.h, e0 + j + 1)
            s_d.inc(v.tensor_add(P.sb["xt"][:, j * T:(j + 1) * T],
                                 P.sb["xt"][:, j * T:(j + 1) * T],
                                 P.sb["tmp"][:, (j % 2) * T:(j % 2 + 1) * T]))
        P.vec(v)
    return evac, s_d


def emit_ffn0(P, tag, s_xn, xn_n):
    xn = P.sb["xn"]
    usb = P.sb["usb"]
    sA = P.sem(tag + "Ae")

    def mmA(t, j, wbuf, ps):
        if j == 0:
            t.wait_ge(s_xn.h, xn_n)
        for c in range(NCH):
            mm = t.matmul(ps[:, :], wbuf[:, c * 128:(c + 1) * 128],
                          xn[:, c * T:(c + 1) * T], start=(c == 0), stop=(c == NCH - 1))
        return mm

    def evA(j, ps, mm_target):
        def v(v):
            v.wait_ge(P.sem(tag + "Am").h, mm_target)
            sA.inc(v.tensor_scalar(usb[:, j * T:(j + 1) * T], ps[:, :],
                                   P.sb["f0_w"][:, j:j + 1], None, OP.mult))
        P.vec(v)

    emit_proj(P, tag + "A", P.dram["f0_A"], 2, 4096, mmA, evA)
    uA_n = sA.n

    def mmM(t, j, wbuf, ps):
        if j == 0:
            t.wait_ge(sA.h, uA_n)
        for c in range(NCH):
            t.matmul(ps[:, :], wbuf[:, c * 128:(c + 1) * 128],
                     xn[:, c * T:(c + 1) * T], start=(c == 0), stop=False)
        t.matmul(ps[:, :], wbuf[:, 32 * 128:33 * 128], usb[:, 0:T],
                 start=False, stop=False)
        return t.matmul(ps[:, :], wbuf[:, 33 * 128:34 * 128], usb[:, T:2 * T],
                        start=False, stop=True)

    evac, s_d = _dense_evac_scaled(P, tag + "M", P.sb["f0_bh"], 0.5)
    emit_proj(P, tag + "M", P.dram["f0_main"], 32, 4352, mmM, evac)
    return s_d, s_d.n


def emit_ffn1(P, tag, s_xn, xn_n):
    xn = P.sb["xn"]
    usb = P.sb["usb"]
    ps_g = P.ps["z"]
    ps_mt = P.ps["y"]
    ps_m64 = P.ps["x"]
    s_g = P.sem(tag + "g")
    s_gv = P.sem(tag + "gv")
    s_mt = P.sem(tag + "mt")
    s_mk = P.sem(tag + "mk")

    def peg(t):
        t.wait_ge(s_xn.h, xn_n)
        for tt in range(4):
            for c in range(NCH):
                t.matmul(ps_g[:, 16 + tt * 4:16 + (tt + 1) * 4],
                         xn[:, c * T + tt * 128: c * T + (tt + 1) * 128],
                         P.sb["gate_b"][:, c * 4:(c + 1) * 4],
                         start=(tt == 0 and c == 0), stop=False, skip_group_check=True)
            s_g.inc(t.matmul(ps_g[:, 16 + tt * 4:16 + (tt + 1) * 4],
                             P.sb["ones_row_b"][0:1, :], P.sb["gateb_b"][0:1, :],
                             start=False, stop=(tt == 3), skip_group_check=True))
    P.pe(peg)
    g_n = s_g.n

    def vg(v):
        v.wait_ge(s_g.h, g_n)
        s_gv.inc(v.tensor_copy(P.sb["gsc"][:, :], ps_g[:, 16:32]))
        v.wait_ge(s_gv.h, s_gv.n)
        s_gv.inc(v.reduce_max(P.sb["gmx"][:, :],
                              P.sb["gsc"][:, :].rearrange("p (a b) -> p a b", b=4),
                              axis=AX.X))
        v.wait_ge(s_gv.h, s_gv.n)
        for tt in range(4):
            s_gv.inc(v.tensor_scalar(P.sb["gms"][:, tt * 4:(tt + 1) * 4],
                                     P.sb["gsc"][:, tt * 4:(tt + 1) * 4],
                                     P.sb["gmx"][:, tt:tt + 1], None, OP.is_ge))
    P.vec(vg)
    gv_n = s_gv.n

    def pet(t):
        t.wait_ge(s_gv.h, gv_n)
        for tt in range(4):
            s_mt.inc(t.transpose(ps_mt[0:4, 512 + tt * 128:512 + (tt + 1) * 128],
                                 P.sb["gms"][:, tt * 4:(tt + 1) * 4], P.sb["identb"][:, :]))
    P.pe(pet)
    mt_n = s_mt.n

    def vt(v):
        v.wait_ge(s_mt.h, mt_n)
        s_mt.inc(v.tensor_copy(P.sb["mtt"][:, :], ps_mt[0:4, 512:1024]))
    P.vec(vt)
    mtc_n = s_mt.n

    def pem(t):
        t.wait_ge(s_mt.h, mtc_n)
        s_mk.inc(t.matmul(ps_m64[0:64, :], P.sb["zrt"][:, :],
                          P.sb["mtt"][0:4, :], start=True, stop=True))
    P.pe(pem)
    mk_n = s_mk.n

    def vm(v):
        v.wait_ge(s_mk.h, mk_n)
        s_mk.inc(v.tensor_copy(P.sb["msk"][0:64, 0:T], ps_m64[0:64, :]))
    P.vec(vm)
    msk_n = s_mk.n

    sA = P.sem(tag + "Ae")

    def mmA(t, j, wbuf, ps):
        for c in range(NCH):
            mm = t.matmul(ps[0:64, :], wbuf[:, c * 64:(c + 1) * 64],
                          xn[:, c * T:(c + 1) * T], start=(c == 0), stop=(c == NCH - 1))
        return mm

    def evA(j, ps, mm_target):
        def v(v):
            v.wait_ge(P.sem(tag + "Am").h, mm_target)
            v.wait_ge(s_mk.h, msk_n)
            sA.inc(v.tensor_mul(usb[0:64, 0:T], ps[0:64, :], P.sb["msk"][0:64, 0:T]))
        P.vec(v)

    emit_proj(P, tag + "A", P.dram["f1_A"], 1, 2048, mmA, evA)
    uA_n = sA.n

    def mmM(t, j, wbuf, ps):
        if j == 0:
            t.wait_ge(sA.h, uA_n)
        for c in range(NCH):
            t.matmul(ps[:, :], wbuf[:, c * 128:(c + 1) * 128],
                     xn[:, c * T:(c + 1) * T], start=(c == 0), stop=False)
        return t.matmul(ps[:, :], wbuf[0:64, 32 * 128:32 * 128 + 128],
                        usb[0:64, 0:T], start=False, stop=True)

    evac, s_d = _dense_evac_scaled(P, tag + "M", P.sb["f1_bh"], 0.5)
    emit_proj(P, tag + "M", P.dram["f1_main"], 32, 4224, mmM, evac)
    if os.environ.get("K_DBG") == "f1" and getattr(P, "cur_chunk", None) == 0:
        sdb = P.sem("dbg")
        dn = s_d.n

        def gdb(g):
            g.wait_ge(s_d.h, dn)
            sdb.inc(g.dma_start(out=P.dram["dbg"][:, 0:16], in_=P.sb["gsc"][:, :]), 16)
            sdb.inc(g.dma_start(out=P.dram["dbg"][:, 16:32], in_=P.sb["gms"][:, :]), 16)
            sdb.inc(g.dma_start(out=P.dram["dbg"][0:4, 32:544], in_=P.sb["mtt"][0:4, :]), 16)
            sdb.inc(g.dma_start(out=P.dram["dbg"][0:64, 544:1056], in_=P.sb["msk"][0:64, 0:T]), 16)
            sdb.inc(g.dma_start(out=P.dram["dbg"][0:64, 1056:1568], in_=P.sb["usb"][0:64, 0:T]), 16)
            g.wait_ge(sdb.h, sdb.n)
        P.pool(gdb)
    return s_d, s_d.n



def emit_attn1(P, tag, s_xn, xn_n):
    xn = P.sb["xn"]
    qgb, k1b, v1b = P.sb["qgb"], P.sb["k1b"], P.sb["v1b"]

    wb = (P.sb["wbuf0"], P.sb["wbuf1"])
    ps = (P.ps["m0"], P.ps["m1"])
    s_w = P.sem(tag + "qw")
    s_m = P.sem(tag + "qm")
    s_e = P.sem(tag + "qe")
    w0, m0_, e0 = s_w.n, s_m.n, s_e.n
    prev = P.last_proj
    img = P.dram["a1_wq"]

    def g0(g):
        if prev is not None:
            g.wait_ge(prev[0].h, prev[1])
        s_w.inc(g.dma_start(out=wb[0][:, 0:4096], in_=img[0]), 16)
        s_w.inc(g.dma_start(out=wb[1][:, 0:4096], in_=img[1]), 16)
    P.pool(g0)

    for h in range(32):
        gidx = h // 4

        def pe(t, h=h, gidx=gidx):
            t.wait_ge(s_w.h, w0 + 16 * (h + 1))
            if h == 0:
                t.wait_ge(s_xn.h, xn_n)
            if h % 4 == 0 and gidx >= 2:
                t.wait_ge(s_e.h, e0 + gidx - 1)
            for c in range(NCH):
                mm = t.matmul(ps[gidx % 2][:, :], wb[h % 2][:, c * 128:(c + 1) * 128],
                              xn[:, c * T:(c + 1) * T],
                              start=(h % 4 == 0 and c == 0),
                              stop=(h % 4 == 3 and c == NCH - 1), skip_group_check=True)
            s_m.inc(mm)
        P.pe(pe)

        if h % 4 == 3:
            mm_now = s_m.n

            def v(v, gidx=gidx):
                v.wait_ge(s_m.h, mm_now)
                s_e.inc(v.tensor_scalar(qgb[:, gidx * T:(gidx + 1) * T], ps[gidx % 2][:, :],
                                        P.sb["a1_bqg"][:, gidx:gidx + 1], None, OP.add))
            P.vec(v)

        def g1(g, h=h):
            if h + 2 < 32:
                g.wait_ge(s_m.h, m0_ + h + 1)
                s_w.inc(g.dma_start(out=wb[h % 2][:, 0:4096], in_=img[h + 2]), 16)
        P.pool(g1)
    P.last_proj = (s_m, s_m.n)
    qg_n = s_e.n

    def mk_kv(which, dst, bias_name, scaled):
        def mm(t, j, wbuf, ps_):
            for c in range(NCH):
                mm_ = t.matmul(ps_[:, :], wbuf[:, c * 128:(c + 1) * 128],
                               xn[:, c * T:(c + 1) * T], start=(c == 0), stop=(c == NCH - 1))
            return mm_

        s_ev = P.sem(tag + which + "e")

        def ev(j, ps_, mm_target):
            def v(v, j=j):
                v.wait_ge(P.sem(tag + which + "m").h, mm_target)
                if scaled:
                    s_ev.inc(v.tensor_scalar(dst[:, j * T:(j + 1) * T], ps_[:, :],
                                             P.sb[bias_name][:, j:j + 1], SCALE1,
                                             OP.add, OP.mult))
                else:
                    s_ev.inc(v.tensor_scalar(dst[:, j * T:(j + 1) * T], ps_[:, :],
                                             P.sb[bias_name][:, j:j + 1], None, OP.add))
            P.vec(v)
        emit_proj(P, tag + which, P.dram["a1_w" + which], 8, 4096, mm, ev)
        return s_ev

    s_k = mk_kv("k", k1b, "c_a1_bk", True)
    k_n = s_k.n
    s_v = mk_kv("v", v1b, "c_a1_bv", False)
    v_n = s_v.n

    ps_x, ps_y, ps_w_, ps_z = P.ps["x"], P.ps["y"], P.ps["w"], P.ps["z"]
    s_tr = P.sem(tag + "tr")
    s_te = P.sem(tag + "te")
    s_sc = P.sem(tag + "sc")
    s_nm = P.sem(tag + "nm")
    s_ex = P.sem(tag + "ex")
    s_pt = P.sem(tag + "pt")
    s_pte = P.sem(tag + "pe2")
    s_pv = P.sem(tag + "pv")
    s_o8 = P.sem(tag + "o8")
    s_ox = P.sem(tag + "ox")
    s_rc = P.sem(tag + "rc")
    s_fin = P.sem(tag + "fin")
    b_tr, b_te, b_sc, b_nm, b_ex = s_tr.n, s_te.n, s_sc.n, s_nm.n, s_ex.n
    b_pt, b_pte, b_pv, b_o8, b_ox, b_rc, b_fin = (s_pt.n, s_pte.n, s_pv.n, s_o8.n,
                                                  s_ox.n, s_rc.n, s_fin.n)
    qgv = qgb.rearrange("p (g t) -> p t g", t=T)
    kv_ = k1b.rearrange("p (g t) -> p t g", t=T)
    vv_ = v1b.rearrange("p (g t) -> p t g", t=T)
    NW = T // 4

    for w in range(NW):
        def pe(t, w=w):
            if w == 0:
                t.wait_ge(s_k.h, k_n)
                t.wait_ge(s_v.h, v_n)
                t.wait_ge(P.sem(tag + "qe").h, qg_n)
            if w >= 1:
                t.wait_ge(s_te.h, b_te + 2 * w)
            for i in range(4):
                tok = w * 4 + i
                t.transpose(ps_y[0:8, i * 128:(i + 1) * 128], qgv[:, tok, :],
                            P.sb["identb"][:, :])
                mmt = t.transpose(ps_y[0:8, 512 + i * 128:512 + (i + 1) * 128],
                                  kv_[:, tok, :], P.sb["identb"][:, :])
            s_tr.inc(mmt)
            t.wait_ge(s_te.h, b_te + 2 * (w + 1))
            if w >= 1:
                t.wait_ge(s_fin.h, b_fin + 4 * w)
            for i in range(4):
                mms = t.matmul(ps_x[:, i * 128:(i + 1) * 128],
                               P.sb["tqsb"][0:8, i * 128:(i + 1) * 128],
                               P.sb["tksb"][0:8, i * 128:(i + 1) * 128],
                               start=True, stop=True)
            s_sc.inc(mms)
            t.wait_ge(s_ex.h, b_ex + 4 * (w + 1))
            if w >= 1:
                t.wait_ge(s_pte.h, b_pte + w)
            for i in range(4):
                mmp = t.transpose(ps_w_[:, i * 128:(i + 1) * 128],
                                  P.sb["p1sb"][:, i * 128:(i + 1) * 128],
                                  P.sb["identb"][:, :])
            s_pt.inc(mmp)
            t.wait_ge(s_pte.h, b_pte + w + 1)
            if w >= 1:
                t.wait_ge(s_o8.h, b_o8 + w)
            for i in range(4):
                tok = w * 4 + i
                mmv = t.matmul(ps_z[0:8, i * 128:(i + 1) * 128], vv_[:, tok, :],
                               P.sb["pt1sb"][:, i * 128:(i + 1) * 128],
                               start=True, stop=True)
            s_pv.inc(mmv)
            t.wait_ge(s_o8.h, b_o8 + w + 1)
            for i in range(4):
                mme = t.matmul(ps_x[:, i * 32:(i + 1) * 32],
                               P.sb["o8sb"][0:8, i * 128:(i + 1) * 128],
                               P.sb["z4"][:, :], start=True, stop=True,
                               skip_group_check=True)
            s_ox.inc(mme)
        P.pe(pe)

        def v(v, w=w):
            v.wait_ge(s_tr.h, b_tr + w + 1)
            if w >= 1:
                v.wait_ge(s_sc.h, b_sc + w)
            s_te.inc(v.tensor_copy(P.sb["tqsb"][0:8, :], ps_y[0:8, 0:512]))
            s_te.inc(v.tensor_copy(P.sb["tksb"][0:8, :], ps_y[0:8, 512:1024]))
            v.wait_ge(s_sc.h, b_sc + w + 1)
            if w >= 1:
                v.wait_ge(s_ex.h, b_ex + 4 * w)
            s_nm.inc(v.reduce_max(P.sb["ngm1"][:, :],
                                  ps_x[:, :].rearrange("p (a b) -> p a b", b=128),
                                  axis=AX.X, negate=True))
            v.wait_ge(s_pt.h, b_pt + w + 1)
            if w >= 1:
                v.wait_ge(s_pv.h, b_pv + w)
            s_pte.inc(v.tensor_copy(P.sb["pt1sb"][:, :], ps_w_[:, 0:512]))
            v.wait_ge(s_pv.h, b_pv + w + 1)
            if w >= 1:
                v.wait_ge(s_ox.h, b_ox + w)
            s_o8.inc(v.tensor_copy(P.sb["o8sb"][0:8, :], ps_z[0:8, :]))
            v.wait_ge(s_ex.h, b_ex + 4 * (w + 1))
            s_rc.inc(v.reciprocal(P.sb["rcp1"][:, :], P.sb["lsm1"][:, :]))
            v.wait_ge(s_rc.h, b_rc + w + 1)
            v.wait_ge(s_ox.h, b_ox + w + 1)
            xtv = P.sb["xt"].rearrange("p (c t) -> p t c", t=T)
            for i in range(4):
                tok = w * 4 + i
                s_fin.inc(v.scalar_tensor_tensor(xtv[:, tok, :], ps_x[:, i * 32:(i + 1) * 32],
                                                 P.sb["rcp1"][:, i:i + 1], xtv[:, tok, :],
                                                 OP.mult, OP.add))
        P.vec(v)

        def a(a, w=w):
            a.wait_ge(s_nm.h, b_nm + w + 1)
            if w >= 1:
                a.wait_ge(s_pt.h, b_pt + w)
                a.wait_ge(s_rc.h, b_rc + w)
            for i in range(4):
                s_ex.inc(a.activation(P.sb["p1sb"][:, i * 128:(i + 1) * 128],
                                      ps_x[:, i * 128:(i + 1) * 128], AF.Exp,
                                      bias=P.sb["ngm1"][:, i:i + 1],
                                      accum_out=P.sb["lsm1"][:, i:i + 1]))
        P.act(a)
    return s_fin, s_fin.n



def _a0_blocks(chunk):
    if chunk == 0:
        return [(0, 0), (1, 0)]
    return [(0, 0), (1, 0), (1, 512), (0, 512)]


A0LVL = int(os.environ.get("K_A0LVL", "5"))
A0SUB = os.environ.get("K_A0SUB", "")


def emit_attn0(P, tag, s_xn, xn_n, chunk):
    xn = P.sb["xn"]
    stage = P.sb["qkstage"]
    sqq = P.sb["sqq"]
    stat = P.sb["stat"]
    ps_s0, ps_s1 = P.ps["s0"], P.ps["s1"]
    ps_x, ps_z, ps_y, ps_w = P.ps["x"], P.ps["z"], P.ps["y"], P.ps["w"]
    s_cl = P.sem(tag + "Pe")
    s_sq = P.sem(tag + "sq")
    s_st = P.sem(tag + "st")
    s_qd = P.sem(tag + "qd")
    s_vt = P.sem(tag + "vt")
    s_ve = P.sem(tag + "ve")
    s_vd = P.sem(tag + "vd")
    cl0, st0, qd0, vt0, ve0, vd0 = s_cl.n, s_st.n, s_qd.n, s_vt.n, s_ve.n, s_vd.n

    def mmQKV(t, j, wbuf, ps):
        if j == 0:
            t.wait_ge(s_xn.h, xn_n)
        for c in range(NCH):
            mm = t.matmul(ps[:, :], wbuf[:, c * 128:(c + 1) * 128],
                          xn[:, c * T:(c + 1) * T], start=(c == 0), stop=(c == NCH - 1))
        return mm

    def evQKV(j, ps, mm_target):
        def v(v, j=j):
            v.wait_ge(P.sem(tag + "Pm").h, mm_target)
            if 2 <= j < 66:
                v.wait_ge(s_qd.h, qd0 + 16 * min(j - 1, 64))
            elif j >= 66 and "v" not in A0SUB:
                v.wait_ge(s_vd.h, vd0 + 16 * (j - 65))
            s_cl.inc(v.tensor_scalar(stage[:, (j % 2) * T:(j % 2 + 1) * T], ps[:, :],
                                     -CLIP, CLIP, OP.max, OP.min))
        P.vec(v)

    def consQKV(j):
        cln = cl0 + j + 1
        if j >= 64 and "v" in A0SUB:
            return
        if j < 64:
            def g2(g, j=j):
                if "r" in A0SUB:
                    g.sem_inc(s_qd.h, 16)
                    s_qd.n += 16
                    return
                g.wait_ge(s_cl.h, cln)
                dst = P.dram["d_qraw"] if j < 32 else P.dram["d_kraw"]
                s_qd.inc(g.dma_start(out=dst[j % 32],
                                     in_=stage[:, (j % 2) * T:(j % 2 + 1) * T]), 16)
            P.pool(g2)
        else:
            jv = j - 64

            def pe(t, j=j, jv=jv):
                t.wait_ge(s_cl.h, cln)
                if jv >= 1:
                    t.wait_ge(s_ve.h, ve0 + jv)
                for tt in range(4):
                    mm = t.transpose(ps_y[:, tt * 128:(tt + 1) * 128],
                                     stage[:, (j % 2) * T + tt * 128:
                                           (j % 2) * T + (tt + 1) * 128],
                                     P.sb["identb"][:, :])
                s_vt.inc(mm)
            P.pe(pe)
            vtn = s_vt.n

            def v2(v, jv=jv):
                v.wait_ge(s_vt.h, vtn)
                if jv >= 1:
                    v.wait_ge(s_vd.h, vd0 + 16 * jv)
                s_ve.inc(v.tensor_copy(P.sb["vout"][:, :], ps_y[:, 0:512]))
            P.vec(v2)
            ven = s_ve.n

            def g3(g, jv=jv):
                g.wait_ge(s_ve.h, ven)
                par = P.par_reg
                srcv = P.sb["vout"][:, :].rearrange("p (tt c) -> p tt c", c=128)
                with g.If(par):
                    g.dma_start(out=P.dram["sh_v"][1, chunk * T:(chunk + 1) * T,
                                                   jv * 128:(jv + 1) * 128]
                                .rearrange("(tt p) c -> p tt c", p=128),
                                in_=srcv).then_inc(s_vd.h, 16)
                with g.Else():
                    g.dma_start(out=P.dram["sh_v"][0, chunk * T:(chunk + 1) * T,
                                                   jv * 128:(jv + 1) * 128]
                                .rearrange("(tt p) c -> p tt c", p=128),
                                in_=srcv).then_inc(s_vd.h, 16)
                s_vd.n += 16
            P.pool(g3)

    emit_proj(P, tag + "P", P.dram["a0_wqkv"], 96, 4096, mmQKV, evQKV,
              consume_cb=consQKV)
    if A0LVL <= 1:
        return s_qd, s_qd.n

    s_bi = P.sem(tag + "bi")
    s_sev = P.sem(tag + "sev")
    bi0 = s_bi.n
    qdn_all = s_qd.n
    vdn_all = s_vd.n
    for grp in range(2):
        srcD = P.dram["d_qraw"] if grp == 0 else P.dram["d_kraw"]
        for oc in range(32):
            i = grp * 32 + oc

            def syb(sy, i=i, oc=oc, srcD=srcD):
                if i == 0:
                    sy.wait_ge(s_qd.h, qdn_all)
                    sy.wait_ge(s_vd.h, vdn_all)
                if i >= 2:
                    sy.wait_ge(s_st.h, st0 + 2 * (i - 1))
                s_bi.inc(sy.dma_start(out=stage[:, (i % 2) * T:(i % 2 + 1) * T],
                                      in_=srcD[oc]), 16)
            P.syn(syb)
            bin_ = s_bi.n

            def gsq(g, i=i):
                g.wait_ge(s_bi.h, bin_)
                if i >= 2:
                    g.wait_ge(s_st.h, st0 + 2 * (i - 1))
                s_sq.inc(g.tensor_mul(sqq[:, (i % 2) * T:(i % 2 + 1) * T],
                                      stage[:, (i % 2) * T:(i % 2 + 1) * T],
                                      stage[:, (i % 2) * T:(i % 2 + 1) * T]))
            P.pool(gsq)
            sqn = s_sq.n
            sevq = s_sev.n

            def peb(t, i=i, oc=oc, grp=grp):
                t.wait_ge(s_bi.h, bin_)
                if grp == 1 and oc == 0:
                    t.wait_ge(s_sev.h, sevq)
                mm0 = t.matmul(ps_s0[0:1, :], P.sb["ones_col_b"][:, :],
                               stage[:, (i % 2) * T:(i % 2 + 1) * T],
                               start=(oc == 0), stop=(oc == 31), skip_group_check=True)
                mm0.then_inc(s_st.h, 1)
                s_st.n += 1
                t.wait_ge(s_sq.h, sqn)
                mm = t.matmul(ps_s1[0:1, :], P.sb["ones_col_b"][:, :],
                              sqq[:, (i % 2) * T:(i % 2 + 1) * T],
                              start=(oc == 0), stop=(oc == 31), skip_group_check=True)
                mm.then_inc(s_st.h, 1)
                s_st.n += 1
            P.pe(peb)
        if grp == 0:
            stq = s_st.n

            def vqe(v):
                v.wait_ge(s_st.h, stq)
                v.tensor_copy(stat[0:1, 0:T], ps_s0[0:1, :])
                s_sev.inc(v.tensor_copy(stat[0:1, T:2 * T], ps_s1[0:1, :]))
            P.vec(vqe)

    if A0LVL == 15:
        return s_st, s_st.n

    s_sf = P.sem(tag + "sf")
    stall = s_st.n

    def vfin(v):
        v.wait_ge(s_st.h, stall)
        v.tensor_scalar(stat[0:1, 0:T], stat[0:1, 0:T], 1.0 / D, None, OP.mult)
        v.tensor_scalar(stat[0:1, T:2 * T], stat[0:1, T:2 * T], 1.0 / D, None, OP.mult)
        v.tensor_scalar(stat[0:1, 2 * T:3 * T], ps_s0[0:1, :], 1.0 / D, None, OP.mult)
        s_sf.inc(v.tensor_scalar(stat[0:1, 3 * T:4 * T], ps_s1[0:1, :], 1.0 / D,
                                 None, OP.mult))
        v.wait_ge(s_sf.h, s_sf.n)
        v.tensor_mul(ps_s0[0:1, :], stat[0:1, 0:T], stat[0:1, 0:T])
        s_sf.inc(v.tensor_mul(ps_s1[0:1, :], stat[0:1, 2 * T:3 * T],
                              stat[0:1, 2 * T:3 * T]))
        v.wait_ge(s_sf.h, s_sf.n)
        v.tensor_sub(stat[0:1, T:2 * T], stat[0:1, T:2 * T], ps_s0[0:1, :])
        s_sf.inc(v.tensor_sub(stat[0:1, 3 * T:4 * T], stat[0:1, 3 * T:4 * T],
                              ps_s1[0:1, :]))
    P.vec(vfin)
    varn = s_sf.n

    def afin(a):
        a.wait_ge(s_sf.h, varn)
        a.activation(ps_s0[0:1, :], stat[0:1, T:2 * T], AF.Ln, bias=P.sb["cst"][0:1, 0:1])
        a.activation(stat[0:1, T:2 * T], ps_s0[0:1, :], AF.Exp, scale=-0.5)
        a.activation(ps_s1[0:1, :], stat[0:1, 3 * T:4 * T], AF.Ln, bias=P.sb["cst"][0:1, 0:1])
        s_sf.inc(a.activation(stat[0:1, 3 * T:4 * T], ps_s1[0:1, :], AF.Exp, scale=-0.5))
    P.act(afin)
    rstdn = s_sf.n

    def pefin(t):
        t.wait_ge(s_sf.h, rstdn)
        t.matmul(P.ps["m0"][:, :], P.sb["ones_row_f"][:, :], stat[0:1, 0:T],
                 start=True, stop=True)
        t.matmul(P.ps["m1"][:, :], P.sb["ones_row_f"][:, :], stat[0:1, T:2 * T],
                 start=True, stop=True)
        t.matmul(ps_x[:, :], P.sb["ones_row_f"][:, :], stat[0:1, 2 * T:3 * T],
                 start=True, stop=True)
        s_sf.inc(t.matmul(ps_z[:, :], P.sb["ones_row_f"][:, :], stat[0:1, 3 * T:4 * T],
                          start=True, stop=True))
    P.pe(pefin)
    bcn = s_sf.n

    def vbc(v):
        v.wait_ge(s_sf.h, bcn)
        v.tensor_copy(P.sb["m_b"][:, :], P.ps["m0"][:, :])
        v.tensor_copy(P.sb["r_b"][:, :], P.ps["m1"][:, :])
        v.tensor_copy(P.sb["km_b"][:, :], ps_x[:, :])
        s_sf.inc(v.tensor_copy(P.sb["kr_b"][:, :], ps_z[:, :]))
    P.vec(vbc)
    bcdone = s_sf.n

    if A0LVL == 16:
        return s_sf, s_sf.n

    s_ai = P.sem(tag + "ai")
    s_ap = P.sem(tag + "ap")
    s_ao = P.sem(tag + "ao")
    s_ko = P.sem(tag + "ko")
    ai0, ap0, ao0, ko0 = s_ai.n, s_ap.n, s_ao.n, s_ko.n
    qn = P.sb["qnst"]
    u = P.sb["uapp"]

    for grp in range(2):
        srcD = P.dram["d_qraw"] if grp == 0 else P.dram["d_kraw"]
        mbn, rbn = ("m_b", "r_b") if grp == 0 else ("km_b", "kr_b")
        wn, bn = ("a0_qws", "a0_qbs") if grp == 0 else ("c_a0_klnw", "c_a0_klnb")
        for oc in range(32):
            i = grp * 32 + oc

            def sy(sy, i=i, oc=oc, srcD=srcD):
                if i == 0:
                    sy.wait_ge(s_qd.h, qd0 + 16 * 64)
                if i >= 2:
                    sy.wait_ge(s_ap.h, ap0 + i - 1)
                s_ai.inc(sy.dma_start(out=stage[:, (i % 2) * T:(i % 2 + 1) * T],
                                      in_=srcD[oc]), 16)
            P.syn(sy)

            def v(v, i=i, oc=oc, mbn=mbn, rbn=rbn, wn=wn, bn=bn, grp=grp):
                v.wait_ge(s_ai.h, ai0 + 16 * (i + 1))
                if i == 0:
                    v.wait_ge(s_sf.h, bcdone)
                if i >= 2:
                    v.wait_ge((s_ao if (i - 2) < 32 else s_ko).h,
                              (ao0 + 16 * (i - 1)) if (i - 2) < 32 else (ko0 + 16 * (i - 33)))
                v.tensor_sub(u[:, 0:T], stage[:, (i % 2) * T:(i % 2 + 1) * T],
                             P.sb[mbn][:, :])
                v.tensor_mul(u[:, T:2 * T], u[:, 0:T], P.sb[rbn][:, :])
                s_ap.inc(v.tensor_scalar(qn[:, (i % 2) * T:(i % 2 + 1) * T],
                                         u[:, T:2 * T], P.sb[wn][:, oc:oc + 1],
                                         P.sb[bn][:, oc:oc + 1], OP.mult, OP.add))
            P.vec(v)
            apn = s_ap.n

            if grp == 0:
                def syo(sy, i=i, oc=oc):
                    sy.wait_ge(s_ap.h, apn)
                    s_ao.inc(sy.dma_start(out=P.dram["d_q"][oc],
                                          in_=qn[:, (i % 2) * T:(i % 2 + 1) * T]), 16)
                P.syn(syo)
            else:
                def go(g, i=i, oc=oc):
                    g.wait_ge(s_ap.h, apn)
                    par = P.par_reg
                    with g.If(par):
                        g.dma_start(out=P.dram["sh_k"][1, oc * 128:(oc + 1) * 128,
                                                       chunk * T:(chunk + 1) * T],
                                    in_=qn[:, (i % 2) * T:(i % 2 + 1) * T]).then_inc(s_ko.h, 16)
                    with g.Else():
                        g.dma_start(out=P.dram["sh_k"][0, oc * 128:(oc + 1) * 128,
                                                       chunk * T:(chunk + 1) * T],
                                    in_=qn[:, (i % 2) * T:(i % 2 + 1) * T]).then_inc(s_ko.h, 16)
                    s_ko.n += 16
                P.pool(go)

    if A0LVL <= 2:
        return s_ko, s_ko.n

    s_rl = P.sem(tag + "rl")
    s_rdy = P.sem(tag + "rdy")
    kon = s_ko.n

    def syr(sy):
        sy.wait_ge(P.sem(tag + "zz").h, P.sem(tag + "zz").n)
        s_rl.inc(sy.dma_start(out=P.sb["rel"][:, :].rearrange("p (q k) -> p q k", k=2048),
                              in_=P.dram["a0_rel"][chunk]
                              .rearrange("q p k -> p q k")), 16)
    P.syn(syr)

    def gfl(g):
        g.wait_ge(s_ko.h, kon)
        par = P.par_reg
        with g.If(par):
            g.dma_start(out=P.dram["sh_flag"][1, chunk:chunk + 1, :],
                        in_=P.sb["flg"][chunk:chunk + 1, :]).then_inc(s_rdy.h, 16)
        with g.Else():
            g.dma_start(out=P.dram["sh_flag"][0, chunk:chunk + 1, :],
                        in_=P.sb["flg"][chunk:chunk + 1, :]).then_inc(s_rdy.h, 16)
        s_rdy.n += 16
        with g.register(tag + "more%d" % chunk) as more, g.register(tag + "rfv%d" % chunk) as rfv, \
             g.register(tag + "rtv%d" % chunk) as rtv, g.register(tag + "rcnt%d" % chunk) as rcnt:
            g.reg_mov(rcnt, 0)
            for ph in range(chunk + 1):
                g.reg_load(rtv, P.sb["flg"][ph:ph + 1, 0:1])
                g.reg_mov(more, 1)
                with g.While(more):
                    g.reg_alu(rcnt, rcnt, 16, op=OP.add)
                    with g.If(par):
                        g.dma_start(out=P.sb["pflg"][:, :],
                                    in_=P.dram["sh_flag"][0, ph:ph + 1, :]).then_inc(
                                        P.sem(tag + "pl").h, 16)
                    with g.Else():
                        g.dma_start(out=P.sb["pflg"][:, :],
                                    in_=P.dram["sh_flag"][1, ph:ph + 1, :]).then_inc(
                                        P.sem(tag + "pl").h, 16)
                    g.wait_ge(P.sem(tag + "pl").h, rcnt)
                    g.reg_load(rfv, P.sb["pflg"][0:1, 0:1])
                    g.reg_alu(more, rfv, rtv, op=OP.not_equal)
        g.sem_inc(s_rdy.h, 1)
        s_rdy.n += 1
    P.pool(gfl)
    rdyn = s_rdy.n
    rln = s_rl.n
    if A0LVL <= 3:
        return s_rdy, s_rdy.n

    blocks = _a0_blocks(chunk)
    nkb = len(blocks)
    nd = 2 * nkb + 1
    s_hd = P.sem(tag + "hd")
    s_sc = P.sem(tag + "sc")
    s_zz = P.sem(tag + "zz")
    s_nm = P.sem(tag + "nm")
    s_ex = P.sem(tag + "ex")
    s_ls = P.sem(tag + "ls")
    s_rc = P.sem(tag + "rc")
    s_pn = P.sem(tag + "pn")
    s_pt = P.sem(tag + "pt")
    s_pe2 = P.sem(tag + "pe2")
    s_pv = P.sem(tag + "pv")
    s_oc = P.sem(tag + "oc")
    hd0, sc0, zz0, nm0, ex0, ls0, rc0 = (s_hd.n, s_sc.n, s_zz.n, s_nm.n, s_ex.n,
                                         s_ls.n, s_rc.n)
    pn0, pt0, pe20, pv0, oc0 = s_pn.n, s_pt.n, s_pe2.n, s_pv.n, s_oc.n
    aon = s_ao.n
    ksb, vsb, qsb = P.sb["ksb"], P.sb["vsb"], P.sb["qsb"]
    zat, pat, ptsb = P.sb["zat"], P.sb["pat"], P.sb["ptsb"]

    for h in range(NHEAD):
        hb = (h % 2)

        def syh(sy, h=h, hb=hb):
            if h == 0:
                sy.wait_ge(s_rdy.h, rdyn)
                sy.wait_ge(s_rl.h, rln)
                sy.wait_ge(s_ao.h, aon)
            if h >= 2:
                sy.wait_ge(s_oc.h, oc0 + 4 * (h - 1))
            for kb, (slot, col0) in enumerate(blocks):
                s_hd.inc(sy.dma_start(
                    out=ksb[:, hb * 2048 + kb * 512: hb * 2048 + (kb + 1) * 512],
                    in_=P.dram["sh_k"][slot, h * 128:(h + 1) * 128, col0:col0 + 512]), 16)
                s_hd.inc(sy.dma_start(
                    out=vsb[:, hb * 2048 + kb * 512: hb * 2048 + (kb + 1) * 512]
                    .rearrange("p (tt c) -> p tt c", c=128),
                    in_=P.dram["sh_v"][slot, col0:col0 + 512, h * 128:(h + 1) * 128]
                    .rearrange("(tt p) c -> p tt c", p=128)), 16)
            s_hd.inc(sy.dma_start(out=qsb[:, hb * T:(hb + 1) * T],
                                  in_=P.dram["d_q"][h]), 16)
        P.syn(syh)
        hdn = s_hd.n
        slope = float(SLOPES[h])

        for qt in range(4):
            for kb in range(nkb):
                def pesc(t, h=h, hb=hb, qt=qt, kb=kb):
                    if qt == 0 and kb == 0:
                        t.wait_ge(s_hd.h, hdn)
                    t.wait_ge(s_zz.h, max(zz0, s_zz.n - 1))
                    s_sc.inc(t.matmul(
                        P.ps["m0" if kb % 2 == 0 else "m1"][:, :],
                        qsb[:, hb * T + qt * 128: hb * T + (qt + 1) * 128],
                        ksb[:, hb * 2048 + kb * 512: hb * 2048 + (kb + 1) * 512],
                        start=True, stop=True))
                P.pe(pesc)
                scn = s_sc.n

                def vz(v, h=h, qt=qt, kb=kb, slope=slope):
                    v.wait_ge(s_sc.h, scn)
                    s_zz.inc(v.scalar_tensor_tensor(
                        zat[:, kb * 512:(kb + 1) * 512],
                        P.sb["rel"][:, qt * 2048 + kb * 512: qt * 2048 + (kb + 1) * 512],
                        slope,
                        P.ps["m0" if kb % 2 == 0 else "m1"][:, :],
                        OP.mult, OP.add))
                P.vec(vz)
            zzn = s_zz.n

            exprev = s_ex.n

            def vnm(v, qt=qt):
                v.wait_ge(s_zz.h, zzn)
                v.wait_ge(s_ex.h, exprev)
                s_nm.inc(v.reduce_max(P.sb["nmx"][:, :], zat[:, 0:nkb * 512],
                                      axis=AX.X, negate=True))
            P.vec(vnm)
            nmn = s_nm.n

            ptprev = s_pt.n
            for kb in range(nkb):
                def aex(a, qt=qt, kb=kb, ptprev=ptprev):
                    a.wait_ge(s_nm.h, nmn)
                    if kb == 0:
                        a.wait_ge(s_pt.h, ptprev)
                        a.wait_ge(s_rc.h, s_rc.n)
                    s_ex.inc(a.activation(pat[:, kb * 512:(kb + 1) * 512],
                                          zat[:, kb * 512:(kb + 1) * 512], AF.Exp,
                                          bias=P.sb["nmx"][:, 0:1],
                                          accum_out=P.sb["lsm"][:, kb:kb + 1]))
                P.act(aex)
            exn = s_ex.n

            def vls(v):
                v.wait_ge(s_ex.h, exn)
                s_ls.inc(v.reduce_sum(P.sb["ltot"][:, :], P.sb["lsm"][:, 0:nkb], axis=AX.X))
                v.wait_ge(s_ls.h, s_ls.n)
                s_rc.inc(v.reciprocal(P.sb["rcpa"][:, :], P.sb["ltot"][:, :]))
                v.wait_ge(s_rc.h, s_rc.n)
            P.vec(vls)
            rcn = s_rc.n

            for kb in range(nkb):
                def gpn(g, kb=kb):
                    g.wait_ge(s_rc.h, rcn)
                    if kb == 0:
                        g.wait_ge(s_pt.h, s_pt.n)
                    s_pn.inc(g.tensor_scalar(zat[:, kb * 512:(kb + 1) * 512],
                                             pat[:, kb * 512:(kb + 1) * 512],
                                             P.sb["rcpa"][:, 0:1], None, OP.mult))
                P.pool(gpn)
                pnn = s_pn.n

                def pet(t, kb=kb):
                    t.wait_ge(s_pn.h, pnn)
                    t.wait_ge(s_pe2.h, max(pe20, s_pe2.n - 1))
                    for tt in range(4):
                        mm = t.transpose(
                            ps_w[:, (kb % 2) * 512 + tt * 128:(kb % 2) * 512 + (tt + 1) * 128],
                            zat[:, kb * 512 + tt * 128: kb * 512 + (tt + 1) * 128],
                            P.sb["identb"][:, :])
                    s_pt.inc(mm)
                P.pe(pet)
                ptn = s_pt.n

                def vpt(v, kb=kb):
                    v.wait_ge(s_pt.h, ptn)
                    v.wait_ge(s_pv.h, max(pv0, s_pv.n - 1))
                    s_pe2.inc(v.tensor_copy(ptsb[:, (kb % 2) * 512:(kb % 2 + 1) * 512],
                                            ps_w[:, (kb % 2) * 512:(kb % 2 + 1) * 512]))
                P.vec(vpt)
                pen = s_pe2.n

                def pev(t, h=h, hb=hb, kb=kb, qt=qt):
                    t.wait_ge(s_pe2.h, pen)
                    if kb == 0 and not (h == 0 and qt == 0 and chunk == 0):
                        t.wait_ge(s_oc.h, s_oc.n)
                    for tt in range(4):
                        mm = t.matmul(
                            ps_x[:, 0:128],
                            vsb[:, hb * 2048 + kb * 512 + tt * 128:
                                hb * 2048 + kb * 512 + (tt + 1) * 128],
                            ptsb[:, (kb % 2) * 512 + tt * 128:(kb % 2) * 512 + (tt + 1) * 128],
                            start=(kb == 0 and tt == 0), stop=(kb == nkb - 1 and tt == 3),
                            skip_group_check=True)
                    s_pv.inc(mm)
                P.pe(pev)
            pvn = s_pv.n

            def voc(v, h=h, qt=qt):
                v.wait_ge(s_pv.h, pvn)
                s_oc.inc(v.tensor_copy(xn[:, h * T + qt * 128: h * T + (qt + 1) * 128],
                                       ps_x[:, 0:128]))
            P.vec(voc)

    ocn = s_oc.n
    if A0LVL <= 4:
        return s_oc, s_oc.n

    def mmW(t, j, wbuf, ps):
        if j == 0:
            t.wait_ge(s_oc.h, ocn)
        for c in range(NCH):
            mm = t.matmul(ps[:, :], wbuf[:, c * 128:(c + 1) * 128],
                          xn[:, c * T:(c + 1) * T], start=(c == 0), stop=(c == NCH - 1))
        return mm

    s_wd = P.sem(tag + "We")

    def evW(j, ps, mm_target):
        def v(v, j=j):
            v.wait_ge(P.sem(tag + "Wm").h, mm_target)
            s_wd.inc(v.tensor_add(P.sb["xt"][:, j * T:(j + 1) * T],
                                  P.sb["xt"][:, j * T:(j + 1) * T], ps[:, :]))
        P.vec(v)

    emit_proj(P, tag + "W", P.dram["a0_wout"], 32, 4096, mmW, evW)
    return s_wd, s_wd.n



def build(steps):
    nc = bass.Bass("TRN2", target_bir_lowering=False)
    want = set(steps)
    with contextlib.ExitStack() as st:
        d = declare_io(nc, want)
        block = st.enter_context(nc.Block())
        P = Prog(nc, block, st)
        P.dram = d
        alloc_common(P, want)
        emit_init(P, want)
        for chunk in (0, 1):
            P.cur_chunk = chunk
            s_ld = emit_stream_load(P, chunk)
            deps = [(s_ld, s_ld.n)]
            for i, step in enumerate(steps):
                s_x, xn_n = emit_ln(P, "ln", deps)
                if step == "lndump":
                    sdmp = P.sem("dmp")

                    def gd(g, chunk=chunk, s_x=s_x, xn_n=xn_n):
                        g.wait_ge(s_x.h, xn_n)
                        sdmp.inc(g.dma_start(
                            out=P.dram["y_out"][:, :, chunk * T:(chunk + 1) * T],
                            in_=P.sb["xn"][:, :].rearrange("p (c t) -> p c t", t=T)), 16)
                        g.wait_ge(sdmp.h, sdmp.n)
                        if chunk == 1:
                            sdmp.inc(g.dma_start(
                                out=P.dram["y_out"][:, 0:1, 0:T],
                                in_=P.sb["m_b"][:, :].unsqueeze(1)), 16)
                            sdmp.inc(g.dma_start(
                                out=P.dram["y_out"][:, 1:2, 0:T],
                                in_=P.sb["r_b"][:, :].unsqueeze(1)), 16)
                            g.wait_ge(sdmp.h, sdmp.n)
                    P.pool(gd)

                    def syw(sy, s_x=s_x, xn_n=xn_n):
                        sy.wait_ge(s_x.h, xn_n)
                    P.syn(syw)
                    s_d, d_n = s_x, xn_n
                    continue
                if step == "f0":
                    s_d, d_n = emit_ffn0(P, "f0", s_x, xn_n)
                elif step == "f1":
                    s_d, d_n = emit_ffn1(P, "f1", s_x, xn_n)
                elif step == "a0":
                    s_d, d_n = emit_attn0(P, "a0", s_x, xn_n, chunk)
                elif step == "a1":
                    s_d, d_n = emit_attn1(P, "a1", s_x, xn_n)
                else:
                    raise ValueError(step)
                deps = [(s_d, d_n)]
            if steps and steps[-1] == "lndump":
                continue
            emit_stream_store(P, chunk, deps)
    return nc


_prog_cache = {}


def _get_prog(steps):
    key = tuple(steps)
    if key not in _prog_cache:
        _prog_cache[key] = build(list(steps))
    return _prog_cache[key]


UNFUSED_SEQ = (("f0",), ("a0",), ("f0",), ("f1",), ("a1",), ("f1",))
FUSED_SEQ = ("f0", "a0", "f0", "f1", "a1", "f1")


def kernel(**inputs):
    in_maps = prep_inputs(inputs)
    if os.environ.get("K_UNFUSED"):
        res = None
        for steps in UNFUSED_SEQ:
            nc = _get_prog(steps)
            res = run_bass_kernel_spmd(nc, in_maps, core_ids=list(range(8)))
            for c in range(8):
                in_maps[c] = dict(in_maps[c])
                in_maps[c]["x_in"] = np.ascontiguousarray(
                    np.asarray(res.results[c]["y_out"], dtype=np.float32))
        return assemble_output(res.results)
    nc = _get_prog(FUSED_SEQ)
    res = run_bass_kernel_spmd(nc, in_maps, core_ids=list(range(8)))
    return assemble_output(res.results)


# version 03
# speedup vs baseline: 1.4641x; 1.1655x over previous
import contextlib
import os
import random

EVAC_ACT = int(os.environ.get("K_EVAC_ACT", "0"))

import numpy as np
import ml_dtypes

import concourse.bass as bass
import concourse.mybir as mybir
from concourse.bass_utils import run_bass_kernel_spmd
from concourse.alu_op_type import AluOpType as OP

F32 = mybir.dt.float32
BF16 = mybir.dt.bfloat16
F16 = mybir.dt.float16
I32 = mybir.dt.int32
AF = mybir.ActivationFunctionType
AX = mybir.AxisListType

D = 4096
NCH = 32
T = 512
S = 2048
B = 4
NHEAD = 32
HDIM = 128
LN_EPS = 1e-5
CLIP = 8.0
SCALE0 = float(HDIM) ** -0.5
SCALE1 = float(HDIM) ** -0.5
ALIBI_MAX = 8.0

BLOCK0 = {0: 0, 1: 512}
BLOCK1 = {0: 1536, 1: 1024}

SLOPES = (1.0 / np.power(2.0, np.arange(1, 33) * (ALIBI_MAX / 32.0))).astype(np.float32)

PERM = (np.arange(4096) % 128) * 32 + (np.arange(4096) // 128)
INV = (np.arange(4096) % 32) * 128 + (np.arange(4096) // 32)



def _img_from_T(wT, ob=128):
    din, dout = wT.shape
    nblk = dout // ob
    a = wT.reshape(din // 128, 128, nblk, ob).transpose(2, 1, 0, 3)
    return np.ascontiguousarray(a.reshape(nblk, 128, (din // 128) * ob).astype(np.float32, copy=False))


def _cols(vec):
    return np.ascontiguousarray(vec.reshape(32, 128).T.astype(np.float32, copy=False))


def prep_inputs(inp):
    f = {k: np.asarray(v, dtype=np.float32) for k, v in inp.items()}
    f["L0_attn_hidden_states"] = f["L0_attn_hidden_states"][:, :, PERM]
    f["L0_ffn_W"] = f["L0_ffn_W"][PERM][:, PERM]
    f["L0_ffn_bias"] = f["L0_ffn_bias"][PERM]
    f["L0_ffn_A"] = f["L0_ffn_A"][:, :, PERM]
    f["L0_ffn_Bf"] = f["L0_ffn_Bf"][:, PERM, :]
    f["L0_attn_Wqkv"] = f["L0_attn_Wqkv"][:, PERM]
    f["L0_attn_Wout"] = f["L0_attn_Wout"][PERM, :]
    f["L1_attn_Wq"] = f["L1_attn_Wq"][PERM, :, :]
    f["L1_attn_Wk"] = f["L1_attn_Wk"][PERM, :, :]
    f["L1_attn_Wv"] = f["L1_attn_Wv"][PERM, :, :]
    f["L1_ffn_W"] = f["L1_ffn_W"][PERM][:, PERM]
    f["L1_ffn_b"] = f["L1_ffn_b"][PERM]
    f["L1_ffn_A"] = f["L1_ffn_A"][:, PERM]
    f["L1_ffn_B"] = f["L1_ffn_B"][PERM, :]
    f["L1_ffn_gate_w"] = f["L1_ffn_gate_w"][:, PERM]
    c = {}

    W0T = np.ascontiguousarray(f["L0_ffn_W"].T)
    BallT = np.ascontiguousarray(f["L0_ffn_Bf"].transpose(0, 2, 1).reshape(256, 4096))
    c["f0_main"] = np.ascontiguousarray(
        np.concatenate([_img_from_T(W0T), _img_from_T(BallT)], axis=2))
    c["f0_A"] = _img_from_T(np.ascontiguousarray(f["L0_ffn_A"].reshape(256, 4096).T))
    c["f0_bias"] = _cols(f["L0_ffn_bias"])
    c["f0_coef"] = np.ascontiguousarray(np.repeat(f["L0_ffn_coeffs"], 16).reshape(2, 128).T)
    c["f0_scal"] = np.ascontiguousarray(np.repeat(f["L0_ffn_scales"], 16).reshape(2, 128).T)

    c["a0_wqkv"] = _img_from_T(np.ascontiguousarray(f["L0_attn_Wqkv"].T))
    c["a0_wout"] = _img_from_T(np.ascontiguousarray(f["L0_attn_Wout"].T))
    c["a0_qlnw"] = _cols(f["L0_attn_q_ln_w"])
    c["a0_qlnb"] = _cols(f["L0_attn_q_ln_b"])
    c["a0_klnw"] = _cols(f["L0_attn_k_ln_w"])
    c["a0_klnb"] = _cols(f["L0_attn_k_ln_b"])

    c["a1_wq"] = _img_from_T(np.ascontiguousarray(
        f["L1_attn_Wq"].transpose(0, 2, 1).reshape(4096, 4096)))
    c["a1_wk"] = _img_from_T(np.ascontiguousarray(
        f["L1_attn_Wk"].transpose(0, 2, 1).reshape(4096, 1024)))
    c["a1_wv"] = _img_from_T(np.ascontiguousarray(
        f["L1_attn_Wv"].transpose(0, 2, 1).reshape(4096, 1024)))
    c["a1_bq"] = np.ascontiguousarray(f["L1_attn_bq"])
    c["a1_bk"] = np.ascontiguousarray(f["L1_attn_bk"])
    c["a1_bv"] = np.ascontiguousarray(f["L1_attn_bv"])

    W1T = np.ascontiguousarray(f["L1_ffn_W"].T)
    B1Tp = np.concatenate([np.ascontiguousarray(f["L1_ffn_B"].T),
                           np.zeros((64, 4096), np.float32)], axis=0)
    c["f1_main"] = np.ascontiguousarray(
        np.concatenate([_img_from_T(W1T), _img_from_T(B1Tp)], axis=2))
    c["f1_A"] = _img_from_T(np.ascontiguousarray(f["L1_ffn_A"].T), ob=64)
    c["f1_gate"] = np.ascontiguousarray(
        f["L1_ffn_gate_w"].T.reshape(32, 128, 4).transpose(1, 0, 2).reshape(128, 128))
    c["f1_gateb"] = np.ascontiguousarray(f["L1_ffn_gate_b"].reshape(1, 4))
    c["f1_bias"] = _cols(f["L1_ffn_b"])

    BIG = np.float16(-30000.0)
    rels = {}
    for par in (0, 1):
        tabs = []
        for chunk in (0, 1):
            nk = 1024 if chunk == 0 else 2048
            q0 = (BLOCK0 if chunk == 0 else BLOCK1)[par]
            tab = np.zeros((4, 128, 2048), np.float16)
            for qt in range(4):
                qabs = q0 + qt * 128 + np.arange(128)[:, None]
                k = np.arange(nk)[None, :]
                r = (k - qabs).astype(np.float32)
                r = np.where(k <= qabs, r, np.float32(-30000.0) * 256.0)
                tab[qt, :, 0:nk] = r.astype(np.float16)
            tabs.append(tab)
        rels[par] = np.ascontiguousarray(np.stack(tabs, 0))
    nonce = random.getrandbits(24)
    flags = np.zeros((4, 16), np.int32)
    for i in range(4):
        flags[i, :] = nonce + 1 + i
    c["sync_flags"] = flags

    x = f["L0_attn_hidden_states"]
    in_maps = []
    for core in range(8):
        b, par = core // 2, core % 2
        rows = np.concatenate([x[b, BLOCK0[par]:BLOCK0[par] + T, :],
                               x[b, BLOCK1[par]:BLOCK1[par] + T, :]], axis=0)
        ximg = np.ascontiguousarray(rows.T.reshape(32, 128, 1024).transpose(1, 0, 2))
        m = dict(c)
        m["x_in"] = ximg
        m["a0_rel"] = rels[par]
        in_maps.append(m)
    return in_maps


def assemble_output(results):
    out = np.empty((B, S, D), np.float32)
    for core in range(8):
        b, par = core // 2, core % 2
        y = np.asarray(results[core]["y_out"])
        rows = y.transpose(1, 0, 2).reshape(4096, 1024).T[:, PERM.argsort()] if False \
            else y.transpose(1, 0, 2).reshape(4096, 1024).T
        rows = rows[:, INV]
        out[b, BLOCK0[par]:BLOCK0[par] + T, :] = rows[:T]
        out[b, BLOCK1[par]:BLOCK1[par] + T, :] = rows[T:]
    return out



class Sem:
    __slots__ = ("h", "n")

    def __init__(self, h):
        self.h = h
        self.n = 0

    def inc(self, instr, k=1):
        instr.then_inc(self.h, k)
        self.n += k
        return instr


class Prog:
    def __init__(self, nc, block, stack):
        self.nc = nc
        self.block = block
        self.st = stack
        self.sems = {}
        self.dram = {}
        self.sb = {}

    def sem(self, name):
        if name not in self.sems:
            self.sems[name] = Sem(self.st.enter_context(self.nc.semaphore(name)))
        return self.sems[name]

    def pe(self, fn):
        self.block.tensor(fn)

    def vec(self, fn):
        self.block.vector(fn)

    def act(self, fn):
        self.block.scalar(fn)

    def pool(self, fn):
        self.block.gpsimd(fn)

    def syn(self, fn):
        self.block.sync(fn)

    def sbuf(self, name, shape, dtype):
        t = self.st.enter_context(self.nc.sbuf_tensor(name, shape, dtype))
        self.sb[name] = t
        return t

    def psum(self, name, shape, dtype=F32):
        return self.st.enter_context(self.nc.psum_tensor(name, shape, dtype))


def declare_io(nc, want):
    d = {}

    def inp(name, shape, dtype=F32):
        d[name] = nc.dram_tensor(name, shape, dtype, kind="ExternalInput").ap()

    inp("x_in", [128, 32, 1024])
    d["y_out"] = nc.dram_tensor("y_out", [128, 32, 1024], F32, kind="ExternalOutput").ap()
    d["dbg"] = nc.dram_tensor("dbg", [128, 4096], F32, kind="ExternalOutput").ap()
    inp("sync_flags", [4, 16], I32)
    if "f0" in want:
        inp("f0_main", [32, 128, 4352])
        inp("f0_A", [2, 128, 4096])
        inp("f0_bias", [128, 32])
        inp("f0_coef", [128, 2])
        inp("f0_scal", [128, 2])
    if "a0" in want:
        inp("a0_wqkv", [96, 128, 4096])
        inp("a0_wout", [32, 128, 4096])
        for n in ("a0_qlnw", "a0_qlnb", "a0_klnw", "a0_klnb"):
            inp(n, [128, 32])
        d["a0_rel"] = nc.dram_tensor("a0_rel", [2, 4, 128, 2048], F16,
                                     kind="ExternalInput").ap()
        d["d_qraw"] = nc.dram_tensor("d_qraw", [32, 128, T], BF16, kind="Internal").ap()
        d["d_kraw"] = nc.dram_tensor("d_kraw", [32, 128, T], BF16, kind="Internal").ap()
        d["d_q"] = nc.dram_tensor("d_q", [32, 128, T], BF16, kind="Internal").ap()
        d["sh_k"] = nc.dram_tensor("sh_k", [2, 4096, 1024], BF16, kind="Internal",
                                   addr_space="Shared").ap()
        d["sh_v"] = nc.dram_tensor("sh_v", [2, 1024, 4096], BF16, kind="Internal",
                                   addr_space="Shared").ap()
        d["sh_flag"] = nc.dram_tensor("sh_flag", [2, 4, 16], I32, kind="Internal",
                                      addr_space="Shared").ap()
    if "a1" in want:
        inp("a1_wq", [32, 128, 4096])
        inp("a1_wk", [8, 128, 4096])
        inp("a1_wv", [8, 128, 4096])
        inp("a1_bq", [128, 32])
        inp("a1_bk", [128, 8])
        inp("a1_bv", [128, 8])
    if "f1" in want:
        inp("f1_main", [32, 128, 4224])
        inp("f1_A", [1, 128, 2048])
        inp("f1_gate", [128, 128])
        inp("f1_gateb", [1, 4])
        inp("f1_bias", [128, 32])
    return d


def alloc_common(P, want):
    P.sbuf("xt", [128, NCH * T], F32)
    P.sbuf("xn", [128, NCH * T], BF16)
    P.sbuf("wbuf0", [128, 4352], BF16)
    P.sbuf("wbuf1", [128, 4352], BF16)
    P.sbuf("identb", [128, 128], BF16)
    P.sbuf("ones_col_b", [128, 1], BF16)
    P.sbuf("ones_row_b", [1, 128], BF16)
    P.sbuf("ones_row_f", [1, 128], F32)
    P.sbuf("flg", [4, 16], I32)
    P.sbuf("pflg", [1, 16], I32)
    P.sbuf("stat", [1, 4 * T], F32)
    P.sbuf("m_b", [128, T], F32)
    P.sbuf("r_b", [128, T], F32)
    P.sbuf("xb16", [128, 2 * T], BF16)
    P.sbuf("sqb", [128, 2 * T], BF16)
    P.sb["uapp"] = None
    P.sbuf("tmp", [128, 2 * T], F32)
    P.sb["uapp"] = P.sb["tmp"][:, :]
    if "f0" in want or "f1" in want:
        P.sbuf("usb", [128, 2 * T], BF16)
    if "f1" in want:
        P.sbuf("msk", [128, T], BF16)
        P.sbuf("gsc", [128, 16], F32)
        P.sbuf("gmx", [128, 4], F32)
        P.sbuf("gms", [128, 16], BF16)
        P.sbuf("mtt", [4, 4 * 128], BF16)
    if "a0" in want or "a1" in want:
        ovl = P.sbuf("ovl", [128, 16384], BF16)
    if "a0" in want:
        P.sb["qkstage"] = P.sb["xb16"][:, :]
        P.sb["sqq"] = P.sb["sqb"][:, :]
        P.sb["vout"] = None

        P.sb["rel"] = ovl[:, 0:8192].bitcast(F16)
        P.sb["zat"] = ovl[:, 8192:12288]
        P.sb["pat"] = ovl[:, 12288:16384]
        P.sbuf("ksb", [128, 2 * 2048], BF16)
        P.sbuf("vsb", [128, 2 * 2048], BF16)
        P.sbuf("qsb", [128, 2 * T], BF16)
        P.sbuf("qnst", [128, 2 * T], BF16)
        P.sb["vout"] = P.sb["qnst"][:, 0:T]
        P.sbuf("km_b", [128, T], F32)
        P.sbuf("kr_b", [128, T], F32)
        P.sbuf("ptsb", [128, 2 * T], BF16)
        P.sbuf("nmx", [128, 2], F32)
        P.sbuf("lsm", [128, 8], F32)
        P.sbuf("ltot", [128, 2], F32)
        P.sbuf("rcpa", [128, 2], F32)
    if "a1" in want:
        P.sb["qgb"] = ovl[:, 0:4096]
        P.sb["k1b"] = ovl[:, 4096:8192]
        P.sb["v1b"] = ovl[:, 8192:12288]
        P.sbuf("tqsb", [8, T], BF16)
        P.sbuf("tksb", [8, T], BF16)
        P.sbuf("p1sb", [128, T], BF16)
        P.sbuf("pt1sb", [128, T], BF16)
        P.sbuf("o8sb", [8, T], BF16)
        P.sbuf("ngm1", [128, 4], F32)
        P.sbuf("lsm1", [128, 4], F32)
        P.sbuf("rcp1", [128, 4], F32)
    P.ps = {
        "m0": P.psum("ps_m0", [128, T], F32),
        "m1": P.psum("ps_m1", [128, T], F32),
        "s0": P.psum("ps_s0", [1, T], F32),
        "s1": P.psum("ps_s1", [1, T], F32),
        "x": P.psum("ps_x", [128, T], F32),
        "y": P.psum("ps_y", [128, 1024], BF16),
        "z": P.psum("ps_z", [128, T], F32),
        "w": P.psum("ps_w", [128, 1024], BF16),
    }
    P.last_proj = None
    return


def emit_init(P, want):
    sld = P.sem("i_ld")
    sio = P.sem("i_io")
    siv = P.sem("i_v")

    loads = [("flg", "sync_flags")]
    specs = {}
    if "f0" in want:
        specs.update({"c_f0_bias": ("f0_bias", [128, 32]), "c_f0_coef": ("f0_coef", [128, 2]),
                      "c_f0_scal": ("f0_scal", [128, 2])})
        P.sbuf("f0_bh", [128, 32], F32)
        P.sbuf("f0_w", [128, 2], F32)
    if "a0" in want:
        for n in ("a0_qlnw", "a0_qlnb", "a0_klnw", "a0_klnb"):
            specs["c_" + n] = (n, [128, 32])
        P.sbuf("a0_qws", [128, 32], F32)
        P.sbuf("a0_qbs", [128, 32], F32)
    if "a1" in want:
        specs.update({"c_a1_bq": ("a1_bq", [128, 32]), "c_a1_bk": ("a1_bk", [128, 8]),
                      "c_a1_bv": ("a1_bv", [128, 8])})
        P.sbuf("a1_bqg", [128, 8], F32)
        P.sbuf("z4", [8, 32], BF16)
        P.sbuf("i_z4", [8, 32], F32)
    if "f1" in want:
        specs.update({"c_f1_gate": ("f1_gate", [128, 128]), "c_f1_gateb": ("f1_gateb", [1, 4]),
                      "c_f1_bias": ("f1_bias", [128, 32])})
        P.sbuf("f1_bh", [128, 32], F32)
        P.sbuf("gate_b", [128, 128], BF16)
        P.sbuf("gateb_b", [1, 4], BF16)
        P.sbuf("zrt", [4, 64], BF16)
        P.sbuf("i_blk", [1, 64], F32)
        P.sbuf("i_iop", [4, 64], F32)
    for sbn, (dn, sh) in specs.items():
        P.sbuf(sbn, sh, F32)
        loads.append((sbn, dn))
    scr_i = P.sb["tmp"][:, 0:128]
    P.sb["scr_i"] = scr_i
    P.sbuf("cst", [1, 8], F32)

    def g(g):
        P.par_reg = g.to_reg(g.partition_id() % 2)
        for sbn, dn in loads:
            sld.inc(g.dma_start(out=P.sb[sbn][:, :], in_=P.dram[dn]), 16)
        sio.inc(g.iota(scr_i[:, :], pattern=[[1, 128]], base=0, channel_multiplier=-1,
                       allow_small_or_imprecise_dtypes=True))
        if "a1" in want:
            sio.inc(g.iota(P.sb["i_z4"][:, :].rearrange("p (a b) -> p a b", b=4),
                           pattern=[[1, 8], [0, 4]], base=0, channel_multiplier=-1,
                           allow_small_or_imprecise_dtypes=True))
        if "f1" in want:
            sio.inc(g.iota(P.sb["i_iop"][:, :], pattern=[[0, 64]], base=0,
                           channel_multiplier=1, allow_small_or_imprecise_dtypes=True))
    P.pool(g)

    def v(v):
        v.memset(P.sb["cst"][0:1, 0:1], LN_EPS)
        v.memset(P.sb["ones_col_b"][:, :], 1.0)
        v.memset(P.sb["ones_row_b"][:, :], 1.0)
        v.memset(P.sb["ones_row_f"][:, :], 1.0)
        v.wait_ge(sio.h, sio.n)
        v.wait_ge(sld.h, sld.n)
        siv.inc(v.tensor_scalar(P.sb["identb"][:, :], scr_i[:, :], 0.0, None, OP.is_equal))
        if "f0" in want:
            v.tensor_scalar(P.sb["f0_bh"][:, :], P.sb["c_f0_bias"][:, :], 0.5, None, OP.mult)
            siv.inc(v.tensor_mul(P.sb["f0_w"][:, :], P.sb["c_f0_coef"][:, :],
                                 P.sb["c_f0_scal"][:, :]))
        if "a0" in want:
            v.tensor_scalar(P.sb["a0_qws"][:, :], P.sb["c_a0_qlnw"][:, :], SCALE0, None, OP.mult)
            siv.inc(v.tensor_scalar(P.sb["a0_qbs"][:, :], P.sb["c_a0_qlnb"][:, :],
                                    SCALE0, None, OP.mult))
        if "a1" in want:
            v.tensor_scalar(P.sb["z4"][:, :], P.sb["i_z4"][:, :], 0.0, None, OP.is_equal)
            siv.inc(v.reduce_sum(P.sb["a1_bqg"][:, :],
                                 P.sb["c_a1_bq"][:, :].rearrange("p (g j) -> p g j", j=4),
                                 axis=AX.X))
            v.wait_ge(siv.h, siv.n)
        if "f1" in want:
            v.tensor_scalar(P.sb["f1_bh"][:, :], P.sb["c_f1_bias"][:, :], 0.5, None, OP.mult)
            v.tensor_copy(P.sb["gate_b"][:, :], P.sb["c_f1_gate"][:, :])
            v.tensor_copy(P.sb["gateb_b"][:, :], P.sb["c_f1_gateb"][:, :])
            v.memset(P.sb["i_blk"][0:1, 0:8], 0.0)
            v.memset(P.sb["i_blk"][0:1, 8:16], 1.0)
            v.memset(P.sb["i_blk"][0:1, 16:32], 2.0)
            siv.inc(v.memset(P.sb["i_blk"][0:1, 32:64], 3.0))
    P.vec(v)

    if "f1" in want:
        ps = P.ps["x"]
        szb = P.sem("i_zb")

        def pe(t):
            t.wait_ge(siv.h, siv.n)
            szb.inc(t.matmul(ps[0:4, 0:64], P.sb["ones_row_f"][0:1, 0:4],
                             P.sb["i_blk"][0:1, :], start=True, stop=True))
        P.pe(pe)

        def v2(v):
            v.wait_ge(szb.h, szb.n)
            v.tensor_sub(P.sb["i_iop"][0:4, :], P.sb["i_iop"][0:4, :], ps[0:4, 0:64])
            siv.inc(v.tensor_scalar(P.sb["zrt"][:, :], P.sb["i_iop"][0:4, :], 0.0,
                                    None, OP.is_ge))
        P.vec(v2)

    def w(e):
        e.wait_ge(siv.h, siv.n)
        e.wait_ge(sld.h, sld.n)
    for sec in (P.pe, P.act, P.pool):
        sec(w)

    def wsy(sy):
        P.par_reg_sy = sy.to_reg(sy.partition_id() % 2)
        sy.wait_ge(siv.h, siv.n)
        sy.wait_ge(sld.h, sld.n)
    P.syn(wsy)



def emit_stream_load(P, chunk):
    s = P.sem("st_ld")

    def sy(sy):
        s.inc(sy.dma_start(out=P.sb["xt"][:, :].rearrange("p (c t) -> p c t", t=T),
                           in_=P.dram["x_in"][:, :, chunk * T:(chunk + 1) * T]), 16)
    P.syn(sy)
    return s


def emit_stream_store(P, chunk, deps):
    s = P.sem("st_st")

    def sy(sy):
        for dsem, dn in deps:
            sy.wait_ge(dsem.h, dn)
        s.inc(sy.dma_start(out=P.dram["y_out"][:, :, chunk * T:(chunk + 1) * T],
                           in_=P.sb["xt"][:, :].rearrange("p (c t) -> p c t", t=T)), 16)
        sy.wait_ge(s.h, s.n)
    P.syn(sy)
    return s



def emit_ln(P, tag, deps):
    xt, xn = P.sb["xt"], P.sb["xn"]
    xb, sq = P.sb["xb16"], P.sb["sqb"]
    stat, mb, rb = P.sb["stat"], P.sb["m_b"], P.sb["r_b"]
    ps_s0, ps_s1 = P.ps["s0"], P.ps["s1"]
    ps_b0, ps_b1 = P.ps["m0"], P.ps["m1"]

    s_c = P.sem(tag + "c")
    s_q = P.sem(tag + "q")
    s_m = P.sem(tag + "m")
    s_f = P.sem(tag + "f")
    s_a = P.sem(tag + "a")
    s_x = P.sem(tag + "x")

    c0 = s_c.n
    q0 = s_q.n
    m0 = s_m.n
    for c in range(NCH):
        def v(v, c=c):
            if c == 0:
                for dsem, dn in deps:
                    v.wait_ge(dsem.h, dn)
            if c >= 2:
                v.wait_ge(s_m.h, m0 + 2 * (c - 2) + 2)
            s_c.inc(v.tensor_copy(xb[:, (c % 2) * T:(c % 2 + 1) * T],
                                  xt[:, c * T:(c + 1) * T]))
        P.vec(v)

        def g(g, c=c):
            g.wait_ge(s_c.h, c0 + c + 1)
            if c >= 2:
                g.wait_ge(s_m.h, m0 + 2 * (c - 2) + 2)
            s_q.inc(g.tensor_mul(sq[:, (c % 2) * T:(c % 2 + 1) * T],
                                 xb[:, (c % 2) * T:(c % 2 + 1) * T],
                                 xb[:, (c % 2) * T:(c % 2 + 1) * T]))
        P.pool(g)

        def pe(t, c=c):
            t.wait_ge(s_c.h, c0 + c + 1)
            s_m.inc(t.matmul(ps_s0[0:1, :], P.sb["ones_col_b"][:, :],
                             xb[:, (c % 2) * T:(c % 2 + 1) * T],
                             start=(c == 0), stop=(c == NCH - 1), skip_group_check=True))
            t.wait_ge(s_q.h, q0 + c + 1)
            s_m.inc(t.matmul(ps_s1[0:1, :], P.sb["ones_col_b"][:, :],
                             sq[:, (c % 2) * T:(c % 2 + 1) * T],
                             start=(c == 0), stop=(c == NCH - 1), skip_group_check=True))
        P.pe(pe)

    mm_all = s_m.n

    def v2(v):
        v.wait_ge(s_m.h, mm_all)
        v.tensor_scalar(stat[0:1, 0:T], ps_s0[0:1, :], 1.0 / D, None, OP.mult)
        v.tensor_scalar(stat[0:1, T:2 * T], ps_s1[0:1, :], 1.0 / D, None, OP.mult)
        v.tensor_mul(stat[0:1, 2 * T:3 * T], stat[0:1, 0:T], stat[0:1, 0:T])
        s_f.inc(v.tensor_sub(stat[0:1, T:2 * T], stat[0:1, T:2 * T],
                             stat[0:1, 2 * T:3 * T]))
    P.vec(v2)
    f_var = s_f.n

    def a(a):
        a.wait_ge(s_f.h, f_var)
        a.activation(stat[0:1, 2 * T:3 * T], stat[0:1, T:2 * T], AF.Ln,
                     bias=P.sb["cst"][0:1, 0:1])
        s_f.inc(a.activation(stat[0:1, 3 * T:4 * T], stat[0:1, 2 * T:3 * T], AF.Exp,
                             scale=-0.5))
    P.act(a)
    f_rstd = s_f.n

    def pe2(t):
        t.wait_ge(s_f.h, f_rstd)
        s_f.inc(t.matmul(ps_b0[:, :], P.sb["ones_row_f"][:, :], stat[0:1, 0:T],
                         start=True, stop=True))
        s_f.inc(t.matmul(ps_b1[:, :], P.sb["ones_row_f"][:, :], stat[0:1, 3 * T:4 * T],
                         start=True, stop=True))
    P.pe(pe2)
    f_bc = s_f.n

    def v3(v):
        v.wait_ge(s_f.h, f_bc)
        v.tensor_copy(mb[:, :], ps_b0[:, :])
        s_f.inc(v.tensor_copy(rb[:, :], ps_b1[:, :]))
    P.vec(v3)
    f_cp = s_f.n

    a0 = s_a.n
    x0 = s_x.n
    u = P.sb["uapp"]
    for c in range(NCH):
        def v4(v, c=c):
            if c == 0:
                v.wait_ge(s_f.h, f_cp)
            if c >= 2:
                v.wait_ge(s_x.h, x0 + c - 1)
            s_a.inc(v.tensor_sub(u[:, (c % 2) * T:(c % 2 + 1) * T],
                                 xt[:, c * T:(c + 1) * T], mb[:, :]))
        P.vec(v4)

        def g4(g, c=c):
            g.wait_ge(s_a.h, a0 + c + 1)
            s_x.inc(g.tensor_mul(xn[:, c * T:(c + 1) * T],
                                 u[:, (c % 2) * T:(c % 2 + 1) * T], rb[:, :]))
        P.pool(g4)
    return s_x, s_x.n



def emit_proj(P, tag, img, nblk, totcols, mm_cb, evac_cb, consume_cb=None):
    wb = (P.sb["wbuf0"], P.sb["wbuf1"])
    ps = (P.ps["m0"], P.ps["m1"])
    s_w = P.sem(tag + "w")
    s_m = P.sem(tag + "m")
    s_e = P.sem(tag + "e")
    w0, m0, e0 = s_w.n, s_m.n, s_e.n
    prev = P.last_proj

    def g0(g):
        if prev is not None:
            g.wait_ge(prev[0].h, prev[1])
        s_w.inc(g.dma_start(out=wb[0][:, 0:totcols], in_=img[0]), 16)
        if nblk > 1:
            s_w.inc(g.dma_start(out=wb[1][:, 0:totcols], in_=img[1]), 16)
    P.pool(g0)

    for j in range(nblk):
        def pe(t, j=j):
            t.wait_ge(s_w.h, w0 + 16 * (j + 1))
            if j >= 2:
                t.wait_ge(s_e.h, e0 + j - 1)
            s_m.inc(mm_cb(t, j, wb[j % 2], ps[j % 2]))
        P.pe(pe)

        evac_cb(j, ps[j % 2], s_m.n)
        if consume_cb is not None and j >= 1:
            consume_cb(j - 1)

        def g1(g, j=j):
            if j + 2 < nblk:
                g.wait_ge(s_m.h, m0 + j + 1)
                s_w.inc(g.dma_start(out=wb[(j + 2) % 2][:, 0:totcols], in_=img[j + 2]), 16)
        P.pool(g1)
    if consume_cb is not None:
        consume_cb(nblk - 1)
    P.last_proj = (s_m, s_m.n)
    return s_e



def _dense_evac_scaled(P, tag, bias_cols, scale):
    s_e = P.sem(tag + "e")
    s_d = P.sem(tag + "d")
    e0 = s_e.n
    d0 = s_d.n

    def evac(j, ps, mm_target):
        def a(a, j=j):
            a.wait_ge(P.sem(tag + "m").h, mm_target)
            if j >= 2:
                a.wait_ge(s_d.h, d0 + j - 1)
            if EVAC_ACT:
                s_e.inc(a.activation(P.sb["tmp"][:, (j % 2) * T:(j % 2 + 1) * T], ps[:, :],
                                     AF.Identity, bias=bias_cols[:, j:j + 1], scale=scale))
            else:
                s_e.inc(a.tensor_scalar(P.sb["tmp"][:, (j % 2) * T:(j % 2 + 1) * T], ps[:, :],
                                        scale, bias_cols[:, j:j + 1], OP.mult, OP.add))
        (P.act if EVAC_ACT else P.vec)(a)

        def v(v, j=j):
            v.wait_ge(s_e.h, e0 + j + 1)
            s_d.inc(v.tensor_add(P.sb["xt"][:, j * T:(j + 1) * T],
                                 P.sb["xt"][:, j * T:(j + 1) * T],
                                 P.sb["tmp"][:, (j % 2) * T:(j % 2 + 1) * T]))
        P.vec(v)
    return evac, s_d


def emit_ffn0(P, tag, s_xn, xn_n):
    xn = P.sb["xn"]
    usb = P.sb["usb"]
    sA = P.sem(tag + "Ae")

    def mmA(t, j, wbuf, ps):
        if j == 0:
            t.wait_ge(s_xn.h, xn_n)
        for c in range(NCH):
            mm = t.matmul(ps[:, :], wbuf[:, c * 128:(c + 1) * 128],
                          xn[:, c * T:(c + 1) * T], start=(c == 0), stop=(c == NCH - 1))
        return mm

    def evA(j, ps, mm_target):
        def v(v):
            v.wait_ge(P.sem(tag + "Am").h, mm_target)
            sA.inc(v.tensor_scalar(usb[:, j * T:(j + 1) * T], ps[:, :],
                                   P.sb["f0_w"][:, j:j + 1], None, OP.mult))
        P.vec(v)

    emit_proj(P, tag + "A", P.dram["f0_A"], 2, 4096, mmA, evA)
    uA_n = sA.n

    def mmM(t, j, wbuf, ps):
        if j == 0:
            t.wait_ge(sA.h, uA_n)
        for c in range(NCH):
            t.matmul(ps[:, :], wbuf[:, c * 128:(c + 1) * 128],
                     xn[:, c * T:(c + 1) * T], start=(c == 0), stop=False)
        t.matmul(ps[:, :], wbuf[:, 32 * 128:33 * 128], usb[:, 0:T],
                 start=False, stop=False)
        return t.matmul(ps[:, :], wbuf[:, 33 * 128:34 * 128], usb[:, T:2 * T],
                        start=False, stop=True)

    evac, s_d = _dense_evac_scaled(P, tag + "M", P.sb["f0_bh"], 0.5)
    emit_proj(P, tag + "M", P.dram["f0_main"], 32, 4352, mmM, evac)
    return s_d, s_d.n


def emit_ffn1(P, tag, s_xn, xn_n):
    xn = P.sb["xn"]
    usb = P.sb["usb"]
    ps_g = P.ps["z"]
    ps_mt = P.ps["y"]
    ps_m64 = P.ps["x"]
    s_g = P.sem(tag + "g")
    s_gv = P.sem(tag + "gv")
    s_mt = P.sem(tag + "mt")
    s_mk = P.sem(tag + "mk")

    def peg(t):
        t.wait_ge(s_xn.h, xn_n)
        for tt in range(4):
            for c in range(NCH):
                t.matmul(ps_g[:, 16 + tt * 4:16 + (tt + 1) * 4],
                         xn[:, c * T + tt * 128: c * T + (tt + 1) * 128],
                         P.sb["gate_b"][:, c * 4:(c + 1) * 4],
                         start=(tt == 0 and c == 0), stop=False, skip_group_check=True)
            s_g.inc(t.matmul(ps_g[:, 16 + tt * 4:16 + (tt + 1) * 4],
                             P.sb["ones_row_b"][0:1, :], P.sb["gateb_b"][0:1, :],
                             start=False, stop=(tt == 3), skip_group_check=True))
    P.pe(peg)
    g_n = s_g.n

    def vg(v):
        v.wait_ge(s_g.h, g_n)
        s_gv.inc(v.tensor_copy(P.sb["gsc"][:, :], ps_g[:, 16:32]))
        v.wait_ge(s_gv.h, s_gv.n)
        s_gv.inc(v.reduce_max(P.sb["gmx"][:, :],
                              P.sb["gsc"][:, :].rearrange("p (a b) -> p a b", b=4),
                              axis=AX.X))
        v.wait_ge(s_gv.h, s_gv.n)
        for tt in range(4):
            s_gv.inc(v.tensor_scalar(P.sb["gms"][:, tt * 4:(tt + 1) * 4],
                                     P.sb["gsc"][:, tt * 4:(tt + 1) * 4],
                                     P.sb["gmx"][:, tt:tt + 1], None, OP.is_ge))
    P.vec(vg)
    gv_n = s_gv.n

    def pet(t):
        t.wait_ge(s_gv.h, gv_n)
        for tt in range(4):
            s_mt.inc(t.transpose(ps_mt[0:4, 512 + tt * 128:512 + (tt + 1) * 128],
                                 P.sb["gms"][:, tt * 4:(tt + 1) * 4], P.sb["identb"][:, :]))
    P.pe(pet)
    mt_n = s_mt.n

    def vt(v):
        v.wait_ge(s_mt.h, mt_n)
        s_mt.inc(v.tensor_copy(P.sb["mtt"][:, :], ps_mt[0:4, 512:1024]))
    P.vec(vt)
    mtc_n = s_mt.n

    def pem(t):
        t.wait_ge(s_mt.h, mtc_n)
        s_mk.inc(t.matmul(ps_m64[0:64, :], P.sb["zrt"][:, :],
                          P.sb["mtt"][0:4, :], start=True, stop=True))
    P.pe(pem)
    mk_n = s_mk.n

    def vm(v):
        v.wait_ge(s_mk.h, mk_n)
        s_mk.inc(v.tensor_copy(P.sb["msk"][0:64, 0:T], ps_m64[0:64, :]))
    P.vec(vm)
    msk_n = s_mk.n

    sA = P.sem(tag + "Ae")

    def mmA(t, j, wbuf, ps):
        for c in range(NCH):
            mm = t.matmul(ps[0:64, :], wbuf[:, c * 64:(c + 1) * 64],
                          xn[:, c * T:(c + 1) * T], start=(c == 0), stop=(c == NCH - 1))
        return mm

    def evA(j, ps, mm_target):
        def v(v):
            v.wait_ge(P.sem(tag + "Am").h, mm_target)
            v.wait_ge(s_mk.h, msk_n)
            sA.inc(v.tensor_mul(usb[0:64, 0:T], ps[0:64, :], P.sb["msk"][0:64, 0:T]))
        P.vec(v)

    emit_proj(P, tag + "A", P.dram["f1_A"], 1, 2048, mmA, evA)
    uA_n = sA.n

    def mmM(t, j, wbuf, ps):
        if j == 0:
            t.wait_ge(sA.h, uA_n)
        for c in range(NCH):
            t.matmul(ps[:, :], wbuf[:, c * 128:(c + 1) * 128],
                     xn[:, c * T:(c + 1) * T], start=(c == 0), stop=False)
        return t.matmul(ps[:, :], wbuf[0:64, 32 * 128:32 * 128 + 128],
                        usb[0:64, 0:T], start=False, stop=True)

    evac, s_d = _dense_evac_scaled(P, tag + "M", P.sb["f1_bh"], 0.5)
    emit_proj(P, tag + "M", P.dram["f1_main"], 32, 4224, mmM, evac)
    if os.environ.get("K_DBG") == "f1" and getattr(P, "cur_chunk", None) == 0:
        sdb = P.sem("dbg")
        dn = s_d.n

        def gdb(g):
            g.wait_ge(s_d.h, dn)
            sdb.inc(g.dma_start(out=P.dram["dbg"][:, 0:16], in_=P.sb["gsc"][:, :]), 16)
            sdb.inc(g.dma_start(out=P.dram["dbg"][:, 16:32], in_=P.sb["gms"][:, :]), 16)
            sdb.inc(g.dma_start(out=P.dram["dbg"][0:4, 32:544], in_=P.sb["mtt"][0:4, :]), 16)
            sdb.inc(g.dma_start(out=P.dram["dbg"][0:64, 544:1056], in_=P.sb["msk"][0:64, 0:T]), 16)
            sdb.inc(g.dma_start(out=P.dram["dbg"][0:64, 1056:1568], in_=P.sb["usb"][0:64, 0:T]), 16)
            g.wait_ge(sdb.h, sdb.n)
        P.pool(gdb)
    return s_d, s_d.n



def emit_attn1(P, tag, s_xn, xn_n):
    xn = P.sb["xn"]
    qgb, k1b, v1b = P.sb["qgb"], P.sb["k1b"], P.sb["v1b"]

    wb = (P.sb["wbuf0"], P.sb["wbuf1"])
    ps = (P.ps["m0"], P.ps["m1"])
    s_w = P.sem(tag + "qw")
    s_m = P.sem(tag + "qm")
    s_e = P.sem(tag + "qe")
    w0, m0_, e0 = s_w.n, s_m.n, s_e.n
    prev = P.last_proj
    img = P.dram["a1_wq"]

    def g0(g):
        if prev is not None:
            g.wait_ge(prev[0].h, prev[1])
        s_w.inc(g.dma_start(out=wb[0][:, 0:4096], in_=img[0]), 16)
        s_w.inc(g.dma_start(out=wb[1][:, 0:4096], in_=img[1]), 16)
    P.pool(g0)

    for h in range(32):
        gidx = h // 4

        def pe(t, h=h, gidx=gidx):
            t.wait_ge(s_w.h, w0 + 16 * (h + 1))
            if h == 0:
                t.wait_ge(s_xn.h, xn_n)
            if h % 4 == 0 and gidx >= 2:
                t.wait_ge(s_e.h, e0 + gidx - 1)
            for c in range(NCH):
                mm = t.matmul(ps[gidx % 2][:, :], wb[h % 2][:, c * 128:(c + 1) * 128],
                              xn[:, c * T:(c + 1) * T],
                              start=(h % 4 == 0 and c == 0),
                              stop=(h % 4 == 3 and c == NCH - 1), skip_group_check=True)
            s_m.inc(mm)
        P.pe(pe)

        if h % 4 == 3:
            mm_now = s_m.n

            def v(v, gidx=gidx):
                v.wait_ge(s_m.h, mm_now)
                s_e.inc(v.tensor_scalar(qgb[:, gidx * T:(gidx + 1) * T], ps[gidx % 2][:, :],
                                        P.sb["a1_bqg"][:, gidx:gidx + 1], None, OP.add))
            P.vec(v)

        def g1(g, h=h):
            if h + 2 < 32:
                g.wait_ge(s_m.h, m0_ + h + 1)
                s_w.inc(g.dma_start(out=wb[h % 2][:, 0:4096], in_=img[h + 2]), 16)
        P.pool(g1)
    P.last_proj = (s_m, s_m.n)
    qg_n = s_e.n

    def mk_kv(which, dst, bias_name, scaled):
        def mm(t, j, wbuf, ps_):
            for c in range(NCH):
                mm_ = t.matmul(ps_[:, :], wbuf[:, c * 128:(c + 1) * 128],
                               xn[:, c * T:(c + 1) * T], start=(c == 0), stop=(c == NCH - 1))
            return mm_

        s_ev = P.sem(tag + which + "e")

        def ev(j, ps_, mm_target):
            def v(v, j=j):
                v.wait_ge(P.sem(tag + which + "m").h, mm_target)
                if scaled:
                    s_ev.inc(v.tensor_scalar(dst[:, j * T:(j + 1) * T], ps_[:, :],
                                             P.sb[bias_name][:, j:j + 1], SCALE1,
                                             OP.add, OP.mult))
                else:
                    s_ev.inc(v.tensor_scalar(dst[:, j * T:(j + 1) * T], ps_[:, :],
                                             P.sb[bias_name][:, j:j + 1], None, OP.add))
            P.vec(v)
        emit_proj(P, tag + which, P.dram["a1_w" + which], 8, 4096, mm, ev)
        return s_ev

    s_k = mk_kv("k", k1b, "c_a1_bk", True)
    k_n = s_k.n
    s_v = mk_kv("v", v1b, "c_a1_bv", False)
    v_n = s_v.n

    ps_x, ps_y, ps_w_, ps_z = P.ps["x"], P.ps["y"], P.ps["w"], P.ps["z"]
    s_tr = P.sem(tag + "tr")
    s_te = P.sem(tag + "te")
    s_sc = P.sem(tag + "sc")
    s_nm = P.sem(tag + "nm")
    s_ex = P.sem(tag + "ex")
    s_pt = P.sem(tag + "pt")
    s_pte = P.sem(tag + "pe2")
    s_pv = P.sem(tag + "pv")
    s_o8 = P.sem(tag + "o8")
    s_ox = P.sem(tag + "ox")
    s_rc = P.sem(tag + "rc")
    s_fin = P.sem(tag + "fin")
    b_tr, b_te, b_sc, b_nm, b_ex = s_tr.n, s_te.n, s_sc.n, s_nm.n, s_ex.n
    b_pt, b_pte, b_pv, b_o8, b_ox, b_rc, b_fin = (s_pt.n, s_pte.n, s_pv.n, s_o8.n,
                                                  s_ox.n, s_rc.n, s_fin.n)
    qgv = qgb.rearrange("p (g t) -> p t g", t=T)
    kv_ = k1b.rearrange("p (g t) -> p t g", t=T)
    vv_ = v1b.rearrange("p (g t) -> p t g", t=T)
    NW = T // 4

    for w in range(NW):
        def pe(t, w=w):
            if w == 0:
                t.wait_ge(s_k.h, k_n)
                t.wait_ge(s_v.h, v_n)
                t.wait_ge(P.sem(tag + "qe").h, qg_n)
            if w >= 1:
                t.wait_ge(s_te.h, b_te + 2 * w)
            for i in range(4):
                tok = w * 4 + i
                t.transpose(ps_y[0:8, i * 128:(i + 1) * 128], qgv[:, tok, :],
                            P.sb["identb"][:, :])
                mmt = t.transpose(ps_y[0:8, 512 + i * 128:512 + (i + 1) * 128],
                                  kv_[:, tok, :], P.sb["identb"][:, :])
            s_tr.inc(mmt)
            t.wait_ge(s_te.h, b_te + 2 * (w + 1))
            if w >= 1:
                t.wait_ge(s_fin.h, b_fin + 4 * w)
            for i in range(4):
                mms = t.matmul(ps_x[:, i * 128:(i + 1) * 128],
                               P.sb["tqsb"][0:8, i * 128:(i + 1) * 128],
                               P.sb["tksb"][0:8, i * 128:(i + 1) * 128],
                               start=True, stop=True)
            s_sc.inc(mms)
            t.wait_ge(s_ex.h, b_ex + 4 * (w + 1))
            if w >= 1:
                t.wait_ge(s_pte.h, b_pte + w)
            for i in range(4):
                mmp = t.transpose(ps_w_[:, i * 128:(i + 1) * 128],
                                  P.sb["p1sb"][:, i * 128:(i + 1) * 128],
                                  P.sb["identb"][:, :])
            s_pt.inc(mmp)
            t.wait_ge(s_pte.h, b_pte + w + 1)
            if w >= 1:
                t.wait_ge(s_o8.h, b_o8 + w)
            for i in range(4):
                tok = w * 4 + i
                mmv = t.matmul(ps_z[0:8, i * 128:(i + 1) * 128], vv_[:, tok, :],
                               P.sb["pt1sb"][:, i * 128:(i + 1) * 128],
                               start=True, stop=True)
            s_pv.inc(mmv)
            t.wait_ge(s_o8.h, b_o8 + w + 1)
            for i in range(4):
                mme = t.matmul(ps_x[:, i * 32:(i + 1) * 32],
                               P.sb["o8sb"][0:8, i * 128:(i + 1) * 128],
                               P.sb["z4"][:, :], start=True, stop=True,
                               skip_group_check=True)
            s_ox.inc(mme)
        P.pe(pe)

        def v(v, w=w):
            v.wait_ge(s_tr.h, b_tr + w + 1)
            if w >= 1:
                v.wait_ge(s_sc.h, b_sc + w)
            s_te.inc(v.tensor_copy(P.sb["tqsb"][0:8, :], ps_y[0:8, 0:512]))
            s_te.inc(v.tensor_copy(P.sb["tksb"][0:8, :], ps_y[0:8, 512:1024]))
            v.wait_ge(s_sc.h, b_sc + w + 1)
            if w >= 1:
                v.wait_ge(s_ex.h, b_ex + 4 * w)
            s_nm.inc(v.reduce_max(P.sb["ngm1"][:, :],
                                  ps_x[:, :].rearrange("p (a b) -> p a b", b=128),
                                  axis=AX.X, negate=True))
            v.wait_ge(s_pt.h, b_pt + w + 1)
            if w >= 1:
                v.wait_ge(s_pv.h, b_pv + w)
            s_pte.inc(v.tensor_copy(P.sb["pt1sb"][:, :], ps_w_[:, 0:512]))
            v.wait_ge(s_pv.h, b_pv + w + 1)
            if w >= 1:
                v.wait_ge(s_ox.h, b_ox + w)
            s_o8.inc(v.tensor_copy(P.sb["o8sb"][0:8, :], ps_z[0:8, :]))
            v.wait_ge(s_ex.h, b_ex + 4 * (w + 1))
            s_rc.inc(v.reciprocal(P.sb["rcp1"][:, :], P.sb["lsm1"][:, :]))
            v.wait_ge(s_rc.h, b_rc + w + 1)
            v.wait_ge(s_ox.h, b_ox + w + 1)
            xtv = P.sb["xt"].rearrange("p (c t) -> p t c", t=T)
            for i in range(4):
                tok = w * 4 + i
                s_fin.inc(v.scalar_tensor_tensor(xtv[:, tok, :], ps_x[:, i * 32:(i + 1) * 32],
                                                 P.sb["rcp1"][:, i:i + 1], xtv[:, tok, :],
                                                 OP.mult, OP.add))
        P.vec(v)

        def a(a, w=w):
            a.wait_ge(s_nm.h, b_nm + w + 1)
            if w >= 1:
                a.wait_ge(s_pt.h, b_pt + w)
                a.wait_ge(s_rc.h, b_rc + w)
            for i in range(4):
                s_ex.inc(a.activation(P.sb["p1sb"][:, i * 128:(i + 1) * 128],
                                      ps_x[:, i * 128:(i + 1) * 128], AF.Exp,
                                      bias=P.sb["ngm1"][:, i:i + 1],
                                      accum_out=P.sb["lsm1"][:, i:i + 1]))
        P.act(a)
    return s_fin, s_fin.n



def _a0_blocks(chunk):
    if chunk == 0:
        return [(0, 0), (1, 0)]
    return [(0, 0), (1, 0), (1, 512), (0, 512)]


A0LVL = int(os.environ.get("K_A0LVL", "5"))
A0SUB = os.environ.get("K_A0SUB", "")


def emit_attn0(P, tag, s_xn, xn_n, chunk):
    xn = P.sb["xn"]
    stage = P.sb["qkstage"]
    sqq = P.sb["sqq"]
    stat = P.sb["stat"]
    ps_s0, ps_s1 = P.ps["s0"], P.ps["s1"]
    ps_x, ps_z, ps_y, ps_w = P.ps["x"], P.ps["z"], P.ps["y"], P.ps["w"]
    s_cl = P.sem(tag + "Pe")
    s_sq = P.sem(tag + "sq")
    s_st = P.sem(tag + "st")
    s_qd = P.sem(tag + "qd")
    s_vt = P.sem(tag + "vt")
    s_ve = P.sem(tag + "ve")
    s_vd = P.sem(tag + "vd")
    cl0, st0, qd0, vt0, ve0, vd0 = s_cl.n, s_st.n, s_qd.n, s_vt.n, s_ve.n, s_vd.n

    def mmQKV(t, j, wbuf, ps):
        if j == 0:
            t.wait_ge(s_xn.h, xn_n)
        for c in range(NCH):
            mm = t.matmul(ps[:, :], wbuf[:, c * 128:(c + 1) * 128],
                          xn[:, c * T:(c + 1) * T], start=(c == 0), stop=(c == NCH - 1))
        return mm

    def evQKV(j, ps, mm_target):
        def v(v, j=j):
            v.wait_ge(P.sem(tag + "Pm").h, mm_target)
            if 2 <= j < 66:
                v.wait_ge(s_qd.h, qd0 + 16 * min(j - 1, 64))
            elif j >= 66 and "v" not in A0SUB:
                v.wait_ge(s_vd.h, vd0 + 16 * (j - 65))
            s_cl.inc(v.tensor_scalar(stage[:, (j % 2) * T:(j % 2 + 1) * T], ps[:, :],
                                     -CLIP, CLIP, OP.max, OP.min))
        P.vec(v)

    def consQKV(j):
        cln = cl0 + j + 1
        if j >= 64 and "v" in A0SUB:
            return
        if j < 64:
            def g2(sy, j=j):
                if "r" in A0SUB:
                    sy.sem_inc(s_qd.h, 16)
                    s_qd.n += 16
                    return
                sy.wait_ge(s_cl.h, cln)
                dst = P.dram["d_qraw"] if j < 32 else P.dram["d_kraw"]
                s_qd.inc(sy.dma_start(out=dst[j % 32],
                                      in_=stage[:, (j % 2) * T:(j % 2 + 1) * T]), 16)
            P.syn(g2)
        else:
            jv = j - 64

            def pe(t, j=j, jv=jv):
                t.wait_ge(s_cl.h, cln)
                if jv >= 1:
                    t.wait_ge(s_ve.h, ve0 + jv)
                for tt in range(4):
                    mm = t.transpose(ps_y[:, tt * 128:(tt + 1) * 128],
                                     stage[:, (j % 2) * T + tt * 128:
                                           (j % 2) * T + (tt + 1) * 128],
                                     P.sb["identb"][:, :])
                s_vt.inc(mm)
            P.pe(pe)
            vtn = s_vt.n

            def v2(v, jv=jv):
                v.wait_ge(s_vt.h, vtn)
                if jv >= 1:
                    v.wait_ge(s_vd.h, vd0 + 16 * jv)
                s_ve.inc(v.tensor_copy(P.sb["vout"][:, :], ps_y[:, 0:512]))
            P.vec(v2)
            ven = s_ve.n

            def g3(sy, jv=jv):
                sy.wait_ge(s_ve.h, ven)
                par = P.par_reg_sy
                srcv = P.sb["vout"][:, :].rearrange("p (tt c) -> p tt c", c=128)
                with sy.If(par):
                    sy.dma_start(out=P.dram["sh_v"][1, chunk * T:(chunk + 1) * T,
                                                    jv * 128:(jv + 1) * 128]
                                 .rearrange("(tt p) c -> p tt c", p=128),
                                 in_=srcv).then_inc(s_vd.h, 16)
                with sy.Else():
                    sy.dma_start(out=P.dram["sh_v"][0, chunk * T:(chunk + 1) * T,
                                                    jv * 128:(jv + 1) * 128]
                                 .rearrange("(tt p) c -> p tt c", p=128),
                                 in_=srcv).then_inc(s_vd.h, 16)
                s_vd.n += 16
            P.syn(g3)

    emit_proj(P, tag + "P", P.dram["a0_wqkv"], 96, 4096, mmQKV, evQKV,
              consume_cb=consQKV)
    if A0LVL <= 1:
        return s_qd, s_qd.n

    s_bi = P.sem(tag + "bi")
    s_sev = P.sem(tag + "sev")
    bi0 = s_bi.n
    qdn_all = s_qd.n
    vdn_all = s_vd.n
    for grp in range(2):
        srcD = P.dram["d_qraw"] if grp == 0 else P.dram["d_kraw"]
        for oc in range(32):
            i = grp * 32 + oc

            def syb(sy, i=i, oc=oc, srcD=srcD):
                if i == 0:
                    sy.wait_ge(s_qd.h, qdn_all)
                    sy.wait_ge(s_vd.h, vdn_all)
                if i >= 2:
                    sy.wait_ge(s_st.h, st0 + 2 * (i - 1))
                s_bi.inc(sy.dma_start(out=stage[:, (i % 2) * T:(i % 2 + 1) * T],
                                      in_=srcD[oc]), 16)
            P.syn(syb)
            bin_ = s_bi.n

            def gsq(g, i=i):
                g.wait_ge(s_bi.h, bin_)
                if i >= 2:
                    g.wait_ge(s_st.h, st0 + 2 * (i - 1))
                s_sq.inc(g.tensor_mul(sqq[:, (i % 2) * T:(i % 2 + 1) * T],
                                      stage[:, (i % 2) * T:(i % 2 + 1) * T],
                                      stage[:, (i % 2) * T:(i % 2 + 1) * T]))
            P.pool(gsq)
            sqn = s_sq.n
            sevq = s_sev.n

            def peb(t, i=i, oc=oc, grp=grp):
                t.wait_ge(s_bi.h, bin_)
                if grp == 1 and oc == 0:
                    t.wait_ge(s_sev.h, sevq)
                mm0 = t.matmul(ps_s0[0:1, :], P.sb["ones_col_b"][:, :],
                               stage[:, (i % 2) * T:(i % 2 + 1) * T],
                               start=(oc == 0), stop=(oc == 31), skip_group_check=True)
                mm0.then_inc(s_st.h, 1)
                s_st.n += 1
                t.wait_ge(s_sq.h, sqn)
                mm = t.matmul(ps_s1[0:1, :], P.sb["ones_col_b"][:, :],
                              sqq[:, (i % 2) * T:(i % 2 + 1) * T],
                              start=(oc == 0), stop=(oc == 31), skip_group_check=True)
                mm.then_inc(s_st.h, 1)
                s_st.n += 1
            P.pe(peb)
        if grp == 0:
            stq = s_st.n

            def vqe(v):
                v.wait_ge(s_st.h, stq)
                v.tensor_copy(stat[0:1, 0:T], ps_s0[0:1, :])
                s_sev.inc(v.tensor_copy(stat[0:1, T:2 * T], ps_s1[0:1, :]))
            P.vec(vqe)

    if A0LVL == 15:
        return s_st, s_st.n

    s_sf = P.sem(tag + "sf")
    stall = s_st.n

    def vfin(v):
        v.wait_ge(s_st.h, stall)
        v.tensor_scalar(stat[0:1, 0:T], stat[0:1, 0:T], 1.0 / D, None, OP.mult)
        v.tensor_scalar(stat[0:1, T:2 * T], stat[0:1, T:2 * T], 1.0 / D, None, OP.mult)
        v.tensor_scalar(stat[0:1, 2 * T:3 * T], ps_s0[0:1, :], 1.0 / D, None, OP.mult)
        s_sf.inc(v.tensor_scalar(stat[0:1, 3 * T:4 * T], ps_s1[0:1, :], 1.0 / D,
                                 None, OP.mult))
        v.wait_ge(s_sf.h, s_sf.n)
        v.tensor_mul(ps_s0[0:1, :], stat[0:1, 0:T], stat[0:1, 0:T])
        s_sf.inc(v.tensor_mul(ps_s1[0:1, :], stat[0:1, 2 * T:3 * T],
                              stat[0:1, 2 * T:3 * T]))
        v.wait_ge(s_sf.h, s_sf.n)
        v.tensor_sub(stat[0:1, T:2 * T], stat[0:1, T:2 * T], ps_s0[0:1, :])
        s_sf.inc(v.tensor_sub(stat[0:1, 3 * T:4 * T], stat[0:1, 3 * T:4 * T],
                              ps_s1[0:1, :]))
    P.vec(vfin)
    varn = s_sf.n

    def afin(a):
        a.wait_ge(s_sf.h, varn)
        a.activation(ps_s0[0:1, :], stat[0:1, T:2 * T], AF.Ln, bias=P.sb["cst"][0:1, 0:1])
        a.activation(stat[0:1, T:2 * T], ps_s0[0:1, :], AF.Exp, scale=-0.5)
        a.activation(ps_s1[0:1, :], stat[0:1, 3 * T:4 * T], AF.Ln, bias=P.sb["cst"][0:1, 0:1])
        s_sf.inc(a.activation(stat[0:1, 3 * T:4 * T], ps_s1[0:1, :], AF.Exp, scale=-0.5))
    P.act(afin)
    rstdn = s_sf.n

    def pefin(t):
        t.wait_ge(s_sf.h, rstdn)
        t.matmul(P.ps["m0"][:, :], P.sb["ones_row_f"][:, :], stat[0:1, 0:T],
                 start=True, stop=True)
        t.matmul(P.ps["m1"][:, :], P.sb["ones_row_f"][:, :], stat[0:1, T:2 * T],
                 start=True, stop=True)
        t.matmul(ps_x[:, :], P.sb["ones_row_f"][:, :], stat[0:1, 2 * T:3 * T],
                 start=True, stop=True)
        s_sf.inc(t.matmul(ps_z[:, :], P.sb["ones_row_f"][:, :], stat[0:1, 3 * T:4 * T],
                          start=True, stop=True))
    P.pe(pefin)
    bcn = s_sf.n

    def vbc(v):
        v.wait_ge(s_sf.h, bcn)
        v.tensor_copy(P.sb["m_b"][:, :], P.ps["m0"][:, :])
        v.tensor_copy(P.sb["r_b"][:, :], P.ps["m1"][:, :])
        v.tensor_copy(P.sb["km_b"][:, :], ps_x[:, :])
        s_sf.inc(v.tensor_copy(P.sb["kr_b"][:, :], ps_z[:, :]))
    P.vec(vbc)
    bcdone = s_sf.n

    if A0LVL == 16:
        return s_sf, s_sf.n

    s_ai = P.sem(tag + "ai")
    s_ap = P.sem(tag + "ap")
    s_ao = P.sem(tag + "ao")
    s_ko = P.sem(tag + "ko")
    ai0, ap0, ao0, ko0 = s_ai.n, s_ap.n, s_ao.n, s_ko.n
    qn = P.sb["qnst"]
    u = P.sb["uapp"]

    for grp in range(2):
        srcD = P.dram["d_qraw"] if grp == 0 else P.dram["d_kraw"]
        mbn, rbn = ("m_b", "r_b") if grp == 0 else ("km_b", "kr_b")
        wn, bn = ("a0_qws", "a0_qbs") if grp == 0 else ("c_a0_klnw", "c_a0_klnb")
        for oc in range(32):
            i = grp * 32 + oc

            def sy(sy, i=i, oc=oc, srcD=srcD):
                if i == 0:
                    sy.wait_ge(s_qd.h, qd0 + 16 * 64)
                if i >= 2:
                    sy.wait_ge(s_ap.h, ap0 + i - 1)
                s_ai.inc(sy.dma_start(out=stage[:, (i % 2) * T:(i % 2 + 1) * T],
                                      in_=srcD[oc]), 16)
            P.syn(sy)

            def v(v, i=i, oc=oc, mbn=mbn, rbn=rbn, wn=wn, bn=bn, grp=grp):
                v.wait_ge(s_ai.h, ai0 + 16 * (i + 1))
                if i == 0:
                    v.wait_ge(s_sf.h, bcdone)
                if i >= 2:
                    v.wait_ge((s_ao if (i - 2) < 32 else s_ko).h,
                              (ao0 + 16 * (i - 1)) if (i - 2) < 32 else (ko0 + 16 * (i - 33)))
                v.tensor_sub(u[:, 0:T], stage[:, (i % 2) * T:(i % 2 + 1) * T],
                             P.sb[mbn][:, :])
                v.tensor_mul(u[:, T:2 * T], u[:, 0:T], P.sb[rbn][:, :])
                s_ap.inc(v.tensor_scalar(qn[:, (i % 2) * T:(i % 2 + 1) * T],
                                         u[:, T:2 * T], P.sb[wn][:, oc:oc + 1],
                                         P.sb[bn][:, oc:oc + 1], OP.mult, OP.add))
            P.vec(v)
            apn = s_ap.n

            if grp == 0:
                def syo(sy, i=i, oc=oc):
                    sy.wait_ge(s_ap.h, apn)
                    s_ao.inc(sy.dma_start(out=P.dram["d_q"][oc],
                                          in_=qn[:, (i % 2) * T:(i % 2 + 1) * T]), 16)
                P.syn(syo)
            else:
                def go(sy, i=i, oc=oc):
                    sy.wait_ge(s_ap.h, apn)
                    par = P.par_reg_sy
                    with sy.If(par):
                        sy.dma_start(out=P.dram["sh_k"][1, oc * 128:(oc + 1) * 128,
                                                        chunk * T:(chunk + 1) * T],
                                     in_=qn[:, (i % 2) * T:(i % 2 + 1) * T]).then_inc(s_ko.h, 16)
                    with sy.Else():
                        sy.dma_start(out=P.dram["sh_k"][0, oc * 128:(oc + 1) * 128,
                                                        chunk * T:(chunk + 1) * T],
                                     in_=qn[:, (i % 2) * T:(i % 2 + 1) * T]).then_inc(s_ko.h, 16)
                    s_ko.n += 16
                P.syn(go)

    if A0LVL <= 2:
        return s_ko, s_ko.n

    s_rl = P.sem(tag + "rl")
    s_rdy = P.sem(tag + "rdy")
    kon = s_ko.n

    def syr(sy):
        sy.wait_ge(P.sem(tag + "zz").h, P.sem(tag + "zz").n)
        s_rl.inc(sy.dma_start(out=P.sb["rel"][:, :].rearrange("p (q k) -> p q k", k=2048),
                              in_=P.dram["a0_rel"][chunk]
                              .rearrange("q p k -> p q k")), 16)
    P.syn(syr)

    def gfl(g):
        g.wait_ge(s_ko.h, kon)
        par = P.par_reg
        with g.If(par):
            g.dma_start(out=P.dram["sh_flag"][1, chunk:chunk + 1, :],
                        in_=P.sb["flg"][chunk:chunk + 1, :]).then_inc(s_rdy.h, 16)
        with g.Else():
            g.dma_start(out=P.dram["sh_flag"][0, chunk:chunk + 1, :],
                        in_=P.sb["flg"][chunk:chunk + 1, :]).then_inc(s_rdy.h, 16)
        s_rdy.n += 16
        with g.register(tag + "more%d" % chunk) as more, g.register(tag + "rfv%d" % chunk) as rfv, \
             g.register(tag + "rtv%d" % chunk) as rtv, g.register(tag + "rcnt%d" % chunk) as rcnt:
            g.reg_mov(rcnt, 0)
            for ph in range(chunk + 1):
                g.reg_load(rtv, P.sb["flg"][ph:ph + 1, 0:1])
                g.reg_mov(more, 1)
                with g.While(more):
                    g.reg_alu(rcnt, rcnt, 16, op=OP.add)
                    with g.If(par):
                        g.dma_start(out=P.sb["pflg"][:, :],
                                    in_=P.dram["sh_flag"][0, ph:ph + 1, :]).then_inc(
                                        P.sem(tag + "pl").h, 16)
                    with g.Else():
                        g.dma_start(out=P.sb["pflg"][:, :],
                                    in_=P.dram["sh_flag"][1, ph:ph + 1, :]).then_inc(
                                        P.sem(tag + "pl").h, 16)
                    g.wait_ge(P.sem(tag + "pl").h, rcnt)
                    g.reg_load(rfv, P.sb["pflg"][0:1, 0:1])
                    g.reg_alu(more, rfv, rtv, op=OP.not_equal)
        g.sem_inc(s_rdy.h, 1)
        s_rdy.n += 1
    P.pool(gfl)
    rdyn = s_rdy.n
    rln = s_rl.n
    if A0LVL <= 3:
        return s_rdy, s_rdy.n

    blocks = _a0_blocks(chunk)
    nkb = len(blocks)
    nd = 2 * nkb + 1
    s_hd = P.sem(tag + "hd")
    s_sc = P.sem(tag + "sc")
    s_zz = P.sem(tag + "zz")
    s_nm = P.sem(tag + "nm")
    s_ex = P.sem(tag + "ex")
    s_ls = P.sem(tag + "ls")
    s_rc = P.sem(tag + "rc")
    s_pn = P.sem(tag + "pn")
    s_pt = P.sem(tag + "pt")
    s_pe2 = P.sem(tag + "pe2")
    s_pv = P.sem(tag + "pv")
    s_oc = P.sem(tag + "oc")
    hd0, sc0, zz0, nm0, ex0, ls0, rc0 = (s_hd.n, s_sc.n, s_zz.n, s_nm.n, s_ex.n,
                                         s_ls.n, s_rc.n)
    pn0, pt0, pe20, pv0, oc0 = s_pn.n, s_pt.n, s_pe2.n, s_pv.n, s_oc.n
    aon = s_ao.n
    ksb, vsb, qsb = P.sb["ksb"], P.sb["vsb"], P.sb["qsb"]
    zat, pat, ptsb = P.sb["zat"], P.sb["pat"], P.sb["ptsb"]

    units = [(h, qt) for h in range(NHEAD) for qt in range(4)]
    pend = {}
    caps = {}

    def emit_head_loads(h):
        hb = h % 2

        def syh(sy, h=h, hb=hb):
            if h == 0:
                sy.wait_ge(s_rdy.h, rdyn)
                sy.wait_ge(s_rl.h, rln)
                sy.wait_ge(s_ao.h, aon)
            if h >= 2:
                sy.wait_ge(s_oc.h, oc0 + 4 * (h - 1))
            for kb, (slot, col0) in enumerate(blocks):
                s_hd.inc(sy.dma_start(
                    out=ksb[:, hb * 2048 + kb * 512: hb * 2048 + (kb + 1) * 512],
                    in_=P.dram["sh_k"][slot, h * 128:(h + 1) * 128, col0:col0 + 512]), 16)
                s_hd.inc(sy.dma_start(
                    out=vsb[:, hb * 2048 + kb * 512: hb * 2048 + (kb + 1) * 512]
                    .rearrange("p (tt c) -> p tt c", c=128),
                    in_=P.dram["sh_v"][slot, col0:col0 + 512, h * 128:(h + 1) * 128]
                    .rearrange("(tt p) c -> p tt c", p=128)), 16)
            s_hd.inc(sy.dma_start(out=qsb[:, hb * T:(hb + 1) * T],
                                  in_=P.dram["d_q"][h]), 16)
        P.syn(syh)
        return s_hd.n

    head_hdn = {}
    for u, (h, qt) in enumerate(units):
        hb = h % 2
        pb = u % 2
        if qt == 0:
            head_hdn[h] = emit_head_loads(h)
        hdn = head_hdn[h]
        slope = float(SLOPES[h])
        zslc = lambda kb, pb=pb: zat[:, pb * 2048 + kb * 512: pb * 2048 + (kb + 1) * 512]
        pslc = lambda kb, pb=pb: pat[:, pb * 2048 + kb * 512: pb * 2048 + (kb + 1) * 512]

        prev2 = caps.get(u - 2, {})
        for kb in range(nkb):
            def pesc(t, h=h, hb=hb, qt=qt, kb=kb, prev2=prev2):
                if qt == 0 and kb == 0:
                    t.wait_ge(s_hd.h, hdn)
                t.wait_ge(s_zz.h, max(zz0, s_zz.n - 1))
                s_sc.inc(t.matmul(
                    P.ps["m0" if kb % 2 == 0 else "m1"][:, :],
                    qsb[:, hb * T + qt * 128: hb * T + (qt + 1) * 128],
                    ksb[:, hb * 2048 + kb * 512: hb * 2048 + (kb + 1) * 512],
                    start=True, stop=True))
            P.pe(pesc)
            scn = s_sc.n

            def vz(v, qt=qt, kb=kb, slope=slope, prev2=prev2):
                v.wait_ge(s_sc.h, scn)
                if kb == 0 and "pt" in prev2:
                    v.wait_ge(s_pt.h, prev2["pt"])
                s_zz.inc(v.scalar_tensor_tensor(
                    zslc(kb),
                    P.sb["rel"][:, qt * 2048 + kb * 512: qt * 2048 + (kb + 1) * 512],
                    slope,
                    P.ps["m0" if kb % 2 == 0 else "m1"][:, :],
                    OP.mult, OP.add))
            P.vec(vz)
        zzn = s_zz.n

        exprev = s_ex.n

        def vnm(v, pb=pb):
            v.wait_ge(s_zz.h, zzn)
            v.wait_ge(s_ex.h, exprev)
            s_nm.inc(v.reduce_max(P.sb["nmx"][:, pb:pb + 1], zat[:, pb * 2048:pb * 2048 + nkb * 512],
                                  axis=AX.X, negate=True))
        P.vec(vnm)
        nmn = s_nm.n

        for kb in range(nkb):
            def aex(a, kb=kb, pb=pb, prev2=prev2):
                a.wait_ge(s_nm.h, nmn)
                if kb == 0 and "pn_ready" in prev2:
                    a.wait_ge(s_pn.h, prev2["pn_ready"])
                s_ex.inc(a.activation(pslc(kb), zslc(kb), AF.Exp,
                                      bias=P.sb["nmx"][:, pb:pb + 1],
                                      accum_out=P.sb["lsm"][:, pb * 4 + kb:pb * 4 + kb + 1]))
            P.act(aex)
        exn = s_ex.n

        def vls(v, pb=pb, prev2=prev2):
            v.wait_ge(s_ex.h, exn)
            if "pn_ready" in prev2:
                v.wait_ge(s_pn.h, prev2["pn_ready"])
            s_ls.inc(v.reduce_sum(P.sb["ltot"][:, pb:pb + 1],
                                  P.sb["lsm"][:, pb * 4:pb * 4 + nkb], axis=AX.X))
            v.wait_ge(s_ls.h, s_ls.n)
            s_rc.inc(v.reciprocal(P.sb["rcpa"][:, pb:pb + 1], P.sb["ltot"][:, pb:pb + 1]))
            v.wait_ge(s_rc.h, s_rc.n)
        P.vec(vls)
        rcn = s_rc.n

        for kb in range(nkb):
            def gpn(g, kb=kb, pb=pb, prev2=prev2):
                g.wait_ge(s_rc.h, rcn)
                if kb == 0 and "pt" in prev2:
                    g.wait_ge(s_pt.h, prev2["pt"])
                s_pn.inc(g.tensor_scalar(zslc(kb), pslc(kb),
                                         P.sb["rcpa"][:, pb:pb + 1], None, OP.mult))
            P.pool(gpn)
        caps[u] = {"pn_ready": s_pn.n}

        def make_tpv(u=u, h=h, hb=hb, qt=qt, pb=pb, pnn=s_pn.n):
            def run():
                obank = P.ps["x"] if u % 2 == 0 else P.ps["z"]
                for kb in range(nkb):
                    def pet(t, kb=kb):
                        t.wait_ge(s_pn.h, pnn)
                        t.wait_ge(s_pe2.h, max(pe20, s_pe2.n - 1))
                        for tt in range(4):
                            mm = t.transpose(
                                ps_w[:, (kb % 2) * 512 + tt * 128:(kb % 2) * 512 + (tt + 1) * 128],
                                zat[:, pb * 2048 + kb * 512 + tt * 128: pb * 2048 + kb * 512 + (tt + 1) * 128],
                                P.sb["identb"][:, :])
                        s_pt.inc(mm)
                    P.pe(pet)
                    ptn = s_pt.n

                    def vpt(v, kb=kb):
                        v.wait_ge(s_pt.h, ptn)
                        v.wait_ge(s_pv.h, max(pv0, s_pv.n - 1))
                        s_pe2.inc(v.tensor_copy(ptsb[:, (kb % 2) * 512:(kb % 2 + 1) * 512],
                                                ps_w[:, (kb % 2) * 512:(kb % 2 + 1) * 512]))
                    P.vec(vpt)
                    pen = s_pe2.n

                    def pev(t, kb=kb):
                        t.wait_ge(s_pe2.h, pen)
                        if kb == 0 and u >= 2:
                            t.wait_ge(s_oc.h, caps[u - 2]["oc"])
                        for tt in range(4):
                            mm = t.matmul(
                                obank[:, 0:128],
                                vsb[:, hb * 2048 + kb * 512 + tt * 128:
                                    hb * 2048 + kb * 512 + (tt + 1) * 128],
                                ptsb[:, (kb % 2) * 512 + tt * 128:(kb % 2) * 512 + (tt + 1) * 128],
                                start=(kb == 0 and tt == 0), stop=(kb == nkb - 1 and tt == 3),
                                skip_group_check=True)
                        s_pv.inc(mm)
                    P.pe(pev)
                pvn = s_pv.n

                def voc(v):
                    v.wait_ge(s_pv.h, pvn)
                    s_oc.inc(v.tensor_copy(xn[:, h * T + qt * 128: h * T + (qt + 1) * 128],
                                           obank[:, 0:128]))
                P.vec(voc)
                caps[u]["oc"] = s_oc.n
                caps[u]["pt"] = s_pt.n
            return run

        pend[u] = make_tpv()
        if u >= 1:
            pend.pop(u - 1)()
    pend.pop(len(units) - 1)()

    ocn = s_oc.n
    if A0LVL <= 4:
        return s_oc, s_oc.n

    def mmW(t, j, wbuf, ps):
        if j == 0:
            t.wait_ge(s_oc.h, ocn)
        for c in range(NCH):
            mm = t.matmul(ps[:, :], wbuf[:, c * 128:(c + 1) * 128],
                          xn[:, c * T:(c + 1) * T], start=(c == 0), stop=(c == NCH - 1))
        return mm

    s_wd = P.sem(tag + "We")

    def evW(j, ps, mm_target):
        def v(v, j=j):
            v.wait_ge(P.sem(tag + "Wm").h, mm_target)
            s_wd.inc(v.tensor_add(P.sb["xt"][:, j * T:(j + 1) * T],
                                  P.sb["xt"][:, j * T:(j + 1) * T], ps[:, :]))
        P.vec(v)

    emit_proj(P, tag + "W", P.dram["a0_wout"], 32, 4096, mmW, evW)
    return s_wd, s_wd.n



def build(steps):
    nc = bass.Bass("TRN2", target_bir_lowering=False)
    want = set(steps)
    with contextlib.ExitStack() as st:
        d = declare_io(nc, want)
        block = st.enter_context(nc.Block())
        P = Prog(nc, block, st)
        P.dram = d
        alloc_common(P, want)
        emit_init(P, want)
        for chunk in (0, 1):
            P.cur_chunk = chunk
            s_ld = emit_stream_load(P, chunk)
            deps = [(s_ld, s_ld.n)]
            for i, step in enumerate(steps):
                s_x, xn_n = emit_ln(P, "ln", deps)
                if step == "lndump":
                    sdmp = P.sem("dmp")

                    def gd(g, chunk=chunk, s_x=s_x, xn_n=xn_n):
                        g.wait_ge(s_x.h, xn_n)
                        sdmp.inc(g.dma_start(
                            out=P.dram["y_out"][:, :, chunk * T:(chunk + 1) * T],
                            in_=P.sb["xn"][:, :].rearrange("p (c t) -> p c t", t=T)), 16)
                        g.wait_ge(sdmp.h, sdmp.n)
                        if chunk == 1:
                            sdmp.inc(g.dma_start(
                                out=P.dram["y_out"][:, 0:1, 0:T],
                                in_=P.sb["m_b"][:, :].unsqueeze(1)), 16)
                            sdmp.inc(g.dma_start(
                                out=P.dram["y_out"][:, 1:2, 0:T],
                                in_=P.sb["r_b"][:, :].unsqueeze(1)), 16)
                            g.wait_ge(sdmp.h, sdmp.n)
                    P.pool(gd)

                    def syw(sy, s_x=s_x, xn_n=xn_n):
                        sy.wait_ge(s_x.h, xn_n)
                    P.syn(syw)
                    s_d, d_n = s_x, xn_n
                    continue
                if step == "f0":
                    s_d, d_n = emit_ffn0(P, "f0", s_x, xn_n)
                elif step == "f1":
                    s_d, d_n = emit_ffn1(P, "f1", s_x, xn_n)
                elif step == "a0":
                    s_d, d_n = emit_attn0(P, "a0", s_x, xn_n, chunk)
                elif step == "a1":
                    s_d, d_n = emit_attn1(P, "a1", s_x, xn_n)
                else:
                    raise ValueError(step)
                deps = [(s_d, d_n)]
            if steps and steps[-1] == "lndump":
                continue
            emit_stream_store(P, chunk, deps)
    return nc


_prog_cache = {}


def _get_prog(steps):
    key = tuple(steps)
    if key not in _prog_cache:
        _prog_cache[key] = build(list(steps))
    return _prog_cache[key]


UNFUSED_SEQ = (("f0",), ("a0",), ("f0",), ("f1",), ("a1",), ("f1",))
FUSED_SEQ = ("f0", "a0", "f0", "f1", "a1", "f1")


def kernel(**inputs):
    in_maps = prep_inputs(inputs)
    if os.environ.get("K_UNFUSED"):
        res = None
        for steps in UNFUSED_SEQ:
            nc = _get_prog(steps)
            res = run_bass_kernel_spmd(nc, in_maps, core_ids=list(range(8)))
            for c in range(8):
                in_maps[c] = dict(in_maps[c])
                in_maps[c]["x_in"] = np.ascontiguousarray(
                    np.asarray(res.results[c]["y_out"], dtype=np.float32))
        return assemble_output(res.results)
    nc = _get_prog(FUSED_SEQ)
    res = run_bass_kernel_spmd(nc, in_maps, core_ids=list(range(8)))
    return assemble_output(res.results)
